# version 01
# baseline (speedup 1.0000x reference)
import numpy as np
import ml_dtypes
import concourse.bass as bass
import concourse.mybir as mybir
from concourse.bass_utils import run_bass_kernel_spmd
from contextlib import ExitStack

F32 = mybir.dt.float32
BF16 = mybir.dt.bfloat16
AF = mybir.ActivationFunctionType
ALU = mybir.AluOpType
AX = mybir.AxisListType.X
PAIRS = [[0, 1], [2, 3], [4, 5], [6, 7]]

T = 1024
D = 2048
HF = 1408
NE = 8

PHASES = 99
A6STOP = False
CUT = None
DBG = False

ENGINES = ["pe", "act", "dve", "pool", "syq", "scq", "gpq"]
DMA_Q = {"syq", "scq", "gpq"}
ENG_OF = {"pe": "tensor", "act": "scalar", "dve": "vector", "pool": "gpsimd",
          "syq": "sync", "scq": "scalar", "gpq": "gpsimd"}


class Step:
    __slots__ = ["eng", "fn", "deps", "inc", "count", "block", "tag", "idx", "wb",
                 "tag_ch"]

    def __init__(self, eng, fn, deps, inc, block, tag, idx, wb=False):
        self.eng, self.fn, self.deps, self.inc = eng, fn, deps, inc
        self.count = None
        self.block = block
        self.tag = tag
        self.idx = idx
        self.wb = wb


class Prog:
    def __init__(self, nc, ctx):
        self.nc = nc
        self.ctx = ctx
        self.sem = {e: ctx.enter_context(nc.semaphore(f"sem_{e}")) for e in ENGINES}
        self.steps = []
        self.cnt = {e: 0 for e in ENGINES}
        self.blk_idx = 0
        self.n_idx = 0
        self.res_w = {}
        self.res_r = {}
        self.chan_sem = {}
        self.chan_cnt = {}
        self.pending_fence = []

    def fence(self, *chs):
        self.pending_fence = [("ch", ch, self.chan_cnt[ch])
                              for ch in chs if ch in self.chan_cnt]

    def _chan(self, name):
        if name not in self.chan_sem:
            self.chan_sem[name] = self.ctx.enter_context(
                self.nc.semaphore(f"ch_{name}"))
            self.chan_cnt[name] = 0
        return self.chan_sem[name]

    def step(self, eng, fn, rd=(), wr=(), extra=(), tag="", wb=False, ch=None):
        raw = []
        for n in rd:
            w = self.res_w.get(n)
            if w is not None:
                raw.append(w)
        for n in wr:
            w = self.res_w.get(n)
            if w is not None:
                raw.append(w)
            raw.extend(self.res_r.get(n, {}).values())
        raw.extend(d for d in extra if d is not None)
        raw.extend(self.pending_fence)
        deps = []
        seen = set()
        for d in raw:
            if isinstance(d, tuple):
                if d[1] in self.chan_sem and d not in seen:
                    seen.add(d)
                    deps.append(d)
            elif d.eng in DMA_Q:
                key = ("ch", d.tag_ch)
                if key not in seen:
                    seen.add(key)
                    deps.append(("ch", d.tag_ch, self.chan_cnt[d.tag_ch]))
            else:
                if id(d) not in seen:
                    seen.add(id(d))
                    deps.append(d)
        if eng in DMA_Q:
            assert ch is not None, f"DMA step needs ch=: {tag}"
            self._chan(ch)
            self.chan_cnt[ch] += 16
            s = Step(eng, fn, deps, 16, self.blk_idx, tag, self.n_idx, wb)
            s.tag_ch = ch
            s.count = self.chan_cnt[ch]
        else:
            s = Step(eng, fn, deps, 1, self.blk_idx, tag, self.n_idx, wb)
            s.tag_ch = None
        self.n_idx += 1
        self.steps.append(s)
        for n in rd:
            self.res_r.setdefault(n, {})[eng] = s
        for n in wr:
            self.res_w[n] = s
            self.res_r[n] = {}
        return s

    def barrier(self):
        if not self.steps:
            self.blk_idx += 1
            return
        steps, self.steps = self.steps, []
        cnt = dict(self.cnt)
        for s in steps:
            if s.eng in DMA_Q:
                continue
            cnt[s.eng] += s.inc
            s.count = cnt[s.eng]
        by_body = {}
        for s in steps:
            by_body.setdefault(ENG_OF[s.eng], []).append(s)
        nc = self.nc
        with nc.Block() as blk:
            for body_name, seq in by_body.items():
                seq.sort(key=lambda s: s.idx)

                def make_body(seq):
                    def body(engobj):
                        hw = {}
                        for s in seq:
                            for d in s.deps:
                                if isinstance(d, tuple):
                                    _, chn, val = d
                                    if val <= 0 or hw.get(("c", chn), 0) >= val:
                                        continue
                                    engobj.wait_ge(self.chan_sem[chn], val)
                                    hw[("c", chn)] = val
                                    continue
                                if d.count is None:
                                    raise RuntimeError(
                                        f"dep not emitted: {d.tag} <- {s.tag}")
                                if (not d.wb) and (
                                        d.block < s.block or
                                        (d.eng == s.eng and d.eng == "pe")):
                                    continue
                                if hw.get(d.eng, 0) >= d.count:
                                    continue
                                engobj.wait_ge(self.sem[d.eng], d.count)
                                hw[d.eng] = d.count
                            instr = s.fn(engobj)
                            if instr is None:
                                raise RuntimeError(f"step returned None: {s.tag}")
                            if s.eng in DMA_Q:
                                instr.then_inc(self.chan_sem[s.tag_ch], 16)
                            else:
                                instr.then_inc(self.sem[s.eng], s.inc)
                    return body
                getattr(blk, body_name)(make_body(seq))
        self.cnt = cnt
        self.blk_idx += 1
        self.pending_fence = []

    def final_wait(self, deps):
        assert not self.steps
        nc = self.nc
        chans = {}
        for d in deps:
            assert d.eng in DMA_Q
            chans[d.tag_ch] = self.chan_cnt[d.tag_ch]
        with nc.Block() as blk:
            @blk.sync
            def _(sy):
                for chn, val in chans.items():
                    sy.wait_ge(self.chan_sem[chn], val)
        self.blk_idx += 1


def _poff_vec(half):
    offs = np.zeros(80, dtype=np.int64)
    for t3 in range(3):
        for r in range(2):
            offs[t3 * 2 + r] = r * 6144 * 1024 + (t3 * 2048 + half * 1024) * 1024
    for r in range(2):
        offs[6 + r] = r * 1024 * 2048 + half * 1024
    return offs.astype(np.int32).reshape(1, 80)


def host_inputs(inputs, core):
    b, half = core // 2, core % 2
    x = np.ascontiguousarray(
        np.asarray(inputs["L0_attn_x"], dtype=np.float32)[b, half * T:(half + 1) * T, :])
    eye_b = np.eye(128, dtype=ml_dtypes.bfloat16)
    eye_f = np.eye(128, dtype=np.float32)
    jj, ii = np.meshgrid(np.arange(128), np.arange(128), indexing="ij")
    maskS = np.where(ii > jj, 0.0, -1e30).astype(np.float32)
    maskI = np.where(ii >= jj, 0.0, -1e30).astype(np.float32)
    selb = np.zeros((8, 8 * 128), dtype=ml_dtypes.bfloat16)
    for e in range(8):
        selb[e, e * 128:(e + 1) * 128] = 1.0
    f32 = lambda a: np.asarray(a, dtype=np.float32)
    cont = np.ascontiguousarray
    return dict(
        x_in=x,
        wqkvT=cont(f32(inputs["L0_attn_W_qkv"]).T),
        conv_w=cont(f32(inputs["L0_attn_conv_w"]).reshape(48, 128, 4)
                    .transpose(1, 0, 2).reshape(128, 192)),
        wabT=cont(f32(inputs["L0_attn_W_ab"]).T.reshape(16, 128, 32)
                  .transpose(1, 0, 2).reshape(128, 512)),
        b_ab=cont(np.broadcast_to(f32(inputs["L0_attn_b_ab"]).reshape(1, 32),
                                  (128, 32))),
        wgT=cont(f32(inputs["L0_attn_W_g"]).T),
        woT=cont(f32(inputs["L0_attn_W_o"]).T),
        gate_wT=cont(f32(inputs["L0_ffn_gate_w"]).T.reshape(16, 128, 8)
                     .transpose(1, 0, 2).reshape(128, 128)),
        gate_b=cont(np.broadcast_to(f32(inputs["L0_ffn_gate_b"]).reshape(1, 8),
                                    (128, 8))),
        gate_bias=cont(np.broadcast_to(f32(inputs["L0_ffn_bias"]).reshape(1, 8),
                                       (128, 8))),
        w1T=cont(f32(inputs["L0_ffn_w1"]).transpose(0, 2, 1)),
        b1T=cont(f32(inputs["L0_ffn_b1"]).T.reshape(11, 128, 8)
                 .transpose(1, 0, 2).reshape(128, 88)),
        w2T=cont(f32(inputs["L0_ffn_w2"]).transpose(0, 2, 1)),
        b2=cont(f32(inputs["L0_ffn_b2"])),
        idb=eye_b, idf=eye_f, maskS=maskS, maskI=maskI,
        onesb=np.ones((8, 128), dtype=ml_dtypes.bfloat16),
        onescolb=np.ones((128, 1), dtype=ml_dtypes.bfloat16),
        poff=_poff_vec(half),
        onesf=np.ones((1, 128), dtype=np.float32),
        ones8=np.ones((8, 128), dtype=np.float32),
        selb=selb,
        m3=np.full((128, 1), float(half), dtype=np.float32),
        epscol=np.full((128, 1), 1e-5, dtype=np.float32),
        m3w=np.full((128, 128), float(half), dtype=np.float32),
    )


def build_program():
    nc = bass.Bass("TRN2", target_bir_lowering=False)

    def din(name, shape, dt=F32):
        return nc.dram_tensor(name, list(shape), dt, kind="ExternalInput").ap()

    x_in = din("x_in", [T, D])
    wqkvT = din("wqkvT", [D, 3 * D])
    conv_w = din("conv_w", [128, 192])
    wabT = din("wabT", [128, 512])
    b_ab = din("b_ab", [128, 32])
    wgT = din("wgT", [D, D])
    woT = din("woT", [D, D])
    gate_wT = din("gate_wT", [128, 128])
    gate_b = din("gate_b", [128, NE])
    gate_bias = din("gate_bias", [128, NE])
    w1T = din("w1T", [NE, D, HF])
    b1T = din("b1T", [128, 88])
    w2T = din("w2T", [NE, HF, D])
    b2 = din("b2", [NE, D])
    idb_i = din("idb", [128, 128], BF16)
    idf_i = din("idf", [128, 128])
    maskS_i = din("maskS", [128, 128])
    maskI_i = din("maskI", [128, 128])
    onesb_i = din("onesb", [8, 128], BF16)
    onescolb_i = din("onescolb", [128, 1], BF16)
    poff_i = din("poff", [1, 80], mybir.dt.int32)
    onesf_i = din("onesf", [1, 128])
    ones8_i = din("ones8", [8, 128])
    selb_i = din("selb", [8, 8 * 128], BF16)
    m3_i = din("m3", [128, 1])
    eps_i = din("epscol", [128, 1])
    m3w_i = din("m3w", [128, 128])

    y_out = nc.dram_tensor("y_out", [T, D], F32, kind="ExternalOutput").ap()

    def dint(name, shape, dt=F32):
        return nc.dram_tensor(name, list(shape), dt, kind="Internal").ap()

    x1d = dint("x1d", [T, D])
    x2d = dint("x2d", [T, D])
    w1b = dint("w1b", [NE, D, HF], BF16)
    w2b = dint("w2b", [NE, HF, D], BF16)
    hdram = dint("hdram", [NE, 128, 11 * 512], BF16)
    zpre = dint("zpre", [3 * D, T], BF16)
    lnfm2d = dint("lnfm2d", [128, 16 * 1024], BF16)
    qkv_out_t = nc.dram_tensor("qkv_out", [3 * D, T], BF16, kind="Internal")
    qkv_gat_t = nc.dram_tensor("qkv_gat", [2, 3 * D, T], BF16, kind="Internal")
    qkv_out = qkv_out_t.ap()
    qkv_gat = qkv_gat_t.ap()
    ab_out = dint("ab_out", [T, 32])
    ab_gat = dint("ab_gat", [2, T, 32])
    ztail_in = dint("ztail_in", [128, 144], BF16)
    ztail_gat = dint("ztail_gat", [2, 128, 144], BF16)
    o_out = dint("o_out", [T, D], BF16)
    o_gat_t = nc.dram_tensor("o_gat", [2, T, D], BF16, kind="Internal")
    o_gat = o_gat_t.ap()

    final_deps = []

    ctx = ExitStack()
    with ctx:
        p = Prog(nc, ctx)
        sb = lambda name, shape, dt=F32: ctx.enter_context(
            nc.sbuf_tensor(name, list(shape), dt))
        c_idb = sb("c_idb", [128, 128], BF16)
        c_idf = sb("c_idf", [128, 128])
        c_maskS = sb("c_maskS", [128, 128])
        c_maskI = sb("c_maskI", [128, 128])
        c_onesb = sb("c_onesb", [8, 128], BF16)
        c_onescolb = sb("c_onescolb", [128, 1], BF16)
        c_poff = sb("c_poff", [1, 80], mybir.dt.int32)
        c_onesf = sb("c_onesf", [1, 128])
        c_ones8 = sb("c_ones8", [8, 128])
        c_selb = sb("c_selb", [8, 8 * 128], BF16)
        c_m3 = sb("c_m3", [128, 1])
        c_eps = sb("c_eps", [128, 1])
        c_m3w = sb("c_m3w", [128, 128])
        c_cw = sb("c_cw", [128, 48 * 4])
        c_b1T = sb("c_b1T", [128, 11 * 8])
        c_b2b = sb("c_b2b", [8, D], BF16)
        c_gwT = sb("c_gwT", [128, 16 * 8], BF16)
        c_wabT = sb("c_wabT", [128, 16 * 32], BF16)
        c_gbias = sb("c_gbias", [128, 8])
        c_abbias = sb("c_abbias", [128, 32])
        c_scr = sb("c_scr", [128, 1024])

        PB = [ctx.enter_context(nc.psum_tensor(f"pb{i}", [128, 512], F32))
              for i in range(8)]

        for dst, src in [
            (c_idb[:], idb_i), (c_idf[:], idf_i), (c_maskS[:], maskS_i),
            (c_maskI[:], maskI_i), (c_onesb[:], onesb_i), (c_onesf[:], onesf_i),
            (c_ones8[:], ones8_i), (c_selb[:], selb_i), (c_m3[:], m3_i),
            (c_eps[:], eps_i), (c_onescolb[:], onescolb_i), (c_poff[:], poff_i),
            (c_m3w[:], m3w_i),
            (c_cw[:], conv_w),
            (c_b1T[:], b1T),
        ]:
            p.step("syq", lambda sy, d=dst, s=src: sy.dma_start(out=d, in_=s), tag="cload", ch="c")
        if CUT == "c1":
            p.barrier()
            p.final_wait(final_deps)
            return nc
        for h2 in range(2):
            p.step("syq", lambda sy, h2=h2: sy.dma_start(
                out=c_scr[0:8, 0:1024], in_=b2[:, h2 * 1024:(h2 + 1) * 1024]),
                wr=["scr"], tag="b2", ch="c")
            p.step("dve", lambda v, h2=h2: v.tensor_copy(
                c_b2b[:, h2 * 1024:(h2 + 1) * 1024], c_scr[0:8, 0:1024]), rd=["scr"])
        p.barrier()
        if CUT == "c2":
            p.final_wait(final_deps)
            return nc
        p.step("syq", lambda sy: sy.dma_start(
            out=c_scr[:, 0:128], in_=gate_wT),
            wr=["scr1"], tag="gw", ch="c")
        p.step("syq", lambda sy: sy.dma_start(
            out=c_scr[:, 128:640], in_=wabT),
            wr=["scr2"], tag="ab", ch="c")
        p.step("syq", lambda sy: sy.dma_start(out=c_gbias[:], in_=gate_b),
               wr=["scr3"], ch="c")
        p.step("syq", lambda sy: sy.dma_start(out=c_scr[:, 640:648], in_=gate_bias),
               wr=["scr4"], ch="c")
        p.step("syq", lambda sy: sy.dma_start(out=c_abbias[:], in_=b_ab),
               wr=["scr5"], ch="c")
        p.step("dve", lambda v: v.tensor_copy(c_gwT[:], c_scr[:, 0:128]), rd=["scr1"])
        p.step("dve", lambda v: v.tensor_copy(c_wabT[:], c_scr[:, 128:640]), rd=["scr2"])
        p.step("dve", lambda v: v.tensor_add(
            c_gbias[:], c_gbias[:], c_scr[:, 640:648]),
            rd=["scr4"], wr=["scr3"], tag="gbsum")
        p.barrier()

        if CUT == "consts":
            p.final_wait(final_deps)
            return nc

        def emit_ffn(call, src_x, dst_x, src_res, dst_res):
            f32w = (call == 1)
            with ExitStack() as fs:
                fsb = lambda name, shape, dt=F32: fs.enter_context(
                    nc.sbuf_tensor(f"f{call}{name}", list(shape), dt))
                lnfm = fsb("lnfm", [128, 16 * 512], BF16)
                lnv = lnfm[:].rearrange("p (k n) -> p k n", k=16)
                xt = [fsb(f"xt{i}", [128, D]) for i in range(2)]
                st24 = fsb("st24", [128, 24])
                mv = fsb("mv", [128, 2])
                rstd = fsb("rstd", [128, 2])
                xnb = fsb("xnb", [128, D], BF16)
                s_sc = fsb("s_sc", [128, 8])
                s_e1 = fsb("s_e1", [128, 8])
                s_mk = fsb("s_mk", [128, 8])
                s_e2 = fsb("s_e2", [128, 8])
                s_m1 = fsb("s_m1", [128, 1])
                s_m2 = fsb("s_m2", [128, 1])
                s_g1 = fsb("s_g1", [128, 1])
                s_g2 = fsb("s_g2", [128, 1])
                s_t1 = fsb("s_t1", [128, 8])
                combb = fsb("combb", [128, 32], BF16)
                comb_fm = fsb("comb_fm", [8, 512], BF16)
                comb_bc = [fsb(f"cbc{i}", [128, 512], BF16) for i in range(2)]
                w1sb = [fsb(f"w1sb{i}", [128, 16 * HF], BF16) for i in range(2)]
                wstage = fsb("wstage", [128, 2816]) if f32w else None
                hstage = fsb("hstage", [128, 11 * 512], BF16)
                hbuf = [fsb(f"hbuf{i}", [128, 11 * 512], BF16) for i in range(2)]
                w2sb = [fsb(f"w2sb{i}", [128, 1024], BF16) for i in range(2)]
                ystage = fsb("ystage", [128, 4 * 512])
                xr = [fsb(f"xr{i}", [128, 512]) for i in range(2)]

                for tt in range(2):
                    tok0 = tt * 512
                    for ttt in range(4):
                        r0 = tok0 + ttt * 128
                        xb = f"xt{ttt % 2}"
                        p.step("scq", lambda s, r0=r0, ttt=ttt: s.dma_start(
                            out=xt[ttt % 2][:], in_=src_x[r0:r0 + 128, :]),
                            rd=[f"{src_res}.{r0}.0.0", f"{src_res}.{r0}.0.1",
                                f"{src_res}.{r0}.1.0", f"{src_res}.{r0}.1.1"],
                            wr=[xb], tag=f"ldx{tt}.{ttt}", ch=f"xt{ttt % 2}")
                        for q in range(4):
                            p.step("dve", lambda v, q=q, ttt=ttt: v.bn_stats(
                                st24[:, q * 6:(q + 1) * 6],
                                xt[ttt % 2][:, q * 512:(q + 1) * 512]),
                                rd=[xb], wr=[f"st24.{q}"], tag=f"bn{ttt}.{q}", wb=True)
                        p.step("dve", lambda v: v.bn_aggr(mv[:], st24[:]),
                               rd=["st24.0", "st24.1", "st24.2", "st24.3"],
                               wr=["mv"], tag="aggr", wb=True)
                        p.step("act", lambda s: s.activation(
                            rstd[:, 0:1], mv[:, 1:2], AF.Ln, bias=c_eps[:, :]),
                            rd=["mv"], wr=["rstd0"])
                        p.step("act", lambda s: s.activation(
                            rstd[:, 1:2], rstd[:, 0:1], AF.Exp, scale=-0.5),
                            rd=["rstd0"], wr=["rstd"])
                        p.step("dve", lambda v, ttt=ttt: v.tensor_scalar(
                            xnb[:], xt[ttt % 2][:], mv[:, 0:1], rstd[:, 1:2],
                            ALU.subtract, ALU.mult),
                            rd=[xb, "mv", "rstd"], wr=["xnb"], tag=f"nrm{ttt}")
                        for bg in range(4):
                            for q in range(4):
                                k = bg * 4 + q
                                p.step("pe", lambda t, k=k, bg=bg, q=q: t.matmul(
                                    PB[bg][:, q * 128:(q + 1) * 128],
                                    xnb[:, k * 128:(k + 1) * 128], c_idb[:, :],
                                    start=True, stop=True),
                                    rd=["xnb"], wr=[f"pb{bg}"], tag=f"trmm{ttt}.{bg}")
                            p.step("act", lambda s, bg=bg, ttt=ttt: s.activation(
                                lnv[:, bg * 4:(bg + 1) * 4, ttt * 128:(ttt + 1) * 128],
                                PB[bg][:, 0:512].rearrange("p (k n) -> p k n", k=4),
                                AF.Copy),
                                rd=[f"pb{bg}"], wr=[f"lnfm.{bg}.{ttt}"],
                                tag=f"trev{ttt}.{bg}")
                    if CUT == "ln1" and tt == 0:
                        p.barrier()
                        return
                    if CUT == "ln":
                        p.barrier()
                        return
                    for ttt in range(4):
                        for k in range(16):
                            p.step("pe", lambda t, k=k, ttt=ttt: t.matmul(
                                PB[4][:, 0:8],
                                lnv[:, k, ttt * 128:(ttt + 1) * 128],
                                c_gwT[:, k * 8:(k + 1) * 8],
                                start=(k == 0), stop=(k == 15)),
                                rd=[f"lnfm.{k // 4}.{ttt}"],
                                wr=["pb4"] if k in (0, 15) else [],
                                tag=f"scmm{ttt}.{k}")
                        p.step("dve", lambda v: v.tensor_add(
                            s_sc[:], PB[4][:, 0:8], c_gbias[:]),
                            rd=["pb4"], wr=["s_sc"])
                        p.step("dve", lambda v: v.reduce_max(s_m1[:], s_sc[:], axis=AX),
                               rd=["s_sc"], wr=["s_m1"], wb=True)
                        p.step("dve", lambda v: v.tensor_scalar(
                            s_e1[:], s_sc[:], s_m1[:, :], None, ALU.is_equal),
                            rd=["s_sc", "s_m1"], wr=["s_e1"])
                        p.step("dve", lambda v: v.scalar_tensor_tensor(
                            s_mk[:], s_e1[:], -1e30, s_sc[:], ALU.mult, ALU.add),
                            rd=["s_e1", "s_sc"], wr=["s_mk"])
                        p.step("dve", lambda v: v.reduce_max(s_m2[:], s_mk[:], axis=AX),
                               rd=["s_mk"], wr=["s_m2"], wb=True)
                        p.step("dve", lambda v: v.tensor_scalar(
                            s_e2[:], s_mk[:], s_m2[:, :], None, ALU.is_equal),
                            rd=["s_mk", "s_m2"], wr=["s_e2"])
                        p.step("dve", lambda v: v.tensor_sub(s_m1[:], s_m1[:], s_m2[:]),
                               rd=["s_m2"], wr=["s_m1"])
                        p.step("act", lambda s: s.activation(
                            s_g1[:], s_m1[:, :], AF.Sigmoid), rd=["s_m1"], wr=["s_g1"])
                        p.step("act", lambda s: s.activation(
                            s_g2[:], s_m1[:, :], AF.Sigmoid, scale=-1.0),
                            rd=["s_m1"], wr=["s_g2"])
                        p.step("dve", lambda v: v.tensor_scalar(
                            s_t1[:], s_e1[:], s_g1[:, :], None, ALU.mult),
                            rd=["s_e1", "s_g1"], wr=["s_t1"])
                        p.step("dve", lambda v, ttt=ttt: v.scalar_tensor_tensor(
                            combb[:, ttt * 8:(ttt + 1) * 8], s_e2[:], s_g2[:, :], s_t1[:],
                            ALU.mult, ALU.add),
                            rd=["s_e2", "s_g2", "s_t1"], wr=[f"combb.{ttt}"], tag="comb")
                        p.step("pe", lambda t, ttt=ttt: t.matmul(
                            PB[5][0:8, ttt * 128:(ttt + 1) * 128],
                            combb[:, ttt * 8:(ttt + 1) * 8], c_idb[:, :],
                            start=True, stop=True),
                            rd=[f"combb.{ttt}"], wr=["pb5"], tag="cmbT")
                        p.step("dve", lambda v, ttt=ttt: v.tensor_copy(
                            comb_fm[:, ttt * 128:(ttt + 1) * 128],
                            PB[5][0:8, ttt * 128:(ttt + 1) * 128]),
                            rd=["pb5"], wr=[f"comb_fm.{ttt}"], tag="cfm")
                    cfm_all = ["comb_fm.0", "comb_fm.1", "comb_fm.2", "comb_fm.3"]
                    if CUT == "scores":
                        p.barrier()
                        return
                    p.barrier()

                    for e in range(NE):
                        wbuf = f"w1sb{e % 2}"
                        for k in range(16):
                            if f32w and tt == 0:
                                sl = k % 2
                                p.step("syq", lambda sy, e=e, k=k, sl=sl: sy.dma_start(
                                    out=wstage[:, sl * 1408:(sl + 1) * 1408],
                                    in_=w1T[e, k * 128:(k + 1) * 128, :]),
                                    wr=[f"wst{sl}"], tag=f"w1ld{e}.{k}", ch=f"wst{sl}")
                                p.step("dve", lambda v, e=e, k=k, sl=sl: v.tensor_copy(
                                    w1sb[e % 2][:, k * HF:(k + 1) * HF],
                                    wstage[:, sl * 1408:(sl + 1) * 1408]),
                                    rd=[f"wst{sl}"],
                                    wr=([wbuf] if k in (0, 15) else []) + [f"{wbuf}.k{k}"],
                                    tag=f"w1cast{e}.{k}")
                                p.step("gpq", lambda g, e=e, k=k: g.dma_start(
                                    out=w1b[e, k * 128:(k + 1) * 128, :],
                                    in_=w1sb[e % 2][:, k * HF:(k + 1) * HF]),
                                    rd=[f"{wbuf}.k{k}"], wr=[f"w1b{e}"] if k == 15 else [],
                                    tag=f"w1wb{e}.{k}", ch="wb1")
                            else:
                                p.step("syq", lambda sy, e=e, k=k: sy.dma_start(
                                    out=w1sb[e % 2][:, k * HF:(k + 1) * HF],
                                    in_=w1b[e, k * 128:(k + 1) * 128, :]),
                                    rd=[f"w1b{e}"] if k == 0 else [],
                                    wr=[wbuf] if k in (0, 15) else [],
                                    tag=f"w1ld{e}.{k}", ch=f"w1{e % 2}")
                        p.step("pe", lambda t, e=e: t.matmul(
                            PB[4][:, 0:512], c_selb[0:8, e * 128:(e + 1) * 128],
                            comb_fm[0:8, :], start=True, stop=True),
                            rd=cfm_all, wr=["pb4"], tag=f"cbcmm{e}")
                        p.step("dve", lambda v, e=e: v.tensor_copy(
                            comb_bc[e % 2][:], PB[4][:, 0:512]),
                            rd=["pb4"], wr=[f"cbc{e % 2}"], tag=f"cbcev{e}")
                        for hc in range(11):
                            bank = 6 + (e * 11 + hc) % 2
                            for k in range(16):
                                p.step("pe", lambda t, k=k, hc=hc, e=e, bank=bank:
                                       t.matmul(PB[bank][:, 0:512],
                                                w1sb[e % 2][:, k * HF + hc * 128:
                                                            k * HF + (hc + 1) * 128],
                                                lnv[:, k, :],
                                                start=(k == 0), stop=(k == 15)),
                                       rd=[wbuf] if k in (0, 15) else [],
                                       wr=[f"pb{bank}"] if k in (0, 15) else [],
                                       tag=f"hmm{e}.{hc}.{k}")
                            p.step("act", lambda s, e=e, hc=hc, bank=bank: s.activation(
                                hstage[:, hc * 512:(hc + 1) * 512], PB[bank][:, 0:512],
                                AF.Silu, bias=c_b1T[:, hc * 8 + e: hc * 8 + e + 1]),
                                rd=[f"pb{bank}"], wr=[f"hstage.{hc}"], tag=f"sil{e}.{hc}")
                            p.step("dve", lambda v, hc=hc, e=e: v.tensor_mul(
                                hstage[:, hc * 512:(hc + 1) * 512],
                                hstage[:, hc * 512:(hc + 1) * 512], comb_bc[e % 2][:]),
                                rd=[f"cbc{e % 2}"], wr=[f"hstage.{hc}"],
                                tag=f"hm{e}.{hc}")
                        p.step("gpq", lambda g, e=e: g.dma_start(
                            out=hdram[e], in_=hstage[:]),
                            rd=[f"hstage.{hc2}" for hc2 in range(11)],
                            wr=[f"hdram{e}"], tag=f"hst{e}", ch="hd")
                    if CUT == "h":
                        p.barrier()
                        return
                    p.barrier()

                    for ocg in range(2):
                        ymm_last = None
                        for oc in range(8):
                            goc = ocg * 8 + oc
                            p.step("pe", lambda t, oc=oc, goc=goc: t.matmul(
                                PB[oc][:, 0:512], c_b2b[0:8, goc * 128:(goc + 1) * 128],
                                comb_fm[0:8, :], start=True, stop=False,
                                skip_group_check=True),
                                rd=cfm_all, wr=[f"pb{oc}"], tag=f"yb2.{ocg}.{oc}")
                        for e in range(NE):
                            p.step("gpq", lambda g, e=e: g.dma_start(
                                out=hbuf[e % 2][:], in_=hdram[e]),
                                rd=[f"hdram{e}"], wr=[f"hbuf{e % 2}"],
                                tag=f"hld{ocg}.{e}", ch=f"hb{e % 2}")
                            for hc in range(11):
                                wi = (e * 11 + hc) % 2
                                if f32w and tt == 0:
                                    p.step("syq", lambda sy, e=e, hc=hc, ocg=ocg, wi=wi:
                                           sy.dma_start(
                                               out=wstage[:, wi * 1408:wi * 1408 + 1024],
                                               in_=w2T[e, hc * 128:(hc + 1) * 128,
                                                       ocg * 1024:(ocg + 1) * 1024]),
                                           wr=[f"wst{wi}"], tag=f"w2ld{ocg}.{e}.{hc}",
                                           ch=f"wst{wi}")
                                    p.step("dve", lambda v, wi=wi: v.tensor_copy(
                                        w2sb[wi][:], wstage[:, wi * 1408:wi * 1408 + 1024]),
                                        rd=[f"wst{wi}"], wr=[f"w2sb{wi}"],
                                        tag=f"w2cast{ocg}.{e}.{hc}")
                                    p.step("gpq", lambda g, e=e, hc=hc, ocg=ocg, wi=wi:
                                           g.dma_start(
                                               out=w2b[e, hc * 128:(hc + 1) * 128,
                                                       ocg * 1024:(ocg + 1) * 1024],
                                               in_=w2sb[wi][:]),
                                           rd=[f"w2sb{wi}"], wr=[f"w2b{e}.{ocg}"],
                                           tag=f"w2wb{ocg}.{e}.{hc}", ch="wb2")
                                else:
                                    p.step("syq", lambda sy, e=e, hc=hc, ocg=ocg, wi=wi:
                                           sy.dma_start(
                                               out=w2sb[wi][:],
                                               in_=w2b[e, hc * 128:(hc + 1) * 128,
                                                       ocg * 1024:(ocg + 1) * 1024]),
                                           rd=[f"w2b{e}.{ocg}"] if hc == 0 else [],
                                           wr=[f"w2sb{wi}"],
                                           tag=f"w2ld{ocg}.{e}.{hc}", ch=f"w2{wi}")
                                for oc in range(8):
                                    s = p.step("pe", lambda t, oc=oc, hc=hc, wi=wi, e=e:
                                               t.matmul(PB[oc][:, 0:512],
                                                        w2sb[wi][:, oc * 128:(oc + 1) * 128],
                                                        hbuf[e % 2][:, hc * 512:(hc + 1) * 512],
                                                        start=False,
                                                        stop=(e == NE - 1 and hc == 10),
                                                        skip_group_check=True),
                                               rd=([f"w2sb{wi}", f"hbuf{e % 2}"]
                                                   if oc == 0 else []) +
                                               ([f"w2sb{wi}", f"hbuf{e % 2}"]
                                                if oc == 7 else []),
                                               wr=[f"pb{oc}"] if (e == NE - 1 and hc == 10)
                                               else [],
                                               tag=f"ymm{ocg}.{e}.{hc}.{oc}")
                                    ymm_last = s
                        for half in range(2):
                            for q4 in range(4):
                                oc = half * 4 + q4
                                p.step("act", lambda s, oc=oc, q4=q4: s.activation(
                                    ystage[:, q4 * 512:(q4 + 1) * 512], PB[oc][:, 0:512],
                                    AF.Copy, scale=0.5),
                                    rd=[f"pb{oc}"], extra=[ymm_last],
                                    wr=[f"ystage.{q4}"], tag=f"yev{ocg}.{oc}")
                            for ttt in range(4):
                                r0 = tok0 + ttt * 128
                                c0 = ocg * 1024 + half * 512
                                bank = ttt if half == 0 else 4 + ttt
                                xb2 = f"xr{ttt % 2}"
                                p.step("scq", lambda s, r0=r0, ttt=ttt, c0=c0: s.dma_start(
                                    out=xr[ttt % 2][:], in_=src_x[r0:r0 + 128, c0:c0 + 512]),
                                    rd=[f"{src_res}.{r0}.{ocg}.0",
                                        f"{src_res}.{r0}.{ocg}.1"],
                                    wr=[xb2], tag=f"ldxr{ocg}.{half}.{ttt}",
                                    ch=f"xr{ttt % 2}")
                                for q in range(4):
                                    p.step("pe", lambda t, q=q, ttt=ttt, bank=bank:
                                           t.transpose(PB[bank][:, q * 128:(q + 1) * 128],
                                                       ystage[:, q * 512 + ttt * 128:
                                                               q * 512 + (ttt + 1) * 128],
                                                       c_idf[:, :]),
                                           rd=[f"ystage.{q}"], wr=[f"pb{bank}"],
                                           tag=f"ytr{ocg}.{ttt}.{half}.{q}")
                                p.step("dve", lambda v, ttt=ttt, bank=bank:
                                       v.tensor_add(xr[ttt % 2][:], xr[ttt % 2][:],
                                                    PB[bank][:, 0:512]),
                                       rd=[f"pb{bank}"], wr=[xb2],
                                       tag=f"yadd{ocg}.{ttt}.{half}")
                                stx = p.step("gpq", lambda g, r0=r0, ttt=ttt, c0=c0:
                                             g.dma_start(
                                                 out=dst_x[r0:r0 + 128, c0:c0 + 512],
                                                 in_=xr[ttt % 2][:]),
                                             rd=[xb2],
                                             wr=[f"{dst_res}.{r0}.{ocg}.{half}"],
                                             tag=f"stx{ocg}.{half}.{ttt}",
                                             ch=f"st_{dst_res}")
                                if dst_x is y_out:
                                    final_deps.append(stx)
                    if CUT == "y":
                        p.barrier()
                        return
                    p.barrier()
            return


        def emit_attn(src_x, dst_x, src_res, dst_res):
            NH = 8
            with ExitStack() as s1:
                asb = lambda name, shape, dt=F32: s1.enter_context(
                    nc.sbuf_tensor(f"a1{name}", list(shape), dt))
                lnfm2 = asb("lnfm2", [128, 16 * 1024], BF16)
                ln2v = lnfm2[:].rearrange("p (k n) -> p k n", k=16)
                xt = [asb(f"xt{i}", [128, D]) for i in range(2)]
                st24 = asb("st24", [128, 24])
                mv = asb("mv", [128, 2])
                rstd = asb("rstd", [128, 2])
                xnb = asb("xnb", [128, D], BF16)
                wqb = [asb(f"wqb{i}", [128, 16 * 1024], BF16) for i in range(2)]
                wstg = [asb(f"wstg{i}", [128, 1024]) for i in range(2)]
                zstage = [asb(f"zst{i}", [128, 512], BF16) for i in range(2)]
                ztail_sb = asb("ztail", [128, 144], BF16)
                abt = asb("abt", [128, 8 * 32])

                for ttt in range(8):
                    r0 = ttt * 128
                    xb = f"axt{ttt % 2}"
                    p.step("scq", lambda s, r0=r0, ttt=ttt: s.dma_start(
                        out=xt[ttt % 2][:], in_=src_x[r0:r0 + 128, :]),
                        rd=[f"{src_res}.{r0}.0.0", f"{src_res}.{r0}.0.1",
                            f"{src_res}.{r0}.1.0", f"{src_res}.{r0}.1.1"],
                        wr=[xb], tag=f"aldx{ttt}", ch=f"axt{ttt % 2}")
                    for q in range(4):
                        p.step("dve", lambda v, q=q, ttt=ttt: v.bn_stats(
                            st24[:, q * 6:(q + 1) * 6],
                            xt[ttt % 2][:, q * 512:(q + 1) * 512]),
                            rd=[xb], wr=[f"ast24.{q}"], tag=f"abn{ttt}.{q}", wb=True)
                    p.step("dve", lambda v: v.bn_aggr(mv[:], st24[:]),
                           rd=["ast24.0", "ast24.1", "ast24.2", "ast24.3"],
                           wr=["amv"], tag="aaggr", wb=True)
                    p.step("act", lambda s: s.activation(
                        rstd[:, 0:1], mv[:, 1:2], AF.Ln, bias=c_eps[:, :]),
                        rd=["amv"], wr=["arstd0"])
                    p.step("act", lambda s: s.activation(
                        rstd[:, 1:2], rstd[:, 0:1], AF.Exp, scale=-0.5),
                        rd=["arstd0"], wr=["arstd"])
                    p.step("dve", lambda v, ttt=ttt: v.tensor_scalar(
                        xnb[:], xt[ttt % 2][:], mv[:, 0:1], rstd[:, 1:2],
                        ALU.subtract, ALU.mult),
                        rd=[xb, "amv", "arstd"], wr=["axnb"], tag=f"anrm{ttt}")
                    for bg in range(4):
                        for q in range(4):
                            k = bg * 4 + q
                            p.step("pe", lambda t, k=k, bg=bg, q=q: t.matmul(
                                PB[bg][:, q * 128:(q + 1) * 128],
                                xnb[:, k * 128:(k + 1) * 128], c_idb[:, :],
                                start=True, stop=True),
                                rd=["axnb"], wr=[f"pb{bg}"], tag=f"atrmm{ttt}.{bg}")
                        p.step("act", lambda s, bg=bg, ttt=ttt: s.activation(
                            ln2v[:, bg * 4:(bg + 1) * 4, ttt * 128:(ttt + 1) * 128],
                            PB[bg][:, 0:512].rearrange("p (k n) -> p k n", k=4),
                            AF.Copy),
                            rd=[f"pb{bg}"], wr=[f"aln.{bg}.{ttt}"],
                            tag=f"atrev{ttt}.{bg}")
                    for k in range(16):
                        p.step("pe", lambda t, k=k, ttt=ttt: t.matmul(
                            PB[4][:, 0:32],
                            ln2v[:, k, ttt * 128:(ttt + 1) * 128],
                            c_wabT[:, k * 32:(k + 1) * 32],
                            start=(k == 0), stop=(k == 15)),
                            rd=[f"aln.{k // 4}.{ttt}"],
                            wr=["pb4"] if k in (0, 15) else [],
                            tag=f"abmm{ttt}.{k}")
                    p.step("dve", lambda v, ttt=ttt: v.tensor_add(
                        abt[:, ttt * 32:(ttt + 1) * 32], PB[4][:, 0:32], c_abbias[:]),
                        rd=["pb4"], wr=[f"aabt.{ttt}"], tag=f"aabadd{ttt}")
                    p.step("gpq", lambda g, ttt=ttt, r0=r0: g.dma_start(
                        out=ab_out[r0:r0 + 128, :], in_=abt[:, ttt * 32:(ttt + 1) * 32]),
                        rd=[f"aabt.{ttt}"], wr=["ab_out_d"], tag=f"abst{ttt}",
                        ch="abst")
                for k in range(16):
                    p.step("gpq", lambda g, k=k: g.dma_start(
                        out=lnfm2d[:, k * 1024:(k + 1) * 1024],
                        in_=lnfm2[:, k * 1024:(k + 1) * 1024]),
                        rd=[f"aln.{k // 4}.{ttt2}" for ttt2 in range(8)],
                        wr=["lnfm2d_d"], tag=f"lnst{k}", ch="lnst")
                p.barrier()

                for j in range(6):
                    jp = j % 2
                    for k in range(16):
                        if f32w_attn:
                            sl = k % 2
                            p.step("syq", lambda sy, j=j, k=k, sl=sl: sy.dma_start(
                                out=wstg[sl][:],
                                in_=wqkvT[k * 128:(k + 1) * 128,
                                          j * 1024:(j + 1) * 1024]),
                                wr=[f"awst{sl}"], tag=f"wqld{j}.{k}", ch=f"awst{sl}")
                            p.step("dve", lambda v, j=j, k=k, sl=sl, jp=jp: v.tensor_copy(
                                wqb[jp][:, k * 1024:(k + 1) * 1024], wstg[sl][:]),
                                rd=[f"awst{sl}"],
                                wr=([f"awqb{jp}"] if k in (0, 15) else []) +
                                [f"awqb{jp}.k{k}"],
                                tag=f"wqcast{j}.{k}")
                    for oc8 in range(8):
                        bank = 5 + (j * 8 + oc8) % 2
                        for tt2 in range(2):
                            for k in range(16):
                                p.step("pe", lambda t, k=k, oc8=oc8, tt2=tt2, jp=jp, bank=bank:
                                       t.matmul(PB[bank][:, 0:512],
                                                wqb[jp][:, k * 1024 + oc8 * 128:
                                                         k * 1024 + (oc8 + 1) * 128],
                                                ln2v[:, k, tt2 * 512:(tt2 + 1) * 512],
                                                start=(k == 0), stop=(k == 15)),
                                       rd=[f"awqb{jp}"] if k in (0, 15) else [],
                                       wr=[f"pb{bank}"] if k in (0, 15) else [],
                                       tag=f"qmm{j}.{oc8}.{tt2}.{k}")
                            p.step("act", lambda s, tt2=tt2, bank=bank: s.activation(
                                zstage[tt2][:], PB[bank][:, 0:512], AF.Copy),
                                rd=[f"pb{bank}"], wr=[f"azst{tt2}"],
                                tag=f"qev{j}.{oc8}.{tt2}")
                            if tt2 == 1:
                                g_ = j * 8 + oc8
                                p.step("dve", lambda v, g_=g_: v.tensor_copy(
                                    ztail_sb[:, g_ * 3:(g_ + 1) * 3],
                                    zstage[1][:, 509:512]),
                                    rd=["azst1"], wr=[f"aztl.{g_}"],
                                    tag=f"ztl{g_}")
                            p.step("gpq", lambda g, g2=j * 8 + oc8, tt2=tt2: g.dma_start(
                                out=zpre[g2 * 128:(g2 + 1) * 128,
                                         tt2 * 512:(tt2 + 1) * 512],
                                in_=zstage[tt2][:]),
                                rd=[f"azst{tt2}"] + ([f"aztl.{j * 8 + oc8}"]
                                                     if tt2 == 1 else []),
                                wr=["zpre_d"], tag=f"zst{j}.{oc8}.{tt2}", ch="zpst")
                p.step("gpq", lambda g: g.dma_start(out=ztail_in, in_=ztail_sb[:]),
                       rd=[f"aztl.{g_}" for g_ in range(48)], wr=["ztail_in_d"],
                       tag="ztlst", ch="ztl")
                p.barrier()

            p.step("pool", lambda g: g.collective_compute(
                "AllGather", ALU.bypass, replica_groups=PAIRS,
                ins=[ztail_in], outs=[ztail_gat]),
                rd=["ztail_in_d"], wr=["ztail_gat_d"], tag="cc_ztl", wb=True)
            p.step("pool", lambda g: g.collective_compute(
                "AllGather", ALU.bypass, replica_groups=PAIRS,
                ins=[ab_out], outs=[ab_gat]),
                rd=["ab_out_d"], wr=["ab_gat_d"], tag="cc_ab", wb=True)
            p.barrier()

            p.fence("lnst", "zpst", "abst", "ztl")
            with ExitStack() as s2:
                asb = lambda name, shape, dt=F32: s2.enter_context(
                    nc.sbuf_tensor(f"a2{name}", list(shape), dt))
                halo = asb("halo", [128, 144], BF16)
                zw = asb("zw", [128, 8 * 1027], BF16)
                zw3 = zw[:].rearrange("p (g n) -> p g n", g=8)
                facc = asb("facc", [128, 1024])
                znorm = asb("znorm", [128, 8 * 1024], BF16)
                zn3 = znorm[:].rearrange("p (g n) -> p g n", g=8)
                sqb = asb("sqb", [128, 1024], BF16)
                rnr = asb("rnr", [8, 1024])
                rnb = asb("rnb", [8, 1024], BF16)

                p.step("gpq", lambda g: g.dma_start(out=halo[:], in_=ztail_gat[0]),
                       rd=["ztail_gat_d"], wr=["ahalo"], tag="halold", ch="halold")
                p.step("dve", lambda v: v.tensor_scalar(
                    halo[:], halo[:], c_m3[:, :], None, ALU.mult),
                    wr=["ahalo"], tag="halomask")
                for j in range(6):
                    for gg in range(8):
                        g_ = j * 8 + gg
                        p.step("gpq", lambda g, g_=g_, gg=gg: g.dma_start(
                            out=zw3[:, gg, 3:1027],
                            in_=zpre[g_ * 128:(g_ + 1) * 128, :]),
                            rd=["zpre_d"], wr=[f"azw.{gg}"], tag=f"zwld{g_}",
                            ch="zwld")
                    for gg in range(8):
                        g_ = j * 8 + gg
                        p.step("dve", lambda v, g_=g_, gg=gg: v.tensor_copy(
                            zw3[:, gg, 0:3], halo[:, g_ * 3:(g_ + 1) * 3]),
                            rd=["ahalo"], wr=[f"azw.{gg}"], tag=f"zwh{g_}")
                    for gg in range(8):
                        g_ = j * 8 + gg
                        p.step("dve", lambda v, gg=gg, g_=g_: v.tensor_scalar(
                            facc[:], zw3[:, gg, 0:1024],
                            c_cw[:, g_ * 4:g_ * 4 + 1], None, ALU.mult),
                            rd=[f"azw.{gg}"], wr=["afacc"], tag=f"cv0.{g_}")
                        for j2 in range(1, 4):
                            p.step("dve", lambda v, gg=gg, g_=g_, j2=j2:
                                   v.scalar_tensor_tensor(
                                       facc[:], zw3[:, gg, j2:j2 + 1024],
                                       c_cw[:, g_ * 4 + j2:g_ * 4 + j2 + 1],
                                       facc[:], ALU.mult, ALU.add),
                                   rd=[f"azw.{gg}"], wr=["afacc"], tag=f"cv{j2}.{g_}")
                        p.step("act", lambda s, gg=gg: s.activation(
                            zn3[:, gg, :], facc[:], AF.Silu),
                            rd=["afacc"], wr=[f"azn.{gg}"], tag=f"cvsilu.{g_}")
                        if j < 4:
                            p.step("act", lambda s, gg=gg: s.activation(
                                sqb[:], zn3[:, gg, :], AF.Square),
                                rd=[f"azn.{gg}"], wr=["asqb"], tag=f"nsq.{g_}")
                            for hh in range(2):
                                p.step("pe", lambda t, hh=hh: t.matmul(
                                    PB[4 + hh][0:1, 0:512], c_onescolb[:, :],
                                    sqb[:, hh * 512:(hh + 1) * 512],
                                    start=True, stop=True),
                                    rd=["asqb"], wr=[f"pb{4 + hh}"],
                                    tag=f"nsum.{g_}.{hh}")
                                p.step("act", lambda s, hh=hh: s.activation(
                                    rnr[0:1, hh * 512:(hh + 1) * 512],
                                    PB[4 + hh][0:1, 0:512], AF.Ln),
                                    rd=[f"pb{4 + hh}"], wr=[f"arnr{hh}"],
                                    tag=f"nln.{g_}.{hh}")
                                p.step("act", lambda s, hh=hh: s.activation(
                                    rnr[0:1, hh * 512:(hh + 1) * 512],
                                    rnr[0:1, hh * 512:(hh + 1) * 512],
                                    AF.Exp, scale=-0.5),
                                    wr=[f"arnr{hh}"], tag=f"nexp.{g_}.{hh}")
                                p.step("dve", lambda v, hh=hh: v.tensor_scalar(
                                    rnb[0:1, hh * 512:(hh + 1) * 512],
                                    rnr[0:1, hh * 512:(hh + 1) * 512],
                                    1e12, None, ALU.min),
                                    rd=[f"arnr{hh}"], wr=[f"arnb{hh}"],
                                    tag=f"nmin.{g_}.{hh}")
                                p.step("pe", lambda t, hh=hh: t.matmul(
                                    PB[6 + hh][:, 0:512], c_onesb[0:1, :],
                                    rnb[0:1, hh * 512:(hh + 1) * 512],
                                    start=True, stop=True),
                                    rd=[f"arnb{hh}"], wr=[f"pb{6 + hh}"],
                                    tag=f"nbc.{g_}.{hh}")
                                p.step("dve", lambda v, gg=gg, hh=hh: v.tensor_mul(
                                    zn3[:, gg, hh * 512:(hh + 1) * 512],
                                    zn3[:, gg, hh * 512:(hh + 1) * 512],
                                    PB[6 + hh][:, 0:512]),
                                    rd=[f"pb{6 + hh}"], wr=[f"azn.{gg}"],
                                    tag=f"nmul.{g_}.{hh}")
                        p.step("gpq", lambda g, g_=g_, gg=gg: g.dma_start(
                            out=qkv_out[g_ * 128:(g_ + 1) * 128, :],
                            in_=zn3[:, gg, :]),
                            rd=[f"azn.{gg}"], wr=["qkv_out_d"], tag=f"qkst{g_}",
                            ch="qkst")
                p.barrier()

            p.step("pool", lambda g: g.collective_compute(
                "AllGather", ALU.bypass, replica_groups=PAIRS,
                ins=[qkv_out], outs=[qkv_gat]),
                rd=["qkv_out_d"], wr=["qkv_gat_d"], tag="cc_qkv", wb=True)
            p.barrier()

            p.fence("qkst", "zwld")
            sA = ExitStack()
            asb = lambda name, shape, dt=F32: sA.enter_context(
                nc.sbuf_tensor(f"a6{name}", list(shape), dt))
            qkvh = asb("qkvh", [128, 24 * 2048], BF16)
            ab_all = asb("aball", [128, 16 * 32])
            alpha_s = asb("alphas", [128, 128])
            beta_s = asb("betas", [128, 128])
            ell_tm = asb("elltm", [128, 128])
            negb_tm = asb("negbtm", [128, 128])
            O_fm = asb("Ofm", [128, 8 * 2048], BF16)
            S0f = asb("S0f", [128, 8 * 128])
            S0b = asb("S0b", [128, 8 * 128], BF16)

            rt = nc.gpsimd.alloc_register("rt")
            qkvh3 = qkvh[:].rearrange("p (s n) -> p s n", s=24)
            for t3 in range(3):
                for r in range(2):
                    i3 = t3 * 2 + r
                    def fnq(g, i3=i3, t3=t3, r=r):
                        g.reg_load(rt, c_poff[0:1, i3:i3 + 1])
                        return g.dma_start(
                            out=qkvh3[:, t3 * 8:(t3 + 1) * 8,
                                      r * 1024:(r + 1) * 1024],
                            in_=bass.AP(qkv_gat_t, rt,
                                        [[T, 128], [128 * T, 8], [1, 1024]]))
                    p.step("gpq", fnq, rd=["qkv_gat_d", "c_poff"], wr=["aqkvh"],
                           tag=f"qhl{t3}.{r}", ch="qhl")
            for tt3 in range(16):
                r, tl = tt3 // 8, tt3 % 8
                p.step("gpq", lambda g, r=r, tl=tl, tt3=tt3: g.dma_start(
                    out=ab_all[:, tt3 * 32:(tt3 + 1) * 32],
                    in_=ab_gat[r, tl * 128:(tl + 1) * 128, :]),
                    rd=["ab_gat_d"], wr=["aaball"], tag=f"abl{tt3}", ch="abl")
            abv = ab_all[:].rearrange("p (t c) -> p t c", t=16)
            alp3 = alpha_s[:].rearrange("p (t c) -> p t c", t=16)
            bet3 = beta_s[:].rearrange("p (t c) -> p t c", t=16)
            p.step("dve", lambda v: v.tensor_sub(
                alp3, abv[:, :, 8:16], abv[:, :, 0:8]),
                rd=["aaball"], wr=["aalpha"], tag="sel_a1")
            p.step("dve", lambda v: v.scalar_tensor_tensor(
                alp3, alp3, c_m3[:, :], abv[:, :, 0:8], ALU.mult, ALU.add),
                rd=["aaball"], wr=["aalpha"], tag="sel_a2")
            p.step("dve", lambda v: v.tensor_sub(
                bet3, abv[:, :, 24:32], abv[:, :, 16:24]),
                rd=["aaball"], wr=["abeta"], tag="sel_b1")
            p.step("dve", lambda v: v.scalar_tensor_tensor(
                bet3, bet3, c_m3[:, :], abv[:, :, 16:24], ALU.mult, ALU.add),
                rd=["aaball"], wr=["abeta"], tag="sel_b2")
            p.step("act", lambda s: s.activation(alpha_s[:], alpha_s[:], AF.Sigmoid),
                   wr=["aalpha"], tag="sig_a")
            p.step("act", lambda s: s.activation(beta_s[:], beta_s[:], AF.Sigmoid),
                   wr=["abeta"], tag="sig_b")
            p.step("act", lambda s: s.activation(ell_tm[:], alpha_s[:], AF.Ln),
                   rd=["aalpha"], wr=["aell"], tag="log_a")
            p.step("dve", lambda v: v.tensor_scalar(
                ell_tm[:], ell_tm[:], -60.0, None, ALU.max),
                wr=["aell"], tag="clamp_a")
            p.step("dve", lambda v: v.tensor_scalar(
                negb_tm[:], beta_s[:], -1.0, None, ALU.mult),
                rd=["abeta"], wr=["anegb"], tag="negb")
            p.step("dve", lambda v: v.memset(S0f[:], 0.0), wr=["aS0f"], tag="s0z")
            p.step("dve", lambda v: v.memset(S0b[:], 0.0), wr=["aS0b"], tag="s0bz")
            p.barrier()

            with ExitStack() as s6:
                csb = lambda name, shape, dt=F32: s6.enter_context(
                    nc.sbuf_tensor(f"a6c{name}", list(shape), dt))
                lell = csb("lell", [8, 128])
                lrow = csb("lrow", [8, 128])
                dtmp = csb("dtmp", [8, 128])
                drow = csb("drow", [8, 128])
                arow = csb("arow", [8, 128])
                acbr = csb("acbr", [8, 128])
                sc_tm = csb("sctm", [128, 32])
                negA8 = csb("negA8", [128, 8])
                lrbuf = csb("lrbuf", [8, 512])
                rsS = csb("rsS", [128, 512])
                rsI = csb("rsI", [128, 512])
                RtS = csb("RtS", [128, 512], BF16)
                RtI = csb("RtI", [128, 512], BF16)
                tDt = csb("tDt", [128, 512], BF16)
                tD = csb("tD", [128, 512], BF16)
                tW = csb("tW", [128, 512], BF16)
                tMt = csb("tMt", [128, 512], BF16)
                tVt = csb("tVt", [128, 512], BF16)
                tO2 = csb("tO2", [128, 512], BF16)
                tOt = csb("tOt", [128, 512], BF16)
                tKb = csb("tKb", [128, 512], BF16)

                bankctr = [0]

                def nb():
                    bankctr[0] = (bankctr[0] + 1) % 8
                    return bankctr[0]

                for c in range(16):
                    p.barrier()
                    b0 = nb()
                    p.step("pe", lambda t, c=c, b0=b0: t.transpose(
                        PB[b0][0:8, 0:128], ell_tm[:, c * 8:(c + 1) * 8], c_idf[:, :]),
                        rd=["aell"], wr=[f"pb{b0}"], tag=f"ellT{c}")
                    p.step("act", lambda s, b0=b0: s.activation(
                        lell[:], PB[b0][0:8, 0:128], AF.Copy),
                        rd=[f"pb{b0}"], wr=["clell"], tag=f"ellcp{c}")
                    p.step("dve", lambda v: v.tensor_tensor_scan(
                        lrow[:], lell[:], lell[:], 0.0, ALU.add, ALU.bypass),
                        rd=["clell"], wr=["clrow"], tag=f"scan{c}", wb=True)
                    p.step("dve", lambda v: v.tensor_scalar(
                        dtmp[:], lrow[:], lrow[:, 127:128], None, ALU.subtract),
                        rd=["clrow"], wr=["cdtmp"], tag=f"dsub{c}")
                    p.step("act", lambda s: s.activation(drow[:], dtmp[:], AF.Exp,
                                                         scale=-1.0),
                           rd=["cdtmp"], wr=["cdrow"], tag=f"dexp{c}")
                    p.step("act", lambda s: s.activation(arow[:], lrow[:], AF.Exp),
                           rd=["clrow"], wr=["carow"], tag=f"aexp{c}")
                    p.step("dve", lambda v: v.tensor_scalar(
                        acbr[:], c_ones8[:], arow[:, 127:128], None, ALU.mult),
                        rd=["carow"], wr=["cacb"], tag=f"acb{c}")
                    b1 = nb()
                    for qi, (rowt, rname) in enumerate(
                            [(drow, "cdrow"), (arow, "carow"),
                             (acbr, "cacb"), (lrow, "clrow")]):
                        p.step("pe", lambda t, rowt=rowt, qi=qi, b1=b1: t.transpose(
                            PB[b1][:, qi * 8:(qi + 1) * 8], rowt[:, :],
                            c_idf[0:8, 0:8]),
                            rd=[rname], wr=[f"pb{b1}"], tag=f"scT{c}.{qi}")
                    p.step("act", lambda s, b1=b1: s.activation(
                        sc_tm[:], PB[b1][:, 0:32], AF.Copy),
                        rd=[f"pb{b1}"], wr=["csc"], tag=f"scev{c}")
                    p.step("dve", lambda v: v.tensor_scalar(
                        negA8[:], sc_tm[:, 8:16], -1.0, None, ALU.mult),
                        rd=["csc"], wr=["cnegA"], tag=f"negA{c}")

                    if A6STOP and A6STOP == (c, -1, 0):
                        p.barrier()
                        s6.pop_all().close()
                        sA.close()
                        return
                    for g2 in range(2):
                        hs = [g2 * 4 + q for q in range(4)]
                        sl = lambda buf, q: buf[:, q * 128:(q + 1) * 128]
                        qf = lambda h, c=c: qkvh[:, (0 * 8 + h) * 2048 + c * 128:
                                                 (0 * 8 + h) * 2048 + (c + 1) * 128]
                        kf = lambda h, c=c: qkvh[:, (1 * 8 + h) * 2048 + c * 128:
                                                 (1 * 8 + h) * 2048 + (c + 1) * 128]
                        vf = lambda h, c=c: qkvh[:, (2 * 8 + h) * 2048 + c * 128:
                                                 (2 * 8 + h) * 2048 + (c + 1) * 128]
                        bkk = nb()
                        bkq = nb()
                        for q, h in enumerate(hs):
                            p.step("pe", lambda t, h=h, q=q, bkk=bkk: t.matmul(
                                PB[bkk][:, q * 128:(q + 1) * 128], kf(h), kf(h),
                                start=True, stop=True),
                                rd=["aqkvh"], wr=[f"pb{bkk}"], tag=f"gkk{c}.{h}")
                            p.step("pe", lambda t, h=h, q=q, bkq=bkq: t.matmul(
                                PB[bkq][:, q * 128:(q + 1) * 128], kf(h), qf(h),
                                start=True, stop=True),
                                rd=["aqkvh"], wr=[f"pb{bkq}"], tag=f"gkq{c}.{h}")
                        if A6STOP and A6STOP == (c, g2, 9):
                            p.barrier()
                            s6.pop_all().close()
                            sA.close()
                            return
                        brow = nb()
                        bbc = nb()
                        for q, h in enumerate(hs):
                            p.step("pe", lambda t, h=h, q=q, brow=brow: t.transpose(
                                PB[brow][0:1, q * 128:(q + 1) * 128],
                                sc_tm[:, 24 + h:25 + h], c_idf[:, :]),
                                rd=["csc"], wr=[f"pb{brow}"], tag=f"lrT{c}.{h}")
                            p.step("act", lambda s, q=q, brow=brow: s.activation(
                                lrbuf[0:1, q * 128:(q + 1) * 128],
                                PB[brow][0:1, q * 128:(q + 1) * 128], AF.Copy),
                                rd=[f"pb{brow}"], wr=[f"clr{q}"], tag=f"lrcp{c}.{q}")
                            p.step("pe", lambda t, q=q, bbc=bbc: t.matmul(
                                PB[bbc][:, q * 128:(q + 1) * 128],
                                c_ones8[0:1, :], lrbuf[0:1, q * 128:(q + 1) * 128],
                                start=True, stop=True),
                                rd=[f"clr{q}"], wr=[f"pb{bbc}"], tag=f"lbc{c}.{q}")
                        for q, h in enumerate(hs):
                            p.step("dve", lambda v, h=h, q=q, bbc=bbc: v.scalar_tensor_tensor(
                                sl(rsS, q), PB[bbc][:, q * 128:(q + 1) * 128],
                                sc_tm[:, 24 + h:25 + h], c_maskS[:, :],
                                ALU.subtract, ALU.add),
                                rd=[f"pb{bbc}", "csc"], wr=["crsS"], tag=f"rsS{c}.{h}")
                            p.step("dve", lambda v, h=h, q=q, bbc=bbc: v.scalar_tensor_tensor(
                                sl(rsI, q), PB[bbc][:, q * 128:(q + 1) * 128],
                                sc_tm[:, 24 + h:25 + h], c_maskI[:, :],
                                ALU.subtract, ALU.add),
                                rd=[f"pb{bbc}", "csc"], wr=["crsI"], tag=f"rsI{c}.{h}")
                        p.step("act", lambda s: s.activation(RtS[:], rsS[:], AF.Exp),
                               rd=["crsS"], wr=["cRtS"], tag=f"expS{c}.{g2}")
                        p.step("act", lambda s: s.activation(RtI[:], rsI[:], AF.Exp),
                               rd=["crsI"], wr=["cRtI"], tag=f"expI{c}.{g2}")
                        for q, h in enumerate(hs):
                            p.step("dve", lambda v, h=h, q=q, c=c, bkk=bkk:
                                   v.scalar_tensor_tensor(
                                       sl(tDt, q), PB[bkk][:, q * 128:(q + 1) * 128],
                                       negb_tm[:, c * 8 + h:c * 8 + h + 1],
                                       sl(RtS, q), ALU.mult, ALU.mult),
                                   rd=[f"pb{bkk}", "cRtS", "anegb"], wr=["cDt"],
                                   tag=f"bt{c}.{h}")
                            p.step("dve", lambda v, q=q, bkq=bkq: v.tensor_mul(
                                sl(tMt, q), PB[bkq][:, q * 128:(q + 1) * 128],
                                sl(RtI, q)),
                                rd=[f"pb{bkq}", "cRtI"], wr=["cMt"], tag=f"mt{c}.{h}")
                        if A6STOP and A6STOP == (c, g2, 1):
                            p.barrier()
                            s6.pop_all().close()
                            sA.close()
                            return
                        bD0 = nb()
                        bVt = nb()
                        bW0 = nb()
                        for q, h in enumerate(hs):
                            p.step("pe", lambda t, q=q, bD0=bD0: t.matmul(
                                PB[bD0][:, q * 128:(q + 1) * 128], sl(tDt, q),
                                c_idb[:, :], start=True, stop=True),
                                rd=["cDt"], wr=[f"pb{bD0}"], tag=f"d0T{c}.{h}")
                            p.step("pe", lambda t, h=h, q=q, bVt=bVt: t.matmul(
                                PB[bVt][:, q * 128:(q + 1) * 128], vf(h),
                                c_idb[:, :], start=True, stop=True),
                                rd=["aqkvh"], wr=[f"pb{bVt}"], tag=f"vT{c}.{h}")
                            p.step("pe", lambda t, h=h, q=q, bW0=bW0: t.matmul(
                                PB[bW0][:, q * 128:(q + 1) * 128], kf(h),
                                S0b[:, h * 128:(h + 1) * 128],
                                start=True, stop=True),
                                rd=["aqkvh", "aS0b"], wr=[f"pb{bW0}"],
                                tag=f"w0{c}.{h}")
                        p.step("act", lambda s, bD0=bD0: s.activation(
                            tD[:], PB[bD0][:, 0:512], AF.Copy),
                            rd=[f"pb{bD0}"], wr=["cD"], tag=f"d0ev{c}.{g2}")
                        p.step("act", lambda s, bVt=bVt: s.activation(
                            tVt[:], PB[bVt][:, 0:512], AF.Copy),
                            rd=[f"pb{bVt}"], wr=["cVt"], tag=f"vev{c}.{g2}")
                        for q, h in enumerate(hs):
                            p.step("dve", lambda v, h=h, q=q, bW0=bW0:
                                   v.scalar_tensor_tensor(
                                       sl(tW, q), PB[bW0][:, q * 128:(q + 1) * 128],
                                       negA8[:, h:h + 1], sl(tVt, q),
                                       ALU.mult, ALU.add),
                                   rd=[f"pb{bW0}", "cVt", "cnegA"], wr=["cW"],
                                   tag=f"vw{c}.{h}")
                        for kk in range(7):
                            ba = nb()
                            for q, h in enumerate(hs):
                                p.step("pe", lambda t, q=q, ba=ba: t.matmul(
                                    PB[ba][:, q * 128:(q + 1) * 128], sl(tDt, q),
                                    sl(tW, q), start=True, stop=False,
                                    skip_group_check=True),
                                    rd=["cDt", "cW"], wr=[f"pb{ba}"],
                                    tag=f"ap{c}.{g2}.{kk}.{q}a")
                                p.step("pe", lambda t, q=q, ba=ba: t.matmul(
                                    PB[ba][:, q * 128:(q + 1) * 128], c_idb[:, :],
                                    sl(tW, q), start=False, stop=True,
                                    skip_group_check=True),
                                    rd=["cW"], wr=[f"pb{ba}"],
                                    tag=f"ap{c}.{g2}.{kk}.{q}b")
                            if kk < 6:
                                bb = nb()
                                bc = nb()
                                for q in range(4):
                                    p.step("pe", lambda t, q=q, bb=bb: t.matmul(
                                        PB[bb][:, q * 128:(q + 1) * 128], sl(tDt, q),
                                        sl(tD, q), start=True, stop=True),
                                        rd=["cDt", "cD"], wr=[f"pb{bb}"],
                                        tag=f"sqD{c}.{g2}.{kk}.{q}")
                                    p.step("pe", lambda t, q=q, bc=bc: t.matmul(
                                        PB[bc][:, q * 128:(q + 1) * 128], sl(tD, q),
                                        sl(tDt, q), start=True, stop=True),
                                        rd=["cDt", "cD"], wr=[f"pb{bc}"],
                                        tag=f"sqDt{c}.{g2}.{kk}.{q}")
                                p.step("act", lambda s, bb=bb: s.activation(
                                    tD[:], PB[bb][:, 0:512], AF.Copy),
                                    rd=[f"pb{bb}"], wr=["cD"], tag=f"evD{c}.{g2}.{kk}")
                                p.step("act", lambda s, bc=bc: s.activation(
                                    tDt[:], PB[bc][:, 0:512], AF.Copy),
                                    rd=[f"pb{bc}"], wr=["cDt"],
                                    tag=f"evDt{c}.{g2}.{kk}")
                            if kk < 6:
                                p.step("act", lambda s, ba=ba: s.activation(
                                    tW[:], PB[ba][:, 0:512], AF.Copy),
                                    rd=[f"pb{ba}"], wr=["cW"], tag=f"evW{c}.{g2}.{kk}")
                            else:
                                for q, h in enumerate(hs):
                                    p.step("act", lambda s, q=q, h=h, c=c, ba=ba:
                                           s.activation(
                                               sl(tW, q),
                                               PB[ba][:, q * 128:(q + 1) * 128],
                                               AF.Copy,
                                               scale=beta_s[:, c * 8 + h:
                                                            c * 8 + h + 1]),
                                           rd=[f"pb{ba}"], wr=["cW"],
                                           tag=f"evU{c}.{g2}.{q}")
                        if A6STOP and A6STOP == (c, g2, 2):
                            p.barrier()
                            s6.pop_all().close()
                            sA.close()
                            return
                        bO = nb()
                        bO2 = nb()
                        for q, h in enumerate(hs):
                            p.step("pe", lambda t, q=q, bO=bO: t.matmul(
                                PB[bO][:, q * 128:(q + 1) * 128], sl(tMt, q),
                                sl(tW, q), start=True, stop=True),
                                rd=["cMt", "cW"], wr=[f"pb{bO}"], tag=f"o1{c}.{h}")
                            p.step("pe", lambda t, h=h, q=q, bO2=bO2: t.matmul(
                                PB[bO2][:, q * 128:(q + 1) * 128], qf(h),
                                S0b[:, h * 128:(h + 1) * 128],
                                start=True, stop=True),
                                rd=["aqkvh", "aS0b"], wr=[f"pb{bO2}"],
                                tag=f"o2{c}.{h}")
                        for q, h in enumerate(hs):
                            p.step("act", lambda s, q=q, h=h, bO2=bO2: s.activation(
                                sl(tO2, q), PB[bO2][:, q * 128:(q + 1) * 128],
                                AF.Copy, scale=sc_tm[:, 8 + h:9 + h]),
                                rd=[f"pb{bO2}", "csc"], wr=["cO2"],
                                tag=f"o2ev{c}.{h}")
                        p.step("act", lambda s, bO=bO: s.activation(
                            tOt[:], PB[bO][:, 0:512], AF.Copy),
                            rd=[f"pb{bO}"], wr=["cOt"], tag=f"oev{c}.{g2}")
                        p.step("dve", lambda v: v.tensor_add(
                            tOt[:], tOt[:], tO2[:]),
                            rd=["cO2"], wr=["cOt"], tag=f"oadd{c}.{g2}")
                        if A6STOP and A6STOP == (c, g2, 3):
                            p.barrier()
                            s6.pop_all().close()
                            sA.close()
                            return
                        bK = nb()
                        bS = nb()
                        for q, h in enumerate(hs):
                            p.step("pe", lambda t, h=h, q=q, bK=bK: t.matmul(
                                PB[bK][:, q * 128:(q + 1) * 128], kf(h),
                                c_idb[:, :], start=True, stop=True),
                                rd=["aqkvh"], wr=[f"pb{bK}"], tag=f"kT{c}.{h}")
                        for q, h in enumerate(hs):
                            p.step("act", lambda s, q=q, h=h, bK=bK: s.activation(
                                sl(tKb, q), PB[bK][:, q * 128:(q + 1) * 128],
                                AF.Copy, scale=sc_tm[:, h:h + 1]),
                                rd=[f"pb{bK}", "csc"], wr=["cKb"], tag=f"kbev{c}.{h}")
                        for q, h in enumerate(hs):
                            p.step("pe", lambda t, q=q, bS=bS: t.matmul(
                                PB[bS][:, q * 128:(q + 1) * 128], sl(tKb, q),
                                sl(tW, q), start=True, stop=True),
                                rd=["cKb", "cW"], wr=[f"pb{bS}"], tag=f"smm{c}.{h}")
                        for q, h in enumerate(hs):
                            p.step("dve", lambda v, q=q, h=h, bS=bS:
                                   v.scalar_tensor_tensor(
                                       S0f[:, h * 128:(h + 1) * 128],
                                       S0f[:, h * 128:(h + 1) * 128],
                                       sc_tm[:, 16 + h:17 + h],
                                       PB[bS][:, q * 128:(q + 1) * 128],
                                       ALU.mult, ALU.add),
                                   rd=[f"pb{bS}", "csc"], wr=["aS0f"],
                                   tag=f"sup{c}.{h}")
                        p.step("act", lambda s, g2=g2: s.activation(
                            S0b[:, g2 * 512:(g2 + 1) * 512],
                            S0f[:, g2 * 512:(g2 + 1) * 512], AF.Copy),
                            rd=["aS0f"], wr=["aS0b"], tag=f"s0b{c}.{g2}")
                        if A6STOP and A6STOP == (c, g2, 4):
                            p.barrier()
                            s6.pop_all().close()
                            sA.close()
                            return
                        bT = nb()
                        for q, h in enumerate(hs):
                            p.step("pe", lambda t, q=q, bT=bT: t.matmul(
                                PB[bT][:, q * 128:(q + 1) * 128], sl(tOt, q),
                                c_idb[:, :], start=True, stop=True),
                                rd=["cOt"], wr=[f"pb{bT}"], tag=f"oT{c}.{h}")
                        g0 = hs[0]
                        p.step("act", lambda s, c=c, g0=g0, bT=bT: s.activation(
                            O_fm[:].rearrange("p (g n) -> p g n", g=8)[
                                :, g0:g0 + 4, c * 128:(c + 1) * 128],
                            PB[bT][:, 0:512].rearrange("p (q n) -> p q n", q=4),
                            AF.Copy),
                            rd=[f"pb{bT}"], wr=[f"aOfm.{c}.{g2}"],
                            tag=f"ofmev{c}.{g2}")
                        if A6STOP and (c, g2) == A6STOP:
                            p.barrier()
                            s6.pop_all().close()
                            sA.close()
                            return
                if CUT == "a6":
                    p.barrier()
                    sA.close()
                    return
                p.barrier()

            for g_ in range(NH):
                p.step("gpq", lambda g, g_=g_: g.dma_start(
                    out=o_out[g_ * 128:(g_ + 1) * 128, :],
                    in_=O_fm[:, g_ * 2048:(g_ + 1) * 2048]),
                    rd=[f"aOfm.{c2}.{g22}" for c2 in range(16) for g22 in range(2)],
                    wr=["o_out_d"], tag=f"ost{g_}", ch="ost")
            p.barrier()
            p.step("pool", lambda g: g.collective_compute(
                "AllGather", ALU.bypass, replica_groups=PAIRS,
                ins=[o_out], outs=[o_gat]),
                rd=["o_out_d"], wr=["o_gat_d"], tag="cc_o", wb=True)
            p.barrier()
            sA.close()

            p.fence("ost")
            with ExitStack() as s8:
                asb = lambda name, shape, dt=F32: s8.enter_context(
                    nc.sbuf_tensor(f"a8{name}", list(shape), dt))
                Ofull = asb("Ofull", [128, 16 * 1024], BF16)
                lnfm2b = asb("lnfm2b", [128, 16 * 1024], BF16)
                ln2vb = lnfm2b[:].rearrange("p (k n) -> p k n", k=16)
                wgb = [asb(f"wgb{i}", [128, 16 * 1024], BF16) for i in range(2)]
                wstg8 = [asb(f"wstg{i}", [128, 1024]) for i in range(2)]
                gstage = asb("gstage", [128, 512], BF16)
                ystage = asb("ystage", [128, 4 * 512])
                xr = [asb(f"xr{i}", [128, 512]) for i in range(2)]

                Ofull3 = Ofull[:].rearrange("p (s n) -> p s n", s=16)
                for r in range(2):
                    def fno(g, r=r):
                        g.reg_load(rt, c_poff[0:1, 6 + r:7 + r])
                        return g.dma_start(
                            out=Ofull3[:, r * 8:(r + 1) * 8, :],
                            in_=bass.AP(o_gat_t, rt,
                                        [[D, 128], [128 * D, 8], [1, 1024]]))
                    p.step("gpq", fno, rd=["o_gat_d", "c_poff"], wr=["aOfull"],
                           tag=f"ofl{r}", ch="ofl")
                for k in range(16):
                    p.step("syq", lambda sy, k=k: sy.dma_start(
                        out=lnfm2b[:, k * 1024:(k + 1) * 1024],
                        in_=lnfm2d[:, k * 1024:(k + 1) * 1024]),
                        rd=["lnfm2d_d"], wr=["aln2b"], tag=f"lnld{k}", ch="lnld")
                for halfg in range(2):
                    jp = halfg % 2
                    for k in range(16):
                        sl2 = k % 2
                        p.step("syq", lambda sy, halfg=halfg, k=k, sl2=sl2: sy.dma_start(
                            out=wstg8[sl2][:],
                            in_=wgT[k * 128:(k + 1) * 128,
                                    halfg * 1024:(halfg + 1) * 1024]),
                            wr=[f"a8wst{sl2}"], tag=f"wgld{halfg}.{k}",
                            ch=f"a8wst{sl2}")
                        p.step("dve", lambda v, k=k, sl2=sl2, jp=jp: v.tensor_copy(
                            wgb[jp][:, k * 1024:(k + 1) * 1024], wstg8[sl2][:]),
                            rd=[f"a8wst{sl2}"],
                            wr=([f"awgb{jp}"] if k in (0, 15) else []),
                            tag=f"wgcast{halfg}.{k}")
                    for oc8 in range(8):
                        cc = halfg * 8 + oc8
                        for tt2 in range(2):
                            bank = (oc8 * 2 + tt2) % 8
                            for k in range(16):
                                p.step("pe", lambda t, k=k, oc8=oc8, tt2=tt2, jp=jp,
                                       bank=bank: t.matmul(
                                           PB[bank][:, 0:512],
                                           wgb[jp][:, k * 1024 + oc8 * 128:
                                                    k * 1024 + (oc8 + 1) * 128],
                                           ln2vb[:, k, tt2 * 512:(tt2 + 1) * 512],
                                           start=(k == 0), stop=(k == 15)),
                                       rd=(["awgb" + str(jp), "aln2b"]
                                           if k in (0, 15) else []),
                                       wr=[f"pb{bank}"] if k in (0, 15) else [],
                                       tag=f"gmm{halfg}.{oc8}.{tt2}.{k}")
                            p.step("act", lambda s, bank=bank: s.activation(
                                gstage[:], PB[bank][:, 0:512], AF.Silu),
                                rd=[f"pb{bank}"], wr=["agst"],
                                tag=f"gsilu{halfg}.{oc8}.{tt2}")
                            p.step("dve", lambda v, cc=cc, tt2=tt2: v.tensor_mul(
                                Ofull[:, cc * 1024 + tt2 * 512:
                                      cc * 1024 + (tt2 + 1) * 512],
                                Ofull[:, cc * 1024 + tt2 * 512:
                                      cc * 1024 + (tt2 + 1) * 512],
                                gstage[:]),
                                rd=["agst", "aOfull"], wr=["aOfull"],
                                tag=f"gate{halfg}.{oc8}.{tt2}")
                p.barrier()
                for ocg in range(2):
                    jp = ocg % 2
                    for k in range(16):
                        sl2 = k % 2
                        p.step("syq", lambda sy, ocg=ocg, k=k, sl2=sl2: sy.dma_start(
                            out=wstg8[sl2][:],
                            in_=woT[k * 128:(k + 1) * 128,
                                    ocg * 1024:(ocg + 1) * 1024]),
                            wr=[f"a8wst{sl2}"], tag=f"wold{ocg}.{k}",
                            ch=f"a8wst{sl2}")
                        p.step("dve", lambda v, k=k, sl2=sl2, jp=jp: v.tensor_copy(
                            wgb[jp][:, k * 1024:(k + 1) * 1024], wstg8[sl2][:]),
                            rd=[f"a8wst{sl2}"],
                            wr=([f"awgb{jp}"] if k in (0, 15) else []),
                            tag=f"wocast{ocg}.{k}")
                    for tt2 in range(2):
                        tok0 = tt2 * 512
                        for oc8 in range(8):
                            for k in range(16):
                                p.step("pe", lambda t, k=k, oc8=oc8, tt2=tt2, jp=jp:
                                       t.matmul(
                                           PB[oc8][:, 0:512],
                                           wgb[jp][:, k * 1024 + oc8 * 128:
                                                    k * 1024 + (oc8 + 1) * 128],
                                           Ofull[:, k * 1024 + tt2 * 512:
                                                 k * 1024 + (tt2 + 1) * 512],
                                           start=(k == 0), stop=(k == 15)),
                                       rd=(["awgb" + str(jp), "aOfull"]
                                           if k in (0, 15) else []),
                                       wr=[f"pb{oc8}"] if k in (0, 15) else [],
                                       tag=f"womm{ocg}.{tt2}.{oc8}.{k}")
                        for half in range(2):
                            for q4 in range(4):
                                oc = half * 4 + q4
                                p.step("act", lambda s, oc=oc, q4=q4: s.activation(
                                    ystage[:, q4 * 512:(q4 + 1) * 512],
                                    PB[oc][:, 0:512], AF.Copy),
                                    rd=[f"pb{oc}"], wr=[f"a8ys.{q4}"],
                                    tag=f"woev{ocg}.{tt2}.{oc}")
                            for ttt in range(4):
                                r0 = tok0 + ttt * 128
                                c0 = ocg * 1024 + half * 512
                                bank = ttt if half == 0 else 4 + ttt
                                xb2 = f"a8xr{ttt % 2}"
                                p.step("scq", lambda s, r0=r0, ttt=ttt, c0=c0:
                                       s.dma_start(out=xr[ttt % 2][:],
                                                   in_=src_x[r0:r0 + 128,
                                                             c0:c0 + 512]),
                                       rd=[f"{src_res}.{r0}.{ocg}.0",
                                           f"{src_res}.{r0}.{ocg}.1"],
                                       wr=[xb2], tag=f"aldxr{ocg}.{tt2}.{half}.{ttt}",
                                       ch=f"a8xr{ttt % 2}")
                                for q in range(4):
                                    p.step("pe", lambda t, q=q, ttt=ttt, bank=bank:
                                           t.transpose(
                                               PB[bank][:, q * 128:(q + 1) * 128],
                                               ystage[:, q * 512 + ttt * 128:
                                                       q * 512 + (ttt + 1) * 128],
                                               c_idf[:, :]),
                                           rd=[f"a8ys.{q}"], wr=[f"pb{bank}"],
                                           tag=f"aytr{ocg}.{tt2}.{ttt}.{half}.{q}")
                                p.step("dve", lambda v, ttt=ttt, bank=bank:
                                       v.tensor_add(xr[ttt % 2][:], xr[ttt % 2][:],
                                                    PB[bank][:, 0:512]),
                                       rd=[f"pb{bank}"], wr=[xb2],
                                       tag=f"ayadd{ocg}.{tt2}.{ttt}.{half}")
                                stx = p.step("gpq", lambda g, r0=r0, ttt=ttt, c0=c0:
                                             g.dma_start(
                                                 out=dst_x[r0:r0 + 128, c0:c0 + 512],
                                                 in_=xr[ttt % 2][:]),
                                             rd=[xb2],
                                             wr=[f"{dst_res}.{r0}.{ocg}.{half}"],
                                             tag=f"astx{ocg}.{tt2}.{half}.{ttt}",
                                             ch=f"st_{dst_res}")
                                if dst_x is y_out:
                                    final_deps.append(stx)
                    p.barrier()
            return

        f32w_attn = True
        emit_ffn(1, x_in, x1d if PHASES > 1 else y_out, "xin", "x1")
        if PHASES > 1:
            p.fence("st_x1", "hd", "wb1", "wb2")
            emit_attn(x1d, x2d if PHASES > 2 else y_out, "x1", "x2")
        if PHASES > 2:
            p.fence("st_x2", "lnld", "ofl")
            emit_ffn(2, x2d, y_out, "x2", "y")

        p.final_wait(final_deps)
    return nc


def kernel(**inputs):
    in_maps = [host_inputs(inputs, c) for c in range(8)]
    nc = build_program()
    res = run_bass_kernel_spmd(nc, in_maps, core_ids=list(range(8)))
    out = np.zeros((4, 2048, D), dtype=np.float32)
    for c in range(8):
        b, half = c // 2, c % 2
        out[b, half * T:(half + 1) * T, :] = res.results[c]["y_out"]
    return out


# version 02
# speedup vs baseline: 1.2136x; 1.2136x over previous
import numpy as np
import ml_dtypes
import concourse.bass as bass
import concourse.mybir as mybir
from concourse.bass_utils import run_bass_kernel_spmd
from contextlib import ExitStack

F32 = mybir.dt.float32
BF16 = mybir.dt.bfloat16
AF = mybir.ActivationFunctionType
ALU = mybir.AluOpType
AX = mybir.AxisListType.X
PAIRS = [[0, 1], [2, 3], [4, 5], [6, 7]]

T = 1024
D = 2048
HF = 1408
NE = 8

PHASES = 99
A6STOP = False
CUT = None
DBG = False

ENGINES = ["pe", "act", "dve", "pool", "syq", "scq", "gpq"]
DMA_Q = {"syq", "scq", "gpq"}
ENG_OF = {"pe": "tensor", "act": "scalar", "dve": "vector", "pool": "gpsimd",
          "syq": "sync", "scq": "scalar", "gpq": "gpsimd"}


class Step:
    __slots__ = ["eng", "fn", "deps", "inc", "count", "block", "tag", "idx", "wb",
                 "tag_ch"]

    def __init__(self, eng, fn, deps, inc, block, tag, idx, wb=False):
        self.eng, self.fn, self.deps, self.inc = eng, fn, deps, inc
        self.count = None
        self.block = block
        self.tag = tag
        self.idx = idx
        self.wb = wb


class Prog:
    def __init__(self, nc, ctx):
        self.nc = nc
        self.ctx = ctx
        self.sem = {e: ctx.enter_context(nc.semaphore(f"sem_{e}")) for e in ENGINES}
        self.steps = []
        self.cnt = {e: 0 for e in ENGINES}
        self.blk_idx = 0
        self.n_idx = 0
        self.res_w = {}
        self.res_r = {}
        self.chan_sem = {}
        self.chan_cnt = {}
        self.pending_fence = []

    def fence(self, *chs):
        self.pending_fence = [("ch", ch, self.chan_cnt[ch])
                              for ch in chs if ch in self.chan_cnt]

    def _chan(self, name):
        if name not in self.chan_sem:
            self.chan_sem[name] = self.ctx.enter_context(
                self.nc.semaphore(f"ch_{name}"))
            self.chan_cnt[name] = 0
        return self.chan_sem[name]

    def step(self, eng, fn, rd=(), wr=(), extra=(), tag="", wb=False, ch=None):
        raw = []
        for n in rd:
            w = self.res_w.get(n)
            if w is not None:
                raw.append(w)
        for n in wr:
            w = self.res_w.get(n)
            if w is not None:
                raw.append(w)
            raw.extend(self.res_r.get(n, {}).values())
        raw.extend(d for d in extra if d is not None)
        raw.extend(self.pending_fence)
        deps = []
        seen = set()
        for d in raw:
            if isinstance(d, tuple):
                if d[1] in self.chan_sem and d not in seen:
                    seen.add(d)
                    deps.append(d)
            elif d.eng in DMA_Q:
                key = ("ch", d.tag_ch)
                if key not in seen:
                    seen.add(key)
                    deps.append(("ch", d.tag_ch, self.chan_cnt[d.tag_ch]))
            else:
                if id(d) not in seen:
                    seen.add(id(d))
                    deps.append(d)
        if eng in DMA_Q:
            assert ch is not None, f"DMA step needs ch=: {tag}"
            self._chan(ch)
            self.chan_cnt[ch] += 16
            s = Step(eng, fn, deps, 16, self.blk_idx, tag, self.n_idx, wb)
            s.tag_ch = ch
            s.count = self.chan_cnt[ch]
        else:
            s = Step(eng, fn, deps, 1, self.blk_idx, tag, self.n_idx, wb)
            s.tag_ch = None
        self.n_idx += 1
        self.steps.append(s)
        for n in rd:
            self.res_r.setdefault(n, {})[eng] = s
        for n in wr:
            self.res_w[n] = s
            self.res_r[n] = {}
        return s

    def barrier(self):
        if not self.steps:
            self.blk_idx += 1
            return
        steps, self.steps = self.steps, []
        cnt = dict(self.cnt)
        for s in steps:
            if s.eng in DMA_Q:
                continue
            cnt[s.eng] += s.inc
            s.count = cnt[s.eng]
        by_body = {}
        for s in steps:
            by_body.setdefault(ENG_OF[s.eng], []).append(s)
        nc = self.nc
        with nc.Block() as blk:
            for body_name, seq in by_body.items():
                seq.sort(key=lambda s: s.idx)

                def make_body(seq):
                    def body(engobj):
                        hw = {}
                        for s in seq:
                            for d in s.deps:
                                if isinstance(d, tuple):
                                    _, chn, val = d
                                    if val <= 0 or hw.get(("c", chn), 0) >= val:
                                        continue
                                    engobj.wait_ge(self.chan_sem[chn], val)
                                    hw[("c", chn)] = val
                                    continue
                                if d.count is None:
                                    raise RuntimeError(
                                        f"dep not emitted: {d.tag} <- {s.tag}")
                                if (not d.wb) and (
                                        d.block < s.block or
                                        (d.eng == s.eng and d.eng == "pe")):
                                    continue
                                if hw.get(d.eng, 0) >= d.count:
                                    continue
                                engobj.wait_ge(self.sem[d.eng], d.count)
                                hw[d.eng] = d.count
                            instr = s.fn(engobj)
                            if instr is None:
                                raise RuntimeError(f"step returned None: {s.tag}")
                            if s.eng in DMA_Q:
                                instr.then_inc(self.chan_sem[s.tag_ch], 16)
                            else:
                                instr.then_inc(self.sem[s.eng], s.inc)
                    return body
                getattr(blk, body_name)(make_body(seq))
        self.cnt = cnt
        self.blk_idx += 1
        self.pending_fence = []

    def final_wait(self, deps):
        assert not self.steps
        nc = self.nc
        chans = {}
        for d in deps:
            assert d.eng in DMA_Q
            chans[d.tag_ch] = self.chan_cnt[d.tag_ch]
        with nc.Block() as blk:
            @blk.sync
            def _(sy):
                for chn, val in chans.items():
                    sy.wait_ge(self.chan_sem[chn], val)
        self.blk_idx += 1


def _poff_vec(half):
    offs = np.zeros(80, dtype=np.int64)
    for t3 in range(3):
        for r in range(2):
            offs[t3 * 2 + r] = r * 6144 * 1024 + (t3 * 2048 + half * 1024) * 1024
    for r in range(2):
        offs[6 + r] = r * 1024 * 2048 + half * 1024
    return offs.astype(np.int32).reshape(1, 80)


def host_inputs(inputs, core):
    b, half = core // 2, core % 2
    x = np.ascontiguousarray(
        np.asarray(inputs["L0_attn_x"], dtype=np.float32)[b, half * T:(half + 1) * T, :])
    eye_b = np.eye(128, dtype=ml_dtypes.bfloat16)
    eye_f = np.eye(128, dtype=np.float32)
    jj, ii = np.meshgrid(np.arange(128), np.arange(128), indexing="ij")
    maskS = np.where(ii > jj, 0.0, -1e30).astype(np.float32)
    maskI = np.where(ii >= jj, 0.0, -1e30).astype(np.float32)
    selb = np.zeros((8, 8 * 128), dtype=ml_dtypes.bfloat16)
    for e in range(8):
        selb[e, e * 128:(e + 1) * 128] = 1.0
    f32 = lambda a: np.asarray(a, dtype=np.float32)
    cont = np.ascontiguousarray
    return dict(
        x_in=x,
        wqkvT=cont(f32(inputs["L0_attn_W_qkv"]).T),
        conv_w=cont(f32(inputs["L0_attn_conv_w"]).reshape(48, 128, 4)
                    .transpose(1, 0, 2).reshape(128, 192)),
        wabT=cont(f32(inputs["L0_attn_W_ab"]).T.reshape(16, 128, 32)
                  .transpose(1, 0, 2).reshape(128, 512)),
        b_ab=cont(np.broadcast_to(f32(inputs["L0_attn_b_ab"]).reshape(1, 32),
                                  (128, 32))),
        wgT=cont(f32(inputs["L0_attn_W_g"]).T),
        woT=cont(f32(inputs["L0_attn_W_o"]).T),
        gate_wT=cont(f32(inputs["L0_ffn_gate_w"]).T.reshape(16, 128, 8)
                     .transpose(1, 0, 2).reshape(128, 128)),
        gate_b=cont(np.broadcast_to(f32(inputs["L0_ffn_gate_b"]).reshape(1, 8),
                                    (128, 8))),
        gate_bias=cont(np.broadcast_to(f32(inputs["L0_ffn_bias"]).reshape(1, 8),
                                       (128, 8))),
        w1T=cont(f32(inputs["L0_ffn_w1"]).transpose(0, 2, 1)),
        b1T=cont(f32(inputs["L0_ffn_b1"]).T.reshape(11, 128, 8)
                 .transpose(1, 0, 2).reshape(128, 88)),
        w2R=cont(f32(inputs["L0_ffn_w2"]).transpose(0, 2, 1)
                 .reshape(NE, 11, 128, 16, 128).transpose(0, 3, 2, 1, 4)
                 .reshape(NE, 16, 128, 11 * 128)),
        b2=cont(f32(inputs["L0_ffn_b2"])),
        idb=eye_b, idf=eye_f, maskS=maskS, maskI=maskI,
        onesb=np.ones((8, 128), dtype=ml_dtypes.bfloat16),
        onescolb=np.ones((128, 1), dtype=ml_dtypes.bfloat16),
        poff=_poff_vec(half),
        onesf=np.ones((1, 128), dtype=np.float32),
        ones8=np.ones((8, 128), dtype=np.float32),
        selb=selb,
        m3=np.full((128, 1), float(half), dtype=np.float32),
        epscol=np.full((128, 1), 1e-5, dtype=np.float32),
        m3w=np.full((128, 128), float(half), dtype=np.float32),
    )


def build_program():
    nc = bass.Bass("TRN2", target_bir_lowering=False)

    def din(name, shape, dt=F32):
        return nc.dram_tensor(name, list(shape), dt, kind="ExternalInput").ap()

    x_in = din("x_in", [T, D])
    wqkvT = din("wqkvT", [D, 3 * D])
    conv_w = din("conv_w", [128, 192])
    wabT = din("wabT", [128, 512])
    b_ab = din("b_ab", [128, 32])
    wgT = din("wgT", [D, D])
    woT = din("woT", [D, D])
    gate_wT = din("gate_wT", [128, 128])
    gate_b = din("gate_b", [128, NE])
    gate_bias = din("gate_bias", [128, NE])
    w1T = din("w1T", [NE, D, HF])
    b1T = din("b1T", [128, 88])
    w2R = din("w2R", [NE, 16, 128, 11 * 128])
    b2 = din("b2", [NE, D])
    idb_i = din("idb", [128, 128], BF16)
    idf_i = din("idf", [128, 128])
    maskS_i = din("maskS", [128, 128])
    maskI_i = din("maskI", [128, 128])
    onesb_i = din("onesb", [8, 128], BF16)
    onescolb_i = din("onescolb", [128, 1], BF16)
    poff_i = din("poff", [1, 80], mybir.dt.int32)
    onesf_i = din("onesf", [1, 128])
    ones8_i = din("ones8", [8, 128])
    selb_i = din("selb", [8, 8 * 128], BF16)
    m3_i = din("m3", [128, 1])
    eps_i = din("epscol", [128, 1])
    m3w_i = din("m3w", [128, 128])

    y_out = nc.dram_tensor("y_out", [T, D], F32, kind="ExternalOutput").ap()

    def dint(name, shape, dt=F32):
        return nc.dram_tensor(name, list(shape), dt, kind="Internal").ap()

    x1d = dint("x1d", [T, D])
    x2d = dint("x2d", [T, D])
    w1b = dint("w1b", [NE, D, HF], BF16)
    w2b = dint("w2b", [NE, HF, D], BF16)
    hdram = dint("hdram", [NE, 128, 11 * 512], BF16)
    zpre = dint("zpre", [3 * D, T], BF16)
    lnfm2d = dint("lnfm2d", [128, 16 * 1024], BF16)
    qkv_out_t = nc.dram_tensor("qkv_out", [3 * D, T], BF16, kind="Internal")
    qkv_gat_t = nc.dram_tensor("qkv_gat", [2, 3 * D, T], BF16, kind="Internal")
    qkv_out = qkv_out_t.ap()
    qkv_gat = qkv_gat_t.ap()
    ab_out = dint("ab_out", [T, 32])
    ab_gat = dint("ab_gat", [2, T, 32])
    ztail_in = dint("ztail_in", [128, 144], BF16)
    ztail_gat = dint("ztail_gat", [2, 128, 144], BF16)
    o_out = dint("o_out", [T, D], BF16)
    o_gat_t = nc.dram_tensor("o_gat", [2, T, D], BF16, kind="Internal")
    o_gat = o_gat_t.ap()

    final_deps = []

    ctx = ExitStack()
    with ctx:
        p = Prog(nc, ctx)
        sb = lambda name, shape, dt=F32: ctx.enter_context(
            nc.sbuf_tensor(name, list(shape), dt))
        c_idb = sb("c_idb", [128, 128], BF16)
        c_idf = sb("c_idf", [128, 128])
        c_maskS = sb("c_maskS", [128, 128])
        c_maskI = sb("c_maskI", [128, 128])
        c_onesb = sb("c_onesb", [8, 128], BF16)
        c_onescolb = sb("c_onescolb", [128, 1], BF16)
        c_poff = sb("c_poff", [1, 80], mybir.dt.int32)
        c_onesf = sb("c_onesf", [1, 128])
        c_ones8 = sb("c_ones8", [8, 128])
        c_selb = sb("c_selb", [8, 8 * 128], BF16)
        c_m3 = sb("c_m3", [128, 1])
        c_eps = sb("c_eps", [128, 1])
        c_m3w = sb("c_m3w", [128, 128])
        c_cw = sb("c_cw", [128, 48 * 4])
        c_b1T = sb("c_b1T", [128, 11 * 8])
        c_b2b = sb("c_b2b", [8, D], BF16)
        c_gwT = sb("c_gwT", [128, 16 * 8], BF16)
        c_wabT = sb("c_wabT", [128, 16 * 32], BF16)
        c_gbias = sb("c_gbias", [128, 8])
        c_abbias = sb("c_abbias", [128, 32])
        c_scr = sb("c_scr", [128, 1024])

        PB = [ctx.enter_context(nc.psum_tensor(f"pb{i}", [128, 512], F32))
              for i in range(8)]

        for dst, src in [
            (c_idb[:], idb_i), (c_idf[:], idf_i), (c_maskS[:], maskS_i),
            (c_maskI[:], maskI_i), (c_onesb[:], onesb_i), (c_onesf[:], onesf_i),
            (c_ones8[:], ones8_i), (c_selb[:], selb_i), (c_m3[:], m3_i),
            (c_eps[:], eps_i), (c_onescolb[:], onescolb_i), (c_poff[:], poff_i),
            (c_m3w[:], m3w_i),
            (c_cw[:], conv_w),
            (c_b1T[:], b1T),
        ]:
            p.step("syq", lambda sy, d=dst, s=src: sy.dma_start(out=d, in_=s), tag="cload", ch="c")
        if CUT == "c1":
            p.barrier()
            p.final_wait(final_deps)
            return nc
        for h2 in range(2):
            p.step("syq", lambda sy, h2=h2: sy.dma_start(
                out=c_scr[0:8, 0:1024], in_=b2[:, h2 * 1024:(h2 + 1) * 1024]),
                wr=["scr"], tag="b2", ch="c")
            p.step("dve", lambda v, h2=h2: v.tensor_copy(
                c_b2b[:, h2 * 1024:(h2 + 1) * 1024], c_scr[0:8, 0:1024]), rd=["scr"])
        p.barrier()
        if CUT == "c2":
            p.final_wait(final_deps)
            return nc
        p.step("syq", lambda sy: sy.dma_start(
            out=c_scr[:, 0:128], in_=gate_wT),
            wr=["scr1"], tag="gw", ch="c")
        p.step("syq", lambda sy: sy.dma_start(
            out=c_scr[:, 128:640], in_=wabT),
            wr=["scr2"], tag="ab", ch="c")
        p.step("syq", lambda sy: sy.dma_start(out=c_gbias[:], in_=gate_b),
               wr=["scr3"], ch="c")
        p.step("syq", lambda sy: sy.dma_start(out=c_scr[:, 640:648], in_=gate_bias),
               wr=["scr4"], ch="c")
        p.step("syq", lambda sy: sy.dma_start(out=c_abbias[:], in_=b_ab),
               wr=["scr5"], ch="c")
        p.step("dve", lambda v: v.tensor_copy(c_gwT[:], c_scr[:, 0:128]), rd=["scr1"])
        p.step("dve", lambda v: v.tensor_copy(c_wabT[:], c_scr[:, 128:640]), rd=["scr2"])
        p.step("dve", lambda v: v.tensor_add(
            c_gbias[:], c_gbias[:], c_scr[:, 640:648]),
            rd=["scr4"], wr=["scr3"], tag="gbsum")
        p.barrier()

        if CUT == "consts":
            p.final_wait(final_deps)
            return nc

        def emit_ffn(call, src_x, dst_x, src_res, dst_res):
            f32w = (call == 1)
            with ExitStack() as fs:
                fsb = lambda name, shape, dt=F32: fs.enter_context(
                    nc.sbuf_tensor(f"f{call}{name}", list(shape), dt))
                lnfm = fsb("lnfm", [128, 16 * 512], BF16)
                lnv = lnfm[:].rearrange("p (k n) -> p k n", k=16)
                xt = [fsb(f"xt{i}", [128, D]) for i in range(2)]
                st24 = fsb("st24", [128, 24])
                mv = fsb("mv", [128, 2])
                rstd = fsb("rstd", [128, 2])
                xnb = fsb("xnb", [128, D], BF16)
                s_sc = fsb("s_sc", [128, 8])
                s_e1 = fsb("s_e1", [128, 8])
                s_mk = fsb("s_mk", [128, 8])
                s_e2 = fsb("s_e2", [128, 8])
                s_m1 = fsb("s_m1", [128, 1])
                s_m2 = fsb("s_m2", [128, 1])
                s_g1 = fsb("s_g1", [128, 1])
                s_g2 = fsb("s_g2", [128, 1])
                s_t1 = fsb("s_t1", [128, 8])
                combb = fsb("combb", [128, 32], BF16)
                comb_fm = fsb("comb_fm", [8, 512], BF16)
                comb_bc = [fsb(f"cbc{i}", [128, 512], BF16) for i in range(2)]
                w1sb = [fsb(f"w1sb{i}", [128, 16 * HF], BF16) for i in range(2)]
                wstage = fsb("wstage", [128, 2816]) if f32w else None
                hstage = fsb("hstage", [128, 11 * 512], BF16)
                hbuf = [fsb(f"hbuf{i}", [128, 11 * 512], BF16) for i in range(2)]
                w2sb = [fsb(f"w2sb{i}", [128, 1024], BF16) for i in range(2)]
                ystage = fsb("ystage", [128, 4 * 512])
                xr = [fsb(f"xr{i}", [128, 512]) for i in range(2)]

                for tt in range(2):
                    tok0 = tt * 512
                    for ttt in range(4):
                        r0 = tok0 + ttt * 128
                        xb = f"xt{ttt % 2}"
                        p.step("scq", lambda s, r0=r0, ttt=ttt: s.dma_start(
                            out=xt[ttt % 2][:], in_=src_x[r0:r0 + 128, :]),
                            rd=[f"{src_res}.{r0}.0.0", f"{src_res}.{r0}.0.1",
                                f"{src_res}.{r0}.1.0", f"{src_res}.{r0}.1.1"],
                            wr=[xb], tag=f"ldx{tt}.{ttt}", ch=f"xt{ttt % 2}")
                        for q in range(4):
                            p.step("dve", lambda v, q=q, ttt=ttt: v.bn_stats(
                                st24[:, q * 6:(q + 1) * 6],
                                xt[ttt % 2][:, q * 512:(q + 1) * 512]),
                                rd=[xb], wr=[f"st24.{q}"], tag=f"bn{ttt}.{q}", wb=True)
                        p.step("dve", lambda v: v.bn_aggr(mv[:], st24[:]),
                               rd=["st24.0", "st24.1", "st24.2", "st24.3"],
                               wr=["mv"], tag="aggr", wb=True)
                        p.step("act", lambda s: s.activation(
                            rstd[:, 0:1], mv[:, 1:2], AF.Ln, bias=c_eps[:, :]),
                            rd=["mv"], wr=["rstd0"])
                        p.step("act", lambda s: s.activation(
                            rstd[:, 1:2], rstd[:, 0:1], AF.Exp, scale=-0.5),
                            rd=["rstd0"], wr=["rstd"])
                        p.step("dve", lambda v, ttt=ttt: v.tensor_scalar(
                            xnb[:], xt[ttt % 2][:], mv[:, 0:1], rstd[:, 1:2],
                            ALU.subtract, ALU.mult),
                            rd=[xb, "mv", "rstd"], wr=["xnb"], tag=f"nrm{ttt}")
                        for bg in range(4):
                            for q in range(4):
                                k = bg * 4 + q
                                p.step("pe", lambda t, k=k, bg=bg, q=q: t.matmul(
                                    PB[bg][:, q * 128:(q + 1) * 128],
                                    xnb[:, k * 128:(k + 1) * 128], c_idb[:, :],
                                    start=True, stop=True),
                                    rd=["xnb"], wr=[f"pb{bg}"], tag=f"trmm{ttt}.{bg}")
                            p.step("act", lambda s, bg=bg, ttt=ttt: s.activation(
                                lnv[:, bg * 4:(bg + 1) * 4, ttt * 128:(ttt + 1) * 128],
                                PB[bg][:, 0:512].rearrange("p (k n) -> p k n", k=4),
                                AF.Copy),
                                rd=[f"pb{bg}"], wr=[f"lnfm.{bg}.{ttt}"],
                                tag=f"trev{ttt}.{bg}")
                    if CUT == "ln1" and tt == 0:
                        p.barrier()
                        return
                    if CUT == "ln":
                        p.barrier()
                        return
                    for ttt in range(4):
                        for k in range(16):
                            p.step("pe", lambda t, k=k, ttt=ttt: t.matmul(
                                PB[4][:, 0:8],
                                lnv[:, k, ttt * 128:(ttt + 1) * 128],
                                c_gwT[:, k * 8:(k + 1) * 8],
                                start=(k == 0), stop=(k == 15)),
                                rd=[f"lnfm.{k // 4}.{ttt}"],
                                wr=["pb4"] if k in (0, 15) else [],
                                tag=f"scmm{ttt}.{k}")
                        p.step("dve", lambda v: v.tensor_add(
                            s_sc[:], PB[4][:, 0:8], c_gbias[:]),
                            rd=["pb4"], wr=["s_sc"])
                        p.step("dve", lambda v: v.reduce_max(s_m1[:], s_sc[:], axis=AX),
                               rd=["s_sc"], wr=["s_m1"], wb=True)
                        p.step("dve", lambda v: v.tensor_scalar(
                            s_e1[:], s_sc[:], s_m1[:, :], None, ALU.is_equal),
                            rd=["s_sc", "s_m1"], wr=["s_e1"])
                        p.step("dve", lambda v: v.scalar_tensor_tensor(
                            s_mk[:], s_e1[:], -1e30, s_sc[:], ALU.mult, ALU.add),
                            rd=["s_e1", "s_sc"], wr=["s_mk"])
                        p.step("dve", lambda v: v.reduce_max(s_m2[:], s_mk[:], axis=AX),
                               rd=["s_mk"], wr=["s_m2"], wb=True)
                        p.step("dve", lambda v: v.tensor_scalar(
                            s_e2[:], s_mk[:], s_m2[:, :], None, ALU.is_equal),
                            rd=["s_mk", "s_m2"], wr=["s_e2"])
                        p.step("dve", lambda v: v.tensor_sub(s_m1[:], s_m1[:], s_m2[:]),
                               rd=["s_m2"], wr=["s_m1"])
                        p.step("act", lambda s: s.activation(
                            s_g1[:], s_m1[:, :], AF.Sigmoid), rd=["s_m1"], wr=["s_g1"])
                        p.step("act", lambda s: s.activation(
                            s_g2[:], s_m1[:, :], AF.Sigmoid, scale=-1.0),
                            rd=["s_m1"], wr=["s_g2"])
                        p.step("dve", lambda v: v.tensor_scalar(
                            s_t1[:], s_e1[:], s_g1[:, :], None, ALU.mult),
                            rd=["s_e1", "s_g1"], wr=["s_t1"])
                        p.step("dve", lambda v, ttt=ttt: v.scalar_tensor_tensor(
                            combb[:, ttt * 8:(ttt + 1) * 8], s_e2[:], s_g2[:, :], s_t1[:],
                            ALU.mult, ALU.add),
                            rd=["s_e2", "s_g2", "s_t1"], wr=[f"combb.{ttt}"], tag="comb")
                        p.step("pe", lambda t, ttt=ttt: t.matmul(
                            PB[5][0:8, ttt * 128:(ttt + 1) * 128],
                            combb[:, ttt * 8:(ttt + 1) * 8], c_idb[:, :],
                            start=True, stop=True),
                            rd=[f"combb.{ttt}"], wr=["pb5"], tag="cmbT")
                        p.step("dve", lambda v, ttt=ttt: v.tensor_copy(
                            comb_fm[:, ttt * 128:(ttt + 1) * 128],
                            PB[5][0:8, ttt * 128:(ttt + 1) * 128]),
                            rd=["pb5"], wr=[f"comb_fm.{ttt}"], tag="cfm")
                    cfm_all = ["comb_fm.0", "comb_fm.1", "comb_fm.2", "comb_fm.3"]
                    if CUT == "scores":
                        p.barrier()
                        return
                    p.barrier()

                    for e in range(NE):
                        wbuf = f"w1sb{e % 2}"
                        for k in range(16):
                            if f32w and tt == 0:
                                sl = k % 2
                                p.step("syq", lambda sy, e=e, k=k, sl=sl: sy.dma_start(
                                    out=wstage[:, sl * 1408:(sl + 1) * 1408],
                                    in_=w1T[e, k * 128:(k + 1) * 128, :]),
                                    wr=[f"wst{sl}"], tag=f"w1ld{e}.{k}", ch=f"wst{sl}")
                                p.step("dve", lambda v, e=e, k=k, sl=sl: v.tensor_copy(
                                    w1sb[e % 2][:, k * HF:(k + 1) * HF],
                                    wstage[:, sl * 1408:(sl + 1) * 1408]),
                                    rd=[f"wst{sl}"],
                                    wr=([wbuf] if k in (0, 15) else []) + [f"{wbuf}.k{k}"],
                                    tag=f"w1cast{e}.{k}")
                                p.step("gpq", lambda g, e=e, k=k: g.dma_start(
                                    out=w1b[e, k * 128:(k + 1) * 128, :],
                                    in_=w1sb[e % 2][:, k * HF:(k + 1) * HF]),
                                    rd=[f"{wbuf}.k{k}"], wr=[f"w1b{e}"] if k == 15 else [],
                                    tag=f"w1wb{e}.{k}", ch="wb1")
                            else:
                                p.step("syq", lambda sy, e=e, k=k: sy.dma_start(
                                    out=w1sb[e % 2][:, k * HF:(k + 1) * HF],
                                    in_=w1b[e, k * 128:(k + 1) * 128, :]),
                                    rd=[f"w1b{e}"] if k == 0 else [],
                                    wr=[wbuf] if k in (0, 15) else [],
                                    tag=f"w1ld{e}.{k}", ch=f"w1{e % 2}")
                        p.step("pe", lambda t, e=e: t.matmul(
                            PB[4][:, 0:512], c_selb[0:8, e * 128:(e + 1) * 128],
                            comb_fm[0:8, :], start=True, stop=True),
                            rd=cfm_all, wr=["pb4"], tag=f"cbcmm{e}")
                        p.step("dve", lambda v, e=e: v.tensor_copy(
                            comb_bc[e % 2][:], PB[4][:, 0:512]),
                            rd=["pb4"], wr=[f"cbc{e % 2}"], tag=f"cbcev{e}")
                        for hc in range(11):
                            bank = 6 + (e * 11 + hc) % 2
                            for k in range(16):
                                p.step("pe", lambda t, k=k, hc=hc, e=e, bank=bank:
                                       t.matmul(PB[bank][:, 0:512],
                                                w1sb[e % 2][:, k * HF + hc * 128:
                                                            k * HF + (hc + 1) * 128],
                                                lnv[:, k, :],
                                                start=(k == 0), stop=(k == 15)),
                                       rd=[wbuf] if k in (0, 15) else [],
                                       wr=[f"pb{bank}"] if k in (0, 15) else [],
                                       tag=f"hmm{e}.{hc}.{k}")
                            p.step("act", lambda s, e=e, hc=hc, bank=bank: s.activation(
                                hstage[:, hc * 512:(hc + 1) * 512], PB[bank][:, 0:512],
                                AF.Silu, bias=c_b1T[:, hc * 8 + e: hc * 8 + e + 1]),
                                rd=[f"pb{bank}"], wr=[f"hstage.{hc}"], tag=f"sil{e}.{hc}")
                            p.step("dve", lambda v, hc=hc, e=e: v.tensor_mul(
                                hstage[:, hc * 512:(hc + 1) * 512],
                                hstage[:, hc * 512:(hc + 1) * 512], comb_bc[e % 2][:]),
                                rd=[f"cbc{e % 2}"], wr=[f"hstage.{hc}"],
                                tag=f"hm{e}.{hc}")
                        p.step("gpq", lambda g, e=e: g.dma_start(
                            out=hdram[e], in_=hstage[:]),
                            rd=[f"hstage.{hc2}" for hc2 in range(11)],
                            wr=[f"hdram{e}"], tag=f"hst{e}", ch="hd")
                    if CUT == "h":
                        p.barrier()
                        return
                    p.barrier()

                    for ocg in range(2):
                        ymm_last = None
                        for oc in range(8):
                            goc = ocg * 8 + oc
                            p.step("pe", lambda t, oc=oc, goc=goc: t.matmul(
                                PB[oc][:, 0:512], c_b2b[0:8, goc * 128:(goc + 1) * 128],
                                comb_fm[0:8, :], start=True, stop=False,
                                skip_group_check=True),
                                rd=cfm_all, wr=[f"pb{oc}"], tag=f"yb2.{ocg}.{oc}")
                        for e in range(NE):
                            p.step("gpq", lambda g, e=e: g.dma_start(
                                out=hbuf[e % 2][:], in_=hdram[e]),
                                rd=[f"hdram{e}"], wr=[f"hbuf{e % 2}"],
                                tag=f"hld{ocg}.{e}", ch=f"hb{e % 2}")
                            for hc in range(11):
                                wi = (e * 11 + hc) % 2
                                if f32w and tt == 0:
                                    p.step("syq", lambda sy, e=e, hc=hc, ocg=ocg, wi=wi:
                                           sy.dma_start(
                                               out=wstage[:, wi * 1408:wi * 1408 + 1024],
                                               in_=w2T[e, hc * 128:(hc + 1) * 128,
                                                       ocg * 1024:(ocg + 1) * 1024]),
                                           wr=[f"wst{wi}"], tag=f"w2ld{ocg}.{e}.{hc}",
                                           ch=f"wst{wi}")
                                    p.step("dve", lambda v, wi=wi: v.tensor_copy(
                                        w2sb[wi][:], wstage[:, wi * 1408:wi * 1408 + 1024]),
                                        rd=[f"wst{wi}"], wr=[f"w2sb{wi}"],
                                        tag=f"w2cast{ocg}.{e}.{hc}")
                                    p.step("gpq", lambda g, e=e, hc=hc, ocg=ocg, wi=wi:
                                           g.dma_start(
                                               out=w2b[e, hc * 128:(hc + 1) * 128,
                                                       ocg * 1024:(ocg + 1) * 1024],
                                               in_=w2sb[wi][:]),
                                           rd=[f"w2sb{wi}"], wr=[f"w2b{e}.{ocg}"],
                                           tag=f"w2wb{ocg}.{e}.{hc}", ch="wb2")
                                else:
                                    p.step("syq", lambda sy, e=e, hc=hc, ocg=ocg, wi=wi:
                                           sy.dma_start(
                                               out=w2sb[wi][:],
                                               in_=w2b[e, hc * 128:(hc + 1) * 128,
                                                       ocg * 1024:(ocg + 1) * 1024]),
                                           rd=[f"w2b{e}.{ocg}"] if hc == 0 else [],
                                           wr=[f"w2sb{wi}"],
                                           tag=f"w2ld{ocg}.{e}.{hc}", ch=f"w2{wi}")
                                for oc in range(8):
                                    s = p.step("pe", lambda t, oc=oc, hc=hc, wi=wi, e=e:
                                               t.matmul(PB[oc][:, 0:512],
                                                        w2sb[wi][:, oc * 128:(oc + 1) * 128],
                                                        hbuf[e % 2][:, hc * 512:(hc + 1) * 512],
                                                        start=False,
                                                        stop=(e == NE - 1 and hc == 10),
                                                        skip_group_check=True),
                                               rd=([f"w2sb{wi}", f"hbuf{e % 2}"]
                                                   if oc == 0 else []) +
                                               ([f"w2sb{wi}", f"hbuf{e % 2}"]
                                                if oc == 7 else []),
                                               wr=[f"pb{oc}"] if (e == NE - 1 and hc == 10)
                                               else [],
                                               tag=f"ymm{ocg}.{e}.{hc}.{oc}")
                                    ymm_last = s
                        for half in range(2):
                            for q4 in range(4):
                                oc = half * 4 + q4
                                p.step("act", lambda s, oc=oc, q4=q4: s.activation(
                                    ystage[:, q4 * 512:(q4 + 1) * 512], PB[oc][:, 0:512],
                                    AF.Copy, scale=0.5),
                                    rd=[f"pb{oc}"], extra=[ymm_last],
                                    wr=[f"ystage.{q4}"], tag=f"yev{ocg}.{oc}")
                            for ttt in range(4):
                                r0 = tok0 + ttt * 128
                                c0 = ocg * 1024 + half * 512
                                bank = ttt if half == 0 else 4 + ttt
                                xb2 = f"xr{ttt % 2}"
                                p.step("scq", lambda s, r0=r0, ttt=ttt, c0=c0: s.dma_start(
                                    out=xr[ttt % 2][:], in_=src_x[r0:r0 + 128, c0:c0 + 512]),
                                    rd=[f"{src_res}.{r0}.{ocg}.0",
                                        f"{src_res}.{r0}.{ocg}.1"],
                                    wr=[xb2], tag=f"ldxr{ocg}.{half}.{ttt}",
                                    ch=f"xr{ttt % 2}")
                                for q in range(4):
                                    p.step("pe", lambda t, q=q, ttt=ttt, bank=bank:
                                           t.transpose(PB[bank][:, q * 128:(q + 1) * 128],
                                                       ystage[:, q * 512 + ttt * 128:
                                                               q * 512 + (ttt + 1) * 128],
                                                       c_idf[:, :]),
                                           rd=[f"ystage.{q}"], wr=[f"pb{bank}"],
                                           tag=f"ytr{ocg}.{ttt}.{half}.{q}")
                                p.step("dve", lambda v, ttt=ttt, bank=bank:
                                       v.tensor_add(xr[ttt % 2][:], xr[ttt % 2][:],
                                                    PB[bank][:, 0:512]),
                                       rd=[f"pb{bank}"], wr=[xb2],
                                       tag=f"yadd{ocg}.{ttt}.{half}")
                                stx = p.step("gpq", lambda g, r0=r0, ttt=ttt, c0=c0:
                                             g.dma_start(
                                                 out=dst_x[r0:r0 + 128, c0:c0 + 512],
                                                 in_=xr[ttt % 2][:]),
                                             rd=[xb2],
                                             wr=[f"{dst_res}.{r0}.{ocg}.{half}"],
                                             tag=f"stx{ocg}.{half}.{ttt}",
                                             ch=f"st_{dst_res}")
                                if dst_x is y_out:
                                    final_deps.append(stx)
                    if CUT == "y":
                        p.barrier()
                        return
                    p.barrier()
            return


        def emit_attn(src_x, dst_x, src_res, dst_res):
            NH = 8
            with ExitStack() as s1:
                asb = lambda name, shape, dt=F32: s1.enter_context(
                    nc.sbuf_tensor(f"a1{name}", list(shape), dt))
                lnfm2 = asb("lnfm2", [128, 16 * 1024], BF16)
                ln2v = lnfm2[:].rearrange("p (k n) -> p k n", k=16)
                xt = [asb(f"xt{i}", [128, D]) for i in range(2)]
                st24 = asb("st24", [128, 24])
                mv = asb("mv", [128, 2])
                rstd = asb("rstd", [128, 2])
                xnb = asb("xnb", [128, D], BF16)
                wqb = [asb(f"wqb{i}", [128, 16 * 1024], BF16) for i in range(2)]
                wstg = [asb(f"wstg{i}", [128, 1024]) for i in range(2)]
                zstage = [asb(f"zst{i}", [128, 512], BF16) for i in range(2)]
                ztail_sb = asb("ztail", [128, 144], BF16)
                abt = asb("abt", [128, 8 * 32])

                for ttt in range(8):
                    r0 = ttt * 128
                    xb = f"axt{ttt % 2}"
                    p.step("scq", lambda s, r0=r0, ttt=ttt: s.dma_start(
                        out=xt[ttt % 2][:], in_=src_x[r0:r0 + 128, :]),
                        rd=[f"{src_res}.{r0}.0.0", f"{src_res}.{r0}.0.1",
                            f"{src_res}.{r0}.1.0", f"{src_res}.{r0}.1.1"],
                        wr=[xb], tag=f"aldx{ttt}", ch=f"axt{ttt % 2}")
                    for q in range(4):
                        p.step("dve", lambda v, q=q, ttt=ttt: v.bn_stats(
                            st24[:, q * 6:(q + 1) * 6],
                            xt[ttt % 2][:, q * 512:(q + 1) * 512]),
                            rd=[xb], wr=[f"ast24.{q}"], tag=f"abn{ttt}.{q}", wb=True)
                    p.step("dve", lambda v: v.bn_aggr(mv[:], st24[:]),
                           rd=["ast24.0", "ast24.1", "ast24.2", "ast24.3"],
                           wr=["amv"], tag="aaggr", wb=True)
                    p.step("act", lambda s: s.activation(
                        rstd[:, 0:1], mv[:, 1:2], AF.Ln, bias=c_eps[:, :]),
                        rd=["amv"], wr=["arstd0"])
                    p.step("act", lambda s: s.activation(
                        rstd[:, 1:2], rstd[:, 0:1], AF.Exp, scale=-0.5),
                        rd=["arstd0"], wr=["arstd"])
                    p.step("dve", lambda v, ttt=ttt: v.tensor_scalar(
                        xnb[:], xt[ttt % 2][:], mv[:, 0:1], rstd[:, 1:2],
                        ALU.subtract, ALU.mult),
                        rd=[xb, "amv", "arstd"], wr=["axnb"], tag=f"anrm{ttt}")
                    for bg in range(4):
                        for q in range(4):
                            k = bg * 4 + q
                            p.step("pe", lambda t, k=k, bg=bg, q=q: t.matmul(
                                PB[bg][:, q * 128:(q + 1) * 128],
                                xnb[:, k * 128:(k + 1) * 128], c_idb[:, :],
                                start=True, stop=True),
                                rd=["axnb"], wr=[f"pb{bg}"], tag=f"atrmm{ttt}.{bg}")
                        p.step("act", lambda s, bg=bg, ttt=ttt: s.activation(
                            ln2v[:, bg * 4:(bg + 1) * 4, ttt * 128:(ttt + 1) * 128],
                            PB[bg][:, 0:512].rearrange("p (k n) -> p k n", k=4),
                            AF.Copy),
                            rd=[f"pb{bg}"], wr=[f"aln.{bg}.{ttt}"],
                            tag=f"atrev{ttt}.{bg}")
                    for k in range(16):
                        p.step("pe", lambda t, k=k, ttt=ttt: t.matmul(
                            PB[4][:, 0:32],
                            ln2v[:, k, ttt * 128:(ttt + 1) * 128],
                            c_wabT[:, k * 32:(k + 1) * 32],
                            start=(k == 0), stop=(k == 15)),
                            rd=[f"aln.{k // 4}.{ttt}"],
                            wr=["pb4"] if k in (0, 15) else [],
                            tag=f"abmm{ttt}.{k}")
                    p.step("dve", lambda v, ttt=ttt: v.tensor_add(
                        abt[:, ttt * 32:(ttt + 1) * 32], PB[4][:, 0:32], c_abbias[:]),
                        rd=["pb4"], wr=[f"aabt.{ttt}"], tag=f"aabadd{ttt}")
                    p.step("gpq", lambda g, ttt=ttt, r0=r0: g.dma_start(
                        out=ab_out[r0:r0 + 128, :], in_=abt[:, ttt * 32:(ttt + 1) * 32]),
                        rd=[f"aabt.{ttt}"], wr=["ab_out_d"], tag=f"abst{ttt}",
                        ch="abst")
                for k in range(16):
                    p.step("gpq", lambda g, k=k: g.dma_start(
                        out=lnfm2d[:, k * 1024:(k + 1) * 1024],
                        in_=lnfm2[:, k * 1024:(k + 1) * 1024]),
                        rd=[f"aln.{k // 4}.{ttt2}" for ttt2 in range(8)],
                        wr=["lnfm2d_d"], tag=f"lnst{k}", ch="lnst")
                p.barrier()

                for j in range(6):
                    jp = j % 2
                    for k in range(16):
                        if f32w_attn:
                            sl = k % 2
                            p.step("syq", lambda sy, j=j, k=k, sl=sl: sy.dma_start(
                                out=wstg[sl][:],
                                in_=wqkvT[k * 128:(k + 1) * 128,
                                          j * 1024:(j + 1) * 1024]),
                                wr=[f"awst{sl}"], tag=f"wqld{j}.{k}", ch=f"awst{sl}")
                            p.step("dve", lambda v, j=j, k=k, sl=sl, jp=jp: v.tensor_copy(
                                wqb[jp][:, k * 1024:(k + 1) * 1024], wstg[sl][:]),
                                rd=[f"awst{sl}"],
                                wr=([f"awqb{jp}"] if k in (0, 15) else []) +
                                [f"awqb{jp}.k{k}"],
                                tag=f"wqcast{j}.{k}")
                    for oc8 in range(8):
                        bank = 5 + (j * 8 + oc8) % 2
                        for tt2 in range(2):
                            for k in range(16):
                                p.step("pe", lambda t, k=k, oc8=oc8, tt2=tt2, jp=jp, bank=bank:
                                       t.matmul(PB[bank][:, 0:512],
                                                wqb[jp][:, k * 1024 + oc8 * 128:
                                                         k * 1024 + (oc8 + 1) * 128],
                                                ln2v[:, k, tt2 * 512:(tt2 + 1) * 512],
                                                start=(k == 0), stop=(k == 15)),
                                       rd=[f"awqb{jp}"] if k in (0, 15) else [],
                                       wr=[f"pb{bank}"] if k in (0, 15) else [],
                                       tag=f"qmm{j}.{oc8}.{tt2}.{k}")
                            p.step("act", lambda s, tt2=tt2, bank=bank: s.activation(
                                zstage[tt2][:], PB[bank][:, 0:512], AF.Copy),
                                rd=[f"pb{bank}"], wr=[f"azst{tt2}"],
                                tag=f"qev{j}.{oc8}.{tt2}")
                            if tt2 == 1:
                                g_ = j * 8 + oc8
                                p.step("dve", lambda v, g_=g_: v.tensor_copy(
                                    ztail_sb[:, g_ * 3:(g_ + 1) * 3],
                                    zstage[1][:, 509:512]),
                                    rd=["azst1"], wr=[f"aztl.{g_}"],
                                    tag=f"ztl{g_}")
                            p.step("gpq", lambda g, g2=j * 8 + oc8, tt2=tt2: g.dma_start(
                                out=zpre[g2 * 128:(g2 + 1) * 128,
                                         tt2 * 512:(tt2 + 1) * 512],
                                in_=zstage[tt2][:]),
                                rd=[f"azst{tt2}"] + ([f"aztl.{j * 8 + oc8}"]
                                                     if tt2 == 1 else []),
                                wr=["zpre_d"], tag=f"zst{j}.{oc8}.{tt2}", ch="zpst")
                p.step("gpq", lambda g: g.dma_start(out=ztail_in, in_=ztail_sb[:]),
                       rd=[f"aztl.{g_}" for g_ in range(48)], wr=["ztail_in_d"],
                       tag="ztlst", ch="ztl")
                p.barrier()

            p.step("pool", lambda g: g.collective_compute(
                "AllGather", ALU.bypass, replica_groups=PAIRS,
                ins=[ztail_in], outs=[ztail_gat]),
                rd=["ztail_in_d"], wr=["ztail_gat_d"], tag="cc_ztl", wb=True)
            p.step("pool", lambda g: g.collective_compute(
                "AllGather", ALU.bypass, replica_groups=PAIRS,
                ins=[ab_out], outs=[ab_gat]),
                rd=["ab_out_d"], wr=["ab_gat_d"], tag="cc_ab", wb=True)
            p.barrier()

            p.fence("lnst", "zpst", "abst", "ztl")
            with ExitStack() as s2:
                asb = lambda name, shape, dt=F32: s2.enter_context(
                    nc.sbuf_tensor(f"a2{name}", list(shape), dt))
                halo = asb("halo", [128, 144], BF16)
                zw = asb("zw", [128, 8 * 1027], BF16)
                zw3 = zw[:].rearrange("p (g n) -> p g n", g=8)
                facc = asb("facc", [128, 1024])
                znorm = asb("znorm", [128, 8 * 1024], BF16)
                zn3 = znorm[:].rearrange("p (g n) -> p g n", g=8)
                sqb = asb("sqb", [128, 1024], BF16)
                rnr = asb("rnr", [8, 1024])
                rnb = asb("rnb", [8, 1024], BF16)

                p.step("gpq", lambda g: g.dma_start(out=halo[:], in_=ztail_gat[0]),
                       rd=["ztail_gat_d"], wr=["ahalo"], tag="halold", ch="halold")
                p.step("dve", lambda v: v.tensor_scalar(
                    halo[:], halo[:], c_m3[:, :], None, ALU.mult),
                    wr=["ahalo"], tag="halomask")
                for j in range(6):
                    for gg in range(8):
                        g_ = j * 8 + gg
                        p.step("gpq", lambda g, g_=g_, gg=gg: g.dma_start(
                            out=zw3[:, gg, 3:1027],
                            in_=zpre[g_ * 128:(g_ + 1) * 128, :]),
                            rd=["zpre_d"], wr=[f"azw.{gg}"], tag=f"zwld{g_}",
                            ch="zwld")
                    for gg in range(8):
                        g_ = j * 8 + gg
                        p.step("dve", lambda v, g_=g_, gg=gg: v.tensor_copy(
                            zw3[:, gg, 0:3], halo[:, g_ * 3:(g_ + 1) * 3]),
                            rd=["ahalo"], wr=[f"azw.{gg}"], tag=f"zwh{g_}")
                    for gg in range(8):
                        g_ = j * 8 + gg
                        p.step("dve", lambda v, gg=gg, g_=g_: v.tensor_scalar(
                            facc[:], zw3[:, gg, 0:1024],
                            c_cw[:, g_ * 4:g_ * 4 + 1], None, ALU.mult),
                            rd=[f"azw.{gg}"], wr=["afacc"], tag=f"cv0.{g_}")
                        for j2 in range(1, 4):
                            p.step("dve", lambda v, gg=gg, g_=g_, j2=j2:
                                   v.scalar_tensor_tensor(
                                       facc[:], zw3[:, gg, j2:j2 + 1024],
                                       c_cw[:, g_ * 4 + j2:g_ * 4 + j2 + 1],
                                       facc[:], ALU.mult, ALU.add),
                                   rd=[f"azw.{gg}"], wr=["afacc"], tag=f"cv{j2}.{g_}")
                        p.step("act", lambda s, gg=gg: s.activation(
                            zn3[:, gg, :], facc[:], AF.Silu),
                            rd=["afacc"], wr=[f"azn.{gg}"], tag=f"cvsilu.{g_}")
                        if j < 4:
                            p.step("act", lambda s, gg=gg: s.activation(
                                sqb[:], zn3[:, gg, :], AF.Square),
                                rd=[f"azn.{gg}"], wr=["asqb"], tag=f"nsq.{g_}")
                            for hh in range(2):
                                p.step("pe", lambda t, hh=hh: t.matmul(
                                    PB[4 + hh][0:1, 0:512], c_onescolb[:, :],
                                    sqb[:, hh * 512:(hh + 1) * 512],
                                    start=True, stop=True),
                                    rd=["asqb"], wr=[f"pb{4 + hh}"],
                                    tag=f"nsum.{g_}.{hh}")
                                p.step("act", lambda s, hh=hh: s.activation(
                                    rnr[0:1, hh * 512:(hh + 1) * 512],
                                    PB[4 + hh][0:1, 0:512], AF.Ln),
                                    rd=[f"pb{4 + hh}"], wr=[f"arnr{hh}"],
                                    tag=f"nln.{g_}.{hh}")
                                p.step("act", lambda s, hh=hh: s.activation(
                                    rnr[0:1, hh * 512:(hh + 1) * 512],
                                    rnr[0:1, hh * 512:(hh + 1) * 512],
                                    AF.Exp, scale=-0.5),
                                    wr=[f"arnr{hh}"], tag=f"nexp.{g_}.{hh}")
                                p.step("dve", lambda v, hh=hh: v.tensor_scalar(
                                    rnb[0:1, hh * 512:(hh + 1) * 512],
                                    rnr[0:1, hh * 512:(hh + 1) * 512],
                                    1e12, None, ALU.min),
                                    rd=[f"arnr{hh}"], wr=[f"arnb{hh}"],
                                    tag=f"nmin.{g_}.{hh}")
                                p.step("pe", lambda t, hh=hh: t.matmul(
                                    PB[6 + hh][:, 0:512], c_onesb[0:1, :],
                                    rnb[0:1, hh * 512:(hh + 1) * 512],
                                    start=True, stop=True),
                                    rd=[f"arnb{hh}"], wr=[f"pb{6 + hh}"],
                                    tag=f"nbc.{g_}.{hh}")
                                p.step("dve", lambda v, gg=gg, hh=hh: v.tensor_mul(
                                    zn3[:, gg, hh * 512:(hh + 1) * 512],
                                    zn3[:, gg, hh * 512:(hh + 1) * 512],
                                    PB[6 + hh][:, 0:512]),
                                    rd=[f"pb{6 + hh}"], wr=[f"azn.{gg}"],
                                    tag=f"nmul.{g_}.{hh}")
                        p.step("gpq", lambda g, g_=g_, gg=gg: g.dma_start(
                            out=qkv_out[g_ * 128:(g_ + 1) * 128, :],
                            in_=zn3[:, gg, :]),
                            rd=[f"azn.{gg}"], wr=["qkv_out_d"], tag=f"qkst{g_}",
                            ch="qkst")
                p.barrier()

            p.step("pool", lambda g: g.collective_compute(
                "AllGather", ALU.bypass, replica_groups=PAIRS,
                ins=[qkv_out], outs=[qkv_gat]),
                rd=["qkv_out_d"], wr=["qkv_gat_d"], tag="cc_qkv", wb=True)
            p.barrier()

            p.fence("qkst", "zwld")
            sA = ExitStack()
            asb = lambda name, shape, dt=F32: sA.enter_context(
                nc.sbuf_tensor(f"a6{name}", list(shape), dt))
            qkvh = asb("qkvh", [128, 24 * 2048], BF16)
            ab_all = asb("aball", [128, 16 * 32])
            alpha_s = asb("alphas", [128, 128])
            beta_s = asb("betas", [128, 128])
            ell_tm = asb("elltm", [128, 128])
            negb_tm = asb("negbtm", [128, 128])
            O_fm = asb("Ofm", [128, 8 * 2048], BF16)
            S0f = asb("S0f", [128, 8 * 128])
            S0b = asb("S0b", [128, 8 * 128], BF16)

            rt = nc.gpsimd.alloc_register("rt")
            qkvh3 = qkvh[:].rearrange("p (s n) -> p s n", s=24)
            for t3 in range(3):
                for r in range(2):
                    i3 = t3 * 2 + r
                    def fnq(g, i3=i3, t3=t3, r=r):
                        g.reg_load(rt, c_poff[0:1, i3:i3 + 1])
                        return g.dma_start(
                            out=qkvh3[:, t3 * 8:(t3 + 1) * 8,
                                      r * 1024:(r + 1) * 1024],
                            in_=bass.AP(qkv_gat_t, rt,
                                        [[T, 128], [128 * T, 8], [1, 1024]]))
                    p.step("gpq", fnq, rd=["qkv_gat_d", "c_poff"], wr=["aqkvh"],
                           tag=f"qhl{t3}.{r}", ch="qhl")
            for tt3 in range(16):
                r, tl = tt3 // 8, tt3 % 8
                p.step("gpq", lambda g, r=r, tl=tl, tt3=tt3: g.dma_start(
                    out=ab_all[:, tt3 * 32:(tt3 + 1) * 32],
                    in_=ab_gat[r, tl * 128:(tl + 1) * 128, :]),
                    rd=["ab_gat_d"], wr=["aaball"], tag=f"abl{tt3}", ch="abl")
            abv = ab_all[:].rearrange("p (t c) -> p t c", t=16)
            alp3 = alpha_s[:].rearrange("p (t c) -> p t c", t=16)
            bet3 = beta_s[:].rearrange("p (t c) -> p t c", t=16)
            p.step("dve", lambda v: v.tensor_sub(
                alp3, abv[:, :, 8:16], abv[:, :, 0:8]),
                rd=["aaball"], wr=["aalpha"], tag="sel_a1")
            p.step("dve", lambda v: v.scalar_tensor_tensor(
                alp3, alp3, c_m3[:, :], abv[:, :, 0:8], ALU.mult, ALU.add),
                rd=["aaball"], wr=["aalpha"], tag="sel_a2")
            p.step("dve", lambda v: v.tensor_sub(
                bet3, abv[:, :, 24:32], abv[:, :, 16:24]),
                rd=["aaball"], wr=["abeta"], tag="sel_b1")
            p.step("dve", lambda v: v.scalar_tensor_tensor(
                bet3, bet3, c_m3[:, :], abv[:, :, 16:24], ALU.mult, ALU.add),
                rd=["aaball"], wr=["abeta"], tag="sel_b2")
            p.step("act", lambda s: s.activation(alpha_s[:], alpha_s[:], AF.Sigmoid),
                   wr=["aalpha"], tag="sig_a")
            p.step("act", lambda s: s.activation(beta_s[:], beta_s[:], AF.Sigmoid),
                   wr=["abeta"], tag="sig_b")
            p.step("act", lambda s: s.activation(ell_tm[:], alpha_s[:], AF.Ln),
                   rd=["aalpha"], wr=["aell"], tag="log_a")
            p.step("dve", lambda v: v.tensor_scalar(
                ell_tm[:], ell_tm[:], -60.0, None, ALU.max),
                wr=["aell"], tag="clamp_a")
            p.step("dve", lambda v: v.tensor_scalar(
                negb_tm[:], beta_s[:], -1.0, None, ALU.mult),
                rd=["abeta"], wr=["anegb"], tag="negb")
            p.step("dve", lambda v: v.memset(S0f[:], 0.0), wr=["aS0f"], tag="s0z")
            p.step("dve", lambda v: v.memset(S0b[:], 0.0), wr=["aS0b"], tag="s0bz")
            p.barrier()

            with ExitStack() as s6:
                csb = lambda name, shape, dt=F32: s6.enter_context(
                    nc.sbuf_tensor(f"a6c{name}", list(shape), dt))
                lell = csb("lell", [8, 128])
                lrow = csb("lrow", [8, 128])
                dtmp = csb("dtmp", [8, 128])
                drow = csb("drow", [8, 128])
                arow = csb("arow", [8, 128])
                acbr = csb("acbr", [8, 128])
                sc_tm = csb("sctm", [128, 32])
                negA8 = csb("negA8", [128, 8])
                lrbuf = csb("lrbuf", [8, 512])
                rsS = csb("rsS", [128, 512])
                rsI = csb("rsI", [128, 512])
                RtS = csb("RtS", [128, 512], BF16)
                RtI = csb("RtI", [128, 512], BF16)
                tDt = csb("tDt", [128, 512], BF16)
                tD = csb("tD", [128, 512], BF16)
                tW = csb("tW", [128, 512], BF16)
                tMt = csb("tMt", [128, 512], BF16)
                tVt = csb("tVt", [128, 512], BF16)
                tO2 = csb("tO2", [128, 512], BF16)
                tOt = csb("tOt", [128, 512], BF16)
                tKb = csb("tKb", [128, 512], BF16)

                bankctr = [0]

                def nb():
                    bankctr[0] = (bankctr[0] + 1) % 8
                    return bankctr[0]

                for c in range(16):
                    p.barrier()
                    b0 = nb()
                    p.step("pe", lambda t, c=c, b0=b0: t.transpose(
                        PB[b0][0:8, 0:128], ell_tm[:, c * 8:(c + 1) * 8], c_idf[:, :]),
                        rd=["aell"], wr=[f"pb{b0}"], tag=f"ellT{c}")
                    p.step("act", lambda s, b0=b0: s.activation(
                        lell[:], PB[b0][0:8, 0:128], AF.Copy),
                        rd=[f"pb{b0}"], wr=["clell"], tag=f"ellcp{c}")
                    p.step("dve", lambda v: v.tensor_tensor_scan(
                        lrow[:], lell[:], lell[:], 0.0, ALU.add, ALU.bypass),
                        rd=["clell"], wr=["clrow"], tag=f"scan{c}", wb=True)
                    p.step("dve", lambda v: v.tensor_scalar(
                        dtmp[:], lrow[:], lrow[:, 127:128], None, ALU.subtract),
                        rd=["clrow"], wr=["cdtmp"], tag=f"dsub{c}")
                    p.step("act", lambda s: s.activation(drow[:], dtmp[:], AF.Exp,
                                                         scale=-1.0),
                           rd=["cdtmp"], wr=["cdrow"], tag=f"dexp{c}")
                    p.step("act", lambda s: s.activation(arow[:], lrow[:], AF.Exp),
                           rd=["clrow"], wr=["carow"], tag=f"aexp{c}")
                    p.step("dve", lambda v: v.tensor_scalar(
                        acbr[:], c_ones8[:], arow[:, 127:128], None, ALU.mult),
                        rd=["carow"], wr=["cacb"], tag=f"acb{c}")
                    b1 = nb()
                    for qi, (rowt, rname) in enumerate(
                            [(drow, "cdrow"), (arow, "carow"),
                             (acbr, "cacb"), (lrow, "clrow")]):
                        p.step("pe", lambda t, rowt=rowt, qi=qi, b1=b1: t.transpose(
                            PB[b1][:, qi * 8:(qi + 1) * 8], rowt[:, :],
                            c_idf[0:8, 0:8]),
                            rd=[rname], wr=[f"pb{b1}"], tag=f"scT{c}.{qi}")
                    p.step("act", lambda s, b1=b1: s.activation(
                        sc_tm[:], PB[b1][:, 0:32], AF.Copy),
                        rd=[f"pb{b1}"], wr=["csc"], tag=f"scev{c}")
                    p.step("dve", lambda v: v.tensor_scalar(
                        negA8[:], sc_tm[:, 8:16], -1.0, None, ALU.mult),
                        rd=["csc"], wr=["cnegA"], tag=f"negA{c}")

                    if A6STOP and A6STOP == (c, -1, 0):
                        p.barrier()
                        s6.pop_all().close()
                        sA.close()
                        return
                    for g2 in range(2):
                        hs = [g2 * 4 + q for q in range(4)]
                        sl = lambda buf, q: buf[:, q * 128:(q + 1) * 128]
                        qf = lambda h, c=c: qkvh[:, (0 * 8 + h) * 2048 + c * 128:
                                                 (0 * 8 + h) * 2048 + (c + 1) * 128]
                        kf = lambda h, c=c: qkvh[:, (1 * 8 + h) * 2048 + c * 128:
                                                 (1 * 8 + h) * 2048 + (c + 1) * 128]
                        vf = lambda h, c=c: qkvh[:, (2 * 8 + h) * 2048 + c * 128:
                                                 (2 * 8 + h) * 2048 + (c + 1) * 128]
                        bkk = nb()
                        bkq = nb()
                        for q, h in enumerate(hs):
                            p.step("pe", lambda t, h=h, q=q, bkk=bkk: t.matmul(
                                PB[bkk][:, q * 128:(q + 1) * 128], kf(h), kf(h),
                                start=True, stop=True),
                                rd=["aqkvh"], wr=[f"pb{bkk}"], tag=f"gkk{c}.{h}")
                            p.step("pe", lambda t, h=h, q=q, bkq=bkq: t.matmul(
                                PB[bkq][:, q * 128:(q + 1) * 128], kf(h), qf(h),
                                start=True, stop=True),
                                rd=["aqkvh"], wr=[f"pb{bkq}"], tag=f"gkq{c}.{h}")
                        if A6STOP and A6STOP == (c, g2, 9):
                            p.barrier()
                            s6.pop_all().close()
                            sA.close()
                            return
                        brow = nb()
                        bbc = nb()
                        for q, h in enumerate(hs):
                            p.step("pe", lambda t, h=h, q=q, brow=brow: t.transpose(
                                PB[brow][0:1, q * 128:(q + 1) * 128],
                                sc_tm[:, 24 + h:25 + h], c_idf[:, :]),
                                rd=["csc"], wr=[f"pb{brow}"], tag=f"lrT{c}.{h}")
                            p.step("act", lambda s, q=q, brow=brow: s.activation(
                                lrbuf[0:1, q * 128:(q + 1) * 128],
                                PB[brow][0:1, q * 128:(q + 1) * 128], AF.Copy),
                                rd=[f"pb{brow}"], wr=[f"clr{q}"], tag=f"lrcp{c}.{q}")
                            p.step("pe", lambda t, q=q, bbc=bbc: t.matmul(
                                PB[bbc][:, q * 128:(q + 1) * 128],
                                c_ones8[0:1, :], lrbuf[0:1, q * 128:(q + 1) * 128],
                                start=True, stop=True),
                                rd=[f"clr{q}"], wr=[f"pb{bbc}"], tag=f"lbc{c}.{q}")
                        for q, h in enumerate(hs):
                            p.step("dve", lambda v, h=h, q=q, bbc=bbc: v.scalar_tensor_tensor(
                                sl(rsS, q), PB[bbc][:, q * 128:(q + 1) * 128],
                                sc_tm[:, 24 + h:25 + h], c_maskS[:, :],
                                ALU.subtract, ALU.add),
                                rd=[f"pb{bbc}", "csc"], wr=["crsS"], tag=f"rsS{c}.{h}")
                            p.step("dve", lambda v, h=h, q=q, bbc=bbc: v.scalar_tensor_tensor(
                                sl(rsI, q), PB[bbc][:, q * 128:(q + 1) * 128],
                                sc_tm[:, 24 + h:25 + h], c_maskI[:, :],
                                ALU.subtract, ALU.add),
                                rd=[f"pb{bbc}", "csc"], wr=["crsI"], tag=f"rsI{c}.{h}")
                        p.step("act", lambda s: s.activation(RtS[:], rsS[:], AF.Exp),
                               rd=["crsS"], wr=["cRtS"], tag=f"expS{c}.{g2}")
                        p.step("act", lambda s: s.activation(RtI[:], rsI[:], AF.Exp),
                               rd=["crsI"], wr=["cRtI"], tag=f"expI{c}.{g2}")
                        for q, h in enumerate(hs):
                            p.step("dve", lambda v, h=h, q=q, c=c, bkk=bkk:
                                   v.scalar_tensor_tensor(
                                       sl(tDt, q), PB[bkk][:, q * 128:(q + 1) * 128],
                                       negb_tm[:, c * 8 + h:c * 8 + h + 1],
                                       sl(RtS, q), ALU.mult, ALU.mult),
                                   rd=[f"pb{bkk}", "cRtS", "anegb"], wr=["cDt"],
                                   tag=f"bt{c}.{h}")
                            p.step("dve", lambda v, q=q, bkq=bkq: v.tensor_mul(
                                sl(tMt, q), PB[bkq][:, q * 128:(q + 1) * 128],
                                sl(RtI, q)),
                                rd=[f"pb{bkq}", "cRtI"], wr=["cMt"], tag=f"mt{c}.{h}")
                        if A6STOP and A6STOP == (c, g2, 1):
                            p.barrier()
                            s6.pop_all().close()
                            sA.close()
                            return
                        bD0 = nb()
                        bVt = nb()
                        bW0 = nb()
                        for q, h in enumerate(hs):
                            p.step("pe", lambda t, q=q, bD0=bD0: t.matmul(
                                PB[bD0][:, q * 128:(q + 1) * 128], sl(tDt, q),
                                c_idb[:, :], start=True, stop=True),
                                rd=["cDt"], wr=[f"pb{bD0}"], tag=f"d0T{c}.{h}")
                            p.step("pe", lambda t, h=h, q=q, bVt=bVt: t.matmul(
                                PB[bVt][:, q * 128:(q + 1) * 128], vf(h),
                                c_idb[:, :], start=True, stop=True),
                                rd=["aqkvh"], wr=[f"pb{bVt}"], tag=f"vT{c}.{h}")
                            p.step("pe", lambda t, h=h, q=q, bW0=bW0: t.matmul(
                                PB[bW0][:, q * 128:(q + 1) * 128], kf(h),
                                S0b[:, h * 128:(h + 1) * 128],
                                start=True, stop=True),
                                rd=["aqkvh", "aS0b"], wr=[f"pb{bW0}"],
                                tag=f"w0{c}.{h}")
                        p.step("act", lambda s, bD0=bD0: s.activation(
                            tD[:], PB[bD0][:, 0:512], AF.Copy),
                            rd=[f"pb{bD0}"], wr=["cD"], tag=f"d0ev{c}.{g2}")
                        p.step("act", lambda s, bVt=bVt: s.activation(
                            tVt[:], PB[bVt][:, 0:512], AF.Copy),
                            rd=[f"pb{bVt}"], wr=["cVt"], tag=f"vev{c}.{g2}")
                        for q, h in enumerate(hs):
                            p.step("dve", lambda v, h=h, q=q, bW0=bW0:
                                   v.scalar_tensor_tensor(
                                       sl(tW, q), PB[bW0][:, q * 128:(q + 1) * 128],
                                       negA8[:, h:h + 1], sl(tVt, q),
                                       ALU.mult, ALU.add),
                                   rd=[f"pb{bW0}", "cVt", "cnegA"], wr=["cW"],
                                   tag=f"vw{c}.{h}")
                        for kk in range(7):
                            ba = nb()
                            for q, h in enumerate(hs):
                                p.step("pe", lambda t, q=q, ba=ba: t.matmul(
                                    PB[ba][:, q * 128:(q + 1) * 128], sl(tDt, q),
                                    sl(tW, q), start=True, stop=False,
                                    skip_group_check=True),
                                    rd=["cDt", "cW"], wr=[f"pb{ba}"],
                                    tag=f"ap{c}.{g2}.{kk}.{q}a")
                                p.step("pe", lambda t, q=q, ba=ba: t.matmul(
                                    PB[ba][:, q * 128:(q + 1) * 128], c_idb[:, :],
                                    sl(tW, q), start=False, stop=True,
                                    skip_group_check=True),
                                    rd=["cW"], wr=[f"pb{ba}"],
                                    tag=f"ap{c}.{g2}.{kk}.{q}b")
                            if kk < 6:
                                bb = nb()
                                bc = nb()
                                for q in range(4):
                                    p.step("pe", lambda t, q=q, bb=bb: t.matmul(
                                        PB[bb][:, q * 128:(q + 1) * 128], sl(tDt, q),
                                        sl(tD, q), start=True, stop=True),
                                        rd=["cDt", "cD"], wr=[f"pb{bb}"],
                                        tag=f"sqD{c}.{g2}.{kk}.{q}")
                                    p.step("pe", lambda t, q=q, bc=bc: t.matmul(
                                        PB[bc][:, q * 128:(q + 1) * 128], sl(tD, q),
                                        sl(tDt, q), start=True, stop=True),
                                        rd=["cDt", "cD"], wr=[f"pb{bc}"],
                                        tag=f"sqDt{c}.{g2}.{kk}.{q}")
                                p.step("act", lambda s, bb=bb: s.activation(
                                    tD[:], PB[bb][:, 0:512], AF.Copy),
                                    rd=[f"pb{bb}"], wr=["cD"], tag=f"evD{c}.{g2}.{kk}")
                                p.step("act", lambda s, bc=bc: s.activation(
                                    tDt[:], PB[bc][:, 0:512], AF.Copy),
                                    rd=[f"pb{bc}"], wr=["cDt"],
                                    tag=f"evDt{c}.{g2}.{kk}")
                            if kk < 6:
                                p.step("act", lambda s, ba=ba: s.activation(
                                    tW[:], PB[ba][:, 0:512], AF.Copy),
                                    rd=[f"pb{ba}"], wr=["cW"], tag=f"evW{c}.{g2}.{kk}")
                            else:
                                for q, h in enumerate(hs):
                                    p.step("act", lambda s, q=q, h=h, c=c, ba=ba:
                                           s.activation(
                                               sl(tW, q),
                                               PB[ba][:, q * 128:(q + 1) * 128],
                                               AF.Copy,
                                               scale=beta_s[:, c * 8 + h:
                                                            c * 8 + h + 1]),
                                           rd=[f"pb{ba}"], wr=["cW"],
                                           tag=f"evU{c}.{g2}.{q}")
                        if A6STOP and A6STOP == (c, g2, 2):
                            p.barrier()
                            s6.pop_all().close()
                            sA.close()
                            return
                        bO = nb()
                        bO2 = nb()
                        for q, h in enumerate(hs):
                            p.step("pe", lambda t, q=q, bO=bO: t.matmul(
                                PB[bO][:, q * 128:(q + 1) * 128], sl(tMt, q),
                                sl(tW, q), start=True, stop=True),
                                rd=["cMt", "cW"], wr=[f"pb{bO}"], tag=f"o1{c}.{h}")
                            p.step("pe", lambda t, h=h, q=q, bO2=bO2: t.matmul(
                                PB[bO2][:, q * 128:(q + 1) * 128], qf(h),
                                S0b[:, h * 128:(h + 1) * 128],
                                start=True, stop=True),
                                rd=["aqkvh", "aS0b"], wr=[f"pb{bO2}"],
                                tag=f"o2{c}.{h}")
                        for q, h in enumerate(hs):
                            p.step("act", lambda s, q=q, h=h, bO2=bO2: s.activation(
                                sl(tO2, q), PB[bO2][:, q * 128:(q + 1) * 128],
                                AF.Copy, scale=sc_tm[:, 8 + h:9 + h]),
                                rd=[f"pb{bO2}", "csc"], wr=["cO2"],
                                tag=f"o2ev{c}.{h}")
                        p.step("act", lambda s, bO=bO: s.activation(
                            tOt[:], PB[bO][:, 0:512], AF.Copy),
                            rd=[f"pb{bO}"], wr=["cOt"], tag=f"oev{c}.{g2}")
                        p.step("dve", lambda v: v.tensor_add(
                            tOt[:], tOt[:], tO2[:]),
                            rd=["cO2"], wr=["cOt"], tag=f"oadd{c}.{g2}")
                        if A6STOP and A6STOP == (c, g2, 3):
                            p.barrier()
                            s6.pop_all().close()
                            sA.close()
                            return
                        bK = nb()
                        bS = nb()
                        for q, h in enumerate(hs):
                            p.step("pe", lambda t, h=h, q=q, bK=bK: t.matmul(
                                PB[bK][:, q * 128:(q + 1) * 128], kf(h),
                                c_idb[:, :], start=True, stop=True),
                                rd=["aqkvh"], wr=[f"pb{bK}"], tag=f"kT{c}.{h}")
                        for q, h in enumerate(hs):
                            p.step("act", lambda s, q=q, h=h, bK=bK: s.activation(
                                sl(tKb, q), PB[bK][:, q * 128:(q + 1) * 128],
                                AF.Copy, scale=sc_tm[:, h:h + 1]),
                                rd=[f"pb{bK}", "csc"], wr=["cKb"], tag=f"kbev{c}.{h}")
                        for q, h in enumerate(hs):
                            p.step("pe", lambda t, q=q, bS=bS: t.matmul(
                                PB[bS][:, q * 128:(q + 1) * 128], sl(tKb, q),
                                sl(tW, q), start=True, stop=True),
                                rd=["cKb", "cW"], wr=[f"pb{bS}"], tag=f"smm{c}.{h}")
                        for q, h in enumerate(hs):
                            p.step("dve", lambda v, q=q, h=h, bS=bS:
                                   v.scalar_tensor_tensor(
                                       S0f[:, h * 128:(h + 1) * 128],
                                       S0f[:, h * 128:(h + 1) * 128],
                                       sc_tm[:, 16 + h:17 + h],
                                       PB[bS][:, q * 128:(q + 1) * 128],
                                       ALU.mult, ALU.add),
                                   rd=[f"pb{bS}", "csc"], wr=["aS0f"],
                                   tag=f"sup{c}.{h}")
                        p.step("act", lambda s, g2=g2: s.activation(
                            S0b[:, g2 * 512:(g2 + 1) * 512],
                            S0f[:, g2 * 512:(g2 + 1) * 512], AF.Copy),
                            rd=["aS0f"], wr=["aS0b"], tag=f"s0b{c}.{g2}")
                        if A6STOP and A6STOP == (c, g2, 4):
                            p.barrier()
                            s6.pop_all().close()
                            sA.close()
                            return
                        bT = nb()
                        for q, h in enumerate(hs):
                            p.step("pe", lambda t, q=q, bT=bT: t.matmul(
                                PB[bT][:, q * 128:(q + 1) * 128], sl(tOt, q),
                                c_idb[:, :], start=True, stop=True),
                                rd=["cOt"], wr=[f"pb{bT}"], tag=f"oT{c}.{h}")
                        g0 = hs[0]
                        p.step("act", lambda s, c=c, g0=g0, bT=bT: s.activation(
                            O_fm[:].rearrange("p (g n) -> p g n", g=8)[
                                :, g0:g0 + 4, c * 128:(c + 1) * 128],
                            PB[bT][:, 0:512].rearrange("p (q n) -> p q n", q=4),
                            AF.Copy),
                            rd=[f"pb{bT}"], wr=[f"aOfm.{c}.{g2}"],
                            tag=f"ofmev{c}.{g2}")
                        if A6STOP and (c, g2) == A6STOP:
                            p.barrier()
                            s6.pop_all().close()
                            sA.close()
                            return
                if CUT == "a6":
                    p.barrier()
                    sA.close()
                    return
                p.barrier()

            for g_ in range(NH):
                p.step("gpq", lambda g, g_=g_: g.dma_start(
                    out=o_out[g_ * 128:(g_ + 1) * 128, :],
                    in_=O_fm[:, g_ * 2048:(g_ + 1) * 2048]),
                    rd=[f"aOfm.{c2}.{g22}" for c2 in range(16) for g22 in range(2)],
                    wr=["o_out_d"], tag=f"ost{g_}", ch="ost")
            p.barrier()
            p.step("pool", lambda g: g.collective_compute(
                "AllGather", ALU.bypass, replica_groups=PAIRS,
                ins=[o_out], outs=[o_gat]),
                rd=["o_out_d"], wr=["o_gat_d"], tag="cc_o", wb=True)
            p.barrier()
            sA.close()

            p.fence("ost")
            with ExitStack() as s8:
                asb = lambda name, shape, dt=F32: s8.enter_context(
                    nc.sbuf_tensor(f"a8{name}", list(shape), dt))
                Ofull = asb("Ofull", [128, 16 * 1024], BF16)
                lnfm2b = asb("lnfm2b", [128, 16 * 1024], BF16)
                ln2vb = lnfm2b[:].rearrange("p (k n) -> p k n", k=16)
                wgb = [asb(f"wgb{i}", [128, 16 * 1024], BF16) for i in range(2)]
                wstg8 = [asb(f"wstg{i}", [128, 1024]) for i in range(2)]
                gstage = asb("gstage", [128, 512], BF16)
                ystage = asb("ystage", [128, 4 * 512])
                xr = [asb(f"xr{i}", [128, 512]) for i in range(2)]

                Ofull3 = Ofull[:].rearrange("p (s n) -> p s n", s=16)
                for r in range(2):
                    def fno(g, r=r):
                        g.reg_load(rt, c_poff[0:1, 6 + r:7 + r])
                        return g.dma_start(
                            out=Ofull3[:, r * 8:(r + 1) * 8, :],
                            in_=bass.AP(o_gat_t, rt,
                                        [[D, 128], [128 * D, 8], [1, 1024]]))
                    p.step("gpq", fno, rd=["o_gat_d", "c_poff"], wr=["aOfull"],
                           tag=f"ofl{r}", ch="ofl")
                for k in range(16):
                    p.step("syq", lambda sy, k=k: sy.dma_start(
                        out=lnfm2b[:, k * 1024:(k + 1) * 1024],
                        in_=lnfm2d[:, k * 1024:(k + 1) * 1024]),
                        rd=["lnfm2d_d"], wr=["aln2b"], tag=f"lnld{k}", ch="lnld")
                for halfg in range(2):
                    jp = halfg % 2
                    for k in range(16):
                        sl2 = k % 2
                        p.step("syq", lambda sy, halfg=halfg, k=k, sl2=sl2: sy.dma_start(
                            out=wstg8[sl2][:],
                            in_=wgT[k * 128:(k + 1) * 128,
                                    halfg * 1024:(halfg + 1) * 1024]),
                            wr=[f"a8wst{sl2}"], tag=f"wgld{halfg}.{k}",
                            ch=f"a8wst{sl2}")
                        p.step("dve", lambda v, k=k, sl2=sl2, jp=jp: v.tensor_copy(
                            wgb[jp][:, k * 1024:(k + 1) * 1024], wstg8[sl2][:]),
                            rd=[f"a8wst{sl2}"],
                            wr=([f"awgb{jp}"] if k in (0, 15) else []),
                            tag=f"wgcast{halfg}.{k}")
                    for oc8 in range(8):
                        cc = halfg * 8 + oc8
                        for tt2 in range(2):
                            bank = (oc8 * 2 + tt2) % 8
                            for k in range(16):
                                p.step("pe", lambda t, k=k, oc8=oc8, tt2=tt2, jp=jp,
                                       bank=bank: t.matmul(
                                           PB[bank][:, 0:512],
                                           wgb[jp][:, k * 1024 + oc8 * 128:
                                                    k * 1024 + (oc8 + 1) * 128],
                                           ln2vb[:, k, tt2 * 512:(tt2 + 1) * 512],
                                           start=(k == 0), stop=(k == 15)),
                                       rd=(["awgb" + str(jp), "aln2b"]
                                           if k in (0, 15) else []),
                                       wr=[f"pb{bank}"] if k in (0, 15) else [],
                                       tag=f"gmm{halfg}.{oc8}.{tt2}.{k}")
                            p.step("act", lambda s, bank=bank: s.activation(
                                gstage[:], PB[bank][:, 0:512], AF.Silu),
                                rd=[f"pb{bank}"], wr=["agst"],
                                tag=f"gsilu{halfg}.{oc8}.{tt2}")
                            p.step("dve", lambda v, cc=cc, tt2=tt2: v.tensor_mul(
                                Ofull[:, cc * 1024 + tt2 * 512:
                                      cc * 1024 + (tt2 + 1) * 512],
                                Ofull[:, cc * 1024 + tt2 * 512:
                                      cc * 1024 + (tt2 + 1) * 512],
                                gstage[:]),
                                rd=["agst", "aOfull"], wr=["aOfull"],
                                tag=f"gate{halfg}.{oc8}.{tt2}")
                p.barrier()
                for ocg in range(2):
                    jp = ocg % 2
                    for k in range(16):
                        sl2 = k % 2
                        p.step("syq", lambda sy, ocg=ocg, k=k, sl2=sl2: sy.dma_start(
                            out=wstg8[sl2][:],
                            in_=woT[k * 128:(k + 1) * 128,
                                    ocg * 1024:(ocg + 1) * 1024]),
                            wr=[f"a8wst{sl2}"], tag=f"wold{ocg}.{k}",
                            ch=f"a8wst{sl2}")
                        p.step("dve", lambda v, k=k, sl2=sl2, jp=jp: v.tensor_copy(
                            wgb[jp][:, k * 1024:(k + 1) * 1024], wstg8[sl2][:]),
                            rd=[f"a8wst{sl2}"],
                            wr=([f"awgb{jp}"] if k in (0, 15) else []),
                            tag=f"wocast{ocg}.{k}")
                    for tt2 in range(2):
                        tok0 = tt2 * 512
                        for oc8 in range(8):
                            for k in range(16):
                                p.step("pe", lambda t, k=k, oc8=oc8, tt2=tt2, jp=jp:
                                       t.matmul(
                                           PB[oc8][:, 0:512],
                                           wgb[jp][:, k * 1024 + oc8 * 128:
                                                    k * 1024 + (oc8 + 1) * 128],
                                           Ofull[:, k * 1024 + tt2 * 512:
                                                 k * 1024 + (tt2 + 1) * 512],
                                           start=(k == 0), stop=(k == 15)),
                                       rd=(["awgb" + str(jp), "aOfull"]
                                           if k in (0, 15) else []),
                                       wr=[f"pb{oc8}"] if k in (0, 15) else [],
                                       tag=f"womm{ocg}.{tt2}.{oc8}.{k}")
                        for half in range(2):
                            for q4 in range(4):
                                oc = half * 4 + q4
                                p.step("act", lambda s, oc=oc, q4=q4: s.activation(
                                    ystage[:, q4 * 512:(q4 + 1) * 512],
                                    PB[oc][:, 0:512], AF.Copy),
                                    rd=[f"pb{oc}"], wr=[f"a8ys.{q4}"],
                                    tag=f"woev{ocg}.{tt2}.{oc}")
                            for ttt in range(4):
                                r0 = tok0 + ttt * 128
                                c0 = ocg * 1024 + half * 512
                                bank = ttt if half == 0 else 4 + ttt
                                xb2 = f"a8xr{ttt % 2}"
                                p.step("scq", lambda s, r0=r0, ttt=ttt, c0=c0:
                                       s.dma_start(out=xr[ttt % 2][:],
                                                   in_=src_x[r0:r0 + 128,
                                                             c0:c0 + 512]),
                                       rd=[f"{src_res}.{r0}.{ocg}.0",
                                           f"{src_res}.{r0}.{ocg}.1"],
                                       wr=[xb2], tag=f"aldxr{ocg}.{tt2}.{half}.{ttt}",
                                       ch=f"a8xr{ttt % 2}")
                                for q in range(4):
                                    p.step("pe", lambda t, q=q, ttt=ttt, bank=bank:
                                           t.transpose(
                                               PB[bank][:, q * 128:(q + 1) * 128],
                                               ystage[:, q * 512 + ttt * 128:
                                                       q * 512 + (ttt + 1) * 128],
                                               c_idf[:, :]),
                                           rd=[f"a8ys.{q}"], wr=[f"pb{bank}"],
                                           tag=f"aytr{ocg}.{tt2}.{ttt}.{half}.{q}")
                                p.step("dve", lambda v, ttt=ttt, bank=bank:
                                       v.tensor_add(xr[ttt % 2][:], xr[ttt % 2][:],
                                                    PB[bank][:, 0:512]),
                                       rd=[f"pb{bank}"], wr=[xb2],
                                       tag=f"ayadd{ocg}.{tt2}.{ttt}.{half}")
                                stx = p.step("gpq", lambda g, r0=r0, ttt=ttt, c0=c0:
                                             g.dma_start(
                                                 out=dst_x[r0:r0 + 128, c0:c0 + 512],
                                                 in_=xr[ttt % 2][:]),
                                             rd=[xb2],
                                             wr=[f"{dst_res}.{r0}.{ocg}.{half}"],
                                             tag=f"astx{ocg}.{tt2}.{half}.{ttt}",
                                             ch=f"st_{dst_res}")
                                if dst_x is y_out:
                                    final_deps.append(stx)
                    p.barrier()
            return

        def emit_ffn(call, src_x, dst_x, src_res, dst_res):
            with ExitStack() as fs:
                fsb = lambda name, shape, dt=F32: fs.enter_context(
                    nc.sbuf_tensor(f"f{call}{name}", list(shape), dt))
                lnfm = fsb("lnfm", [128, 16 * 1024], BF16)
                lnv = lnfm[:].rearrange("p (k n) -> p k n", k=16)
                xt = fsb("xt", [128, D])
                st24 = fsb("st24", [128, 24])
                mv = fsb("mv", [128, 2])
                rstd = fsb("rstd", [128, 2])
                xnb = fsb("xnb", [128, D], BF16)
                s_sc = fsb("s_sc", [128, 8])
                s_e1 = fsb("s_e1", [128, 8])
                s_mk = fsb("s_mk", [128, 8])
                s_e2 = fsb("s_e2", [128, 8])
                s_m1 = fsb("s_m1", [128, 1])
                s_m2 = fsb("s_m2", [128, 1])
                s_g1 = fsb("s_g1", [128, 1])
                s_g2 = fsb("s_g2", [128, 1])
                s_t1 = fsb("s_t1", [128, 8])
                combb = fsb("combb", [128, 64], BF16)
                comb_fm = fsb("comb_fm", [8, 1024], BF16)
                comb_bc = fsb("cbc", [128, 1024], BF16)
                w1sb = fsb("w1sb", [128, 16 * HF], BF16)
                wst = fsb("wst", [128, 2 * 704])
                hstage = fsb("hst", [128, 11 * 1024], BF16)
                hsv = hstage[:].rearrange("p (hc n) -> p hc n", hc=11)
                w2sb = [fsb(f"w2sb{i}", [128, 1408], BF16) for i in range(2)]
                y_acc = fsb("yac", [128, 16 * 1024])

                wslot = [0]

                def w_load(src_ap, tag):
                    sl = wslot[0] % 2
                    wslot[0] += 1
                    p.step("syq", lambda sy, src_ap=src_ap, sl=sl: sy.dma_start(
                        out=wst[:, sl * 704:(sl + 1) * 704], in_=src_ap),
                        rd=[], wr=[f"wst{sl}"], tag=tag, ch=f"wst{call}{sl}")
                    return sl

                def w2_pair(e, ot):
                    wi = ot % 2
                    for hf in range(2):
                        sl = w_load(w2R[e, ot, :, hf * 704:(hf + 1) * 704],
                                    f"w2ld{e}.{ot}.{hf}")
                        p.step("dve", lambda v, sl=sl, wi=wi, hf=hf: v.tensor_copy(
                            w2sb[wi][:, hf * 704:(hf + 1) * 704],
                            wst[:, sl * 704:(sl + 1) * 704]),
                            rd=[f"wst{sl}"], wr=[f"w2sb{wi}"],
                            tag=f"w2cast{e}.{ot}.{hf}")

                def w1_pair(e, k, extra=()):
                    for hf in range(2):
                        sl = w_load(w1T[e, k * 128:(k + 1) * 128,
                                        hf * 704:(hf + 1) * 704],
                                    f"w1ld{e}.{k}.{hf}")
                        p.step("dve", lambda v, k=k, sl=sl, hf=hf: v.tensor_copy(
                            w1sb[:, k * HF + hf * 704:k * HF + (hf + 1) * 704],
                            wst[:, sl * 704:(sl + 1) * 704]),
                            rd=[f"wst{sl}"],
                            wr=["w1rdy"] if (k == 15 and hf == 1) else [f"w1c{k}"],
                            extra=list(extra) if hf == 0 else [],
                            tag=f"w1cast{e}.{k}.{hf}")

                for ttt in range(8):
                    r0 = ttt * 128
                    p.step("scq", lambda s, r0=r0: s.dma_start(
                        out=xt[:], in_=src_x[r0:r0 + 128, :]),
                        rd=[f"{src_res}.{r0}.0.0", f"{src_res}.{r0}.0.1",
                            f"{src_res}.{r0}.1.0", f"{src_res}.{r0}.1.1"],
                        wr=["xtb"], tag=f"ldx{ttt}", ch=f"xt{call}")
                    for q in range(4):
                        p.step("dve", lambda v, q=q: v.bn_stats(
                            st24[:, q * 6:(q + 1) * 6],
                            xt[:, q * 512:(q + 1) * 512]),
                            rd=["xtb"], wr=[f"st24.{q}"], tag=f"bn{ttt}.{q}", wb=True)
                    p.step("dve", lambda v: v.bn_aggr(mv[:], st24[:]),
                           rd=["st24.0", "st24.1", "st24.2", "st24.3"],
                           wr=["mv"], tag="aggr", wb=True)
                    p.step("act", lambda s: s.activation(
                        rstd[:, 0:1], mv[:, 1:2], AF.Ln, bias=c_eps[:, :]),
                        rd=["mv"], wr=["rstd0"])
                    p.step("act", lambda s: s.activation(
                        rstd[:, 1:2], rstd[:, 0:1], AF.Exp, scale=-0.5),
                        rd=["rstd0"], wr=["rstd"])
                    p.step("dve", lambda v: v.tensor_scalar(
                        xnb[:], xt[:], mv[:, 0:1], rstd[:, 1:2],
                        ALU.subtract, ALU.mult),
                        rd=["xtb", "mv", "rstd"], wr=["xnb"], tag=f"nrm{ttt}")
                    for bg in range(4):
                        for q in range(4):
                            k = bg * 4 + q
                            p.step("pe", lambda t, k=k, bg=bg, q=q: t.matmul(
                                PB[bg][:, q * 128:(q + 1) * 128],
                                xnb[:, k * 128:(k + 1) * 128], c_idb[:, :],
                                start=True, stop=True),
                                rd=["xnb"], wr=[f"pb{bg}"], tag=f"trmm{ttt}.{bg}")
                        p.step("act", lambda s, bg=bg, ttt=ttt: s.activation(
                            lnv[:, bg * 4:(bg + 1) * 4, ttt * 128:(ttt + 1) * 128],
                            PB[bg][:, 0:512].rearrange("p (k n) -> p k n", k=4),
                            AF.Copy),
                            rd=[f"pb{bg}"], wr=[f"lnfm.{bg}.{ttt}"],
                            tag=f"trev{ttt}.{bg}")
                for ttt in range(8):
                    for k in range(16):
                        p.step("pe", lambda t, k=k, ttt=ttt: t.matmul(
                            PB[4][:, 0:8],
                            lnv[:, k, ttt * 128:(ttt + 1) * 128],
                            c_gwT[:, k * 8:(k + 1) * 8],
                            start=(k == 0), stop=(k == 15)),
                            rd=[f"lnfm.{k // 4}.{ttt}"],
                            wr=["pb4"] if k in (0, 15) else [],
                            tag=f"scmm{ttt}.{k}")
                    p.step("dve", lambda v: v.tensor_add(
                        s_sc[:], PB[4][:, 0:8], c_gbias[:]),
                        rd=["pb4"], wr=["s_sc"])
                    p.step("dve", lambda v: v.reduce_max(s_m1[:], s_sc[:], axis=AX),
                           rd=["s_sc"], wr=["s_m1"], wb=True)
                    p.step("dve", lambda v: v.tensor_scalar(
                        s_e1[:], s_sc[:], s_m1[:, :], None, ALU.is_equal),
                        rd=["s_sc", "s_m1"], wr=["s_e1"])
                    p.step("dve", lambda v: v.scalar_tensor_tensor(
                        s_mk[:], s_e1[:], -1e30, s_sc[:], ALU.mult, ALU.add),
                        rd=["s_e1", "s_sc"], wr=["s_mk"])
                    p.step("dve", lambda v: v.reduce_max(s_m2[:], s_mk[:], axis=AX),
                           rd=["s_mk"], wr=["s_m2"], wb=True)
                    p.step("dve", lambda v: v.tensor_scalar(
                        s_e2[:], s_mk[:], s_m2[:, :], None, ALU.is_equal),
                        rd=["s_mk", "s_m2"], wr=["s_e2"])
                    p.step("dve", lambda v: v.tensor_sub(s_m1[:], s_m1[:], s_m2[:]),
                           rd=["s_m2"], wr=["s_m1"])
                    p.step("act", lambda s: s.activation(
                        s_g1[:], s_m1[:, :], AF.Sigmoid), rd=["s_m1"], wr=["s_g1"])
                    p.step("act", lambda s: s.activation(
                        s_g2[:], s_m1[:, :], AF.Sigmoid, scale=-1.0),
                        rd=["s_m1"], wr=["s_g2"])
                    p.step("dve", lambda v: v.tensor_scalar(
                        s_t1[:], s_e1[:], s_g1[:, :], None, ALU.mult),
                        rd=["s_e1", "s_g1"], wr=["s_t1"])
                    p.step("dve", lambda v, ttt=ttt: v.scalar_tensor_tensor(
                        combb[:, ttt * 8:(ttt + 1) * 8], s_e2[:], s_g2[:, :], s_t1[:],
                        ALU.mult, ALU.add),
                        rd=["s_e2", "s_g2", "s_t1"], wr=[f"combb.{ttt}"], tag="comb")
                    pbx = 5 + ttt // 4
                    p.step("pe", lambda t, ttt=ttt, pbx=pbx: t.matmul(
                        PB[pbx][0:8, (ttt % 4) * 128:(ttt % 4 + 1) * 128],
                        combb[:, ttt * 8:(ttt + 1) * 8], c_idb[:, :],
                        start=True, stop=True),
                        rd=[f"combb.{ttt}"], wr=[f"pb{pbx}"], tag="cmbT")
                    p.step("dve", lambda v, ttt=ttt, pbx=pbx: v.tensor_copy(
                        comb_fm[:, ttt * 128:(ttt + 1) * 128],
                        PB[pbx][0:8, (ttt % 4) * 128:(ttt % 4 + 1) * 128]),
                        rd=[f"pb{pbx}"], wr=[f"comb_fm.{ttt}"], tag="cfm")
                cfm_all = [f"comb_fm.{i}" for i in range(8)]
                p.barrier()

                for k in range(16):
                    w1_pair(0, k)

                for e in range(NE):
                    p.step("pe", lambda t, e=e: t.matmul(
                        PB[4][:, 0:512], c_selb[0:8, e * 128:(e + 1) * 128],
                        comb_fm[0:8, 0:512], start=True, stop=True),
                        rd=cfm_all[:4], wr=["pb4"], tag=f"cbmm{e}.0")
                    p.step("pe", lambda t, e=e: t.matmul(
                        PB[5][:, 0:512], c_selb[0:8, e * 128:(e + 1) * 128],
                        comb_fm[0:8, 512:1024], start=True, stop=True),
                        rd=cfm_all[4:], wr=["pb5"], tag=f"cbmm{e}.1")
                    p.step("dve", lambda v: v.tensor_copy(
                        comb_bc[:, 0:512], PB[4][:, 0:512]),
                        rd=["pb4"], wr=["cbc0"], tag=f"cbev{e}.0")
                    p.step("dve", lambda v: v.tensor_copy(
                        comb_bc[:, 512:1024], PB[5][:, 0:512]),
                        rd=["pb5"], wr=["cbc1"], tag=f"cbev{e}.1")
                    h_done = None
                    for hc in range(11):
                        for th in range(2):
                            b = (hc * 2 + th) % 2
                            for k in range(16):
                                s = p.step("pe", lambda t, k=k, hc=hc, th=th, b=b:
                                           t.matmul(
                                               PB[b][:, 0:512],
                                               w1sb[:, k * HF + hc * 128:
                                                    k * HF + (hc + 1) * 128],
                                               lnv[:, k, th * 512:(th + 1) * 512],
                                               start=(k == 0), stop=(k == 15)),
                                           rd=["w1rdy"] if k == 0 else [],
                                           wr=[f"pb{b}"] if k in (0, 15) else [],
                                           tag=f"hmm{e}.{hc}.{th}.{k}")
                                if hc == 10 and th == 1 and k == 15:
                                    h_done = s
                            p.step("act", lambda s, e=e, hc=hc, th=th, b=b:
                                   s.activation(
                                       hsv[:, hc, th * 512:(th + 1) * 512],
                                       PB[b][:, 0:512], AF.Silu,
                                       bias=c_b1T[:, hc * 8 + e:hc * 8 + e + 1]),
                                   rd=[f"pb{b}"],
                                   wr=[f"hsi.{hc}.{th}", f"hst.{hc}.{th}"],
                                   tag=f"sil{e}.{hc}.{th}")
                            p.step("dve", lambda v, hc=hc, th=th: v.tensor_mul(
                                hsv[:, hc, th * 512:(th + 1) * 512],
                                hsv[:, hc, th * 512:(th + 1) * 512],
                                comb_bc[:, th * 512:(th + 1) * 512]),
                                rd=[f"hsi.{hc}.{th}", f"cbc{th}"],
                                wr=[f"hst.{hc}.{th}"] +
                                   (["hrdy"] if hc == 10 and th == 1 else []),
                                tag=f"hm{e}.{hc}.{th}")
                    w2_pair(e, 0)
                    w2_pair(e, 1)
                    for ot in range(16):
                        wi = ot % 2
                        for th in range(2):
                            b = 2 + (ot * 2 + th) % 2
                            if e == 0:
                                p.step("pe", lambda t, ot=ot, th=th, b=b, wi=wi:
                                       t.matmul(
                                           PB[b][:, 0:512],
                                           c_b2b[0:8, ot * 128:(ot + 1) * 128],
                                           comb_fm[0:8, th * 512:(th + 1) * 512],
                                           start=True, stop=False),
                                       rd=["hrdy", f"w2sb{wi}"],
                                       wr=[f"pb{b}"], tag=f"ymm{e}.{ot}.{th}.s")
                            for hc in range(11):
                                rdl = []
                                if hc == 0 and e > 0:
                                    rdl += ["hrdy", f"w2sb{wi}"]
                                if hc == 10:
                                    rdl.append(f"w2sb{wi}")
                                if ot == 15:
                                    rdl.append(f"hst.{hc}.{th}")
                                p.step("pe", lambda t, hc=hc, th=th, b=b, wi=wi, e=e:
                                       t.matmul(
                                           PB[b][:, 0:512],
                                           w2sb[wi][:, hc * 128:(hc + 1) * 128],
                                           hsv[:, hc, th * 512:(th + 1) * 512],
                                           start=(hc == 0 and e > 0),
                                           stop=(hc == 10)),
                                       rd=rdl,
                                       wr=[f"pb{b}"] if hc in (0, 10) else [],
                                       tag=f"ymm{e}.{ot}.{th}.{hc}")
                            ysl = y_acc[:, ot * 1024 + th * 512:
                                        ot * 1024 + (th + 1) * 512]
                            if e == 0:
                                p.step("dve", lambda v, ysl=ysl, b=b: v.tensor_copy(
                                    ysl, PB[b][:, 0:512]),
                                    rd=[f"pb{b}"], wr=[f"yac.{ot}.{th}"],
                                    tag=f"yev{e}.{ot}.{th}")
                            else:
                                p.step("dve", lambda v, ysl=ysl, b=b: v.tensor_add(
                                    ysl, ysl, PB[b][:, 0:512]),
                                    rd=[f"pb{b}", f"yac.{ot}.{th}"],
                                    wr=[f"yac.{ot}.{th}"],
                                    tag=f"yev{e}.{ot}.{th}")
                        if ot + 2 < 16:
                            w2_pair(e, ot + 2)
                        if e + 1 < NE:
                            w1_pair(e + 1, ot,
                                    [h_done] if ot == 0 else [])
                for tv in range(8):
                    r0 = tv * 128
                    p.step("scq", lambda s, r0=r0: s.dma_start(
                        out=xt[:], in_=src_x[r0:r0 + 128, :]),
                        rd=[f"{src_res}.{r0}.0.0", f"{src_res}.{r0}.0.1",
                            f"{src_res}.{r0}.1.0", f"{src_res}.{r0}.1.1"],
                        wr=["xtb"], tag=f"ldxr{tv}", ch=f"xt{call}")
                    bks = [4, 5, 6, 7] if tv % 2 == 0 else [0, 1, 2, 3]
                    for g in range(4):
                        b = bks[g]
                        for oq in range(4):
                            ot = g * 4 + oq
                            p.step("pe", lambda t, ot=ot, tv=tv, b=b, oq=oq:
                                   t.transpose(
                                       PB[b][:, oq * 128:(oq + 1) * 128],
                                       y_acc[:, ot * 1024 + tv * 128:
                                             ot * 1024 + tv * 128 + 128],
                                       c_idf[:, :]),
                                   rd=[f"yac.{ot}.{tv // 4}"], wr=[f"pb{b}"],
                                   tag=f"ytr{tv}.{ot}")
                        p.step("dve", lambda v, g=g, b=b: v.scalar_tensor_tensor(
                            xt[:, g * 512:(g + 1) * 512], PB[b][:, 0:512], 0.5,
                            xt[:, g * 512:(g + 1) * 512], ALU.mult, ALU.add),
                            rd=[f"pb{b}", "xtb"], wr=["xtb"], tag=f"yfin{tv}.{g}")
                    stx = p.step("gpq", lambda gq, r0=r0: gq.dma_start(
                        out=dst_x[r0:r0 + 128, :], in_=xt[:]),
                        rd=["xtb"],
                        wr=[f"{dst_res}.{r0}.{a}.{bb}"
                            for a in range(2) for bb in range(2)],
                        tag=f"stx{tv}", ch=f"st_{dst_res}")
                    if dst_x is y_out:
                        final_deps.append(stx)
                p.barrier()
            return

        f32w_attn = True
        emit_ffn(1, x_in, x1d if PHASES > 1 else y_out, "xin", "x1")
        if PHASES > 1:
            p.fence("st_x1")
            emit_attn(x1d, x2d if PHASES > 2 else y_out, "x1", "x2")
        if PHASES > 2:
            p.fence("st_x2", "lnld", "ofl")
            emit_ffn(2, x2d, y_out, "x2", "y")

        p.final_wait(final_deps)
    return nc


def kernel(**inputs):
    in_maps = [host_inputs(inputs, c) for c in range(8)]
    nc = build_program()
    res = run_bass_kernel_spmd(nc, in_maps, core_ids=list(range(8)))
    out = np.zeros((4, 2048, D), dtype=np.float32)
    for c in range(8):
        b, half = c // 2, c % 2
        out[b, half * T:(half + 1) * T, :] = res.results[c]["y_out"]
    return out


# version 03 (final)
# speedup vs baseline: 1.2602x; 1.0384x over previous
import numpy as np
import ml_dtypes
import concourse.bass as bass
import concourse.mybir as mybir
from concourse.bass_utils import run_bass_kernel_spmd
from contextlib import ExitStack

F32 = mybir.dt.float32
BF16 = mybir.dt.bfloat16
AF = mybir.ActivationFunctionType
ALU = mybir.AluOpType
AX = mybir.AxisListType.X
PAIRS = [[0, 1], [2, 3], [4, 5], [6, 7]]

T = 1024
D = 2048
HF = 1408
NE = 8

PHASES = 99
A6STOP = False
CUT = None
DBG = False

ENGINES = ["pe", "act", "dve", "pool", "syq", "scq", "gpq"]
DMA_Q = {"syq", "scq", "gpq"}
ENG_OF = {"pe": "tensor", "act": "scalar", "dve": "vector", "pool": "gpsimd",
          "syq": "sync", "scq": "scalar", "gpq": "gpsimd"}


class Step:
    __slots__ = ["eng", "fn", "deps", "inc", "count", "block", "tag", "idx", "wb",
                 "tag_ch"]

    def __init__(self, eng, fn, deps, inc, block, tag, idx, wb=False):
        self.eng, self.fn, self.deps, self.inc = eng, fn, deps, inc
        self.count = None
        self.block = block
        self.tag = tag
        self.idx = idx
        self.wb = wb


class Prog:
    def __init__(self, nc, ctx):
        self.nc = nc
        self.ctx = ctx
        self.sem = {e: ctx.enter_context(nc.semaphore(f"sem_{e}")) for e in ENGINES}
        self.steps = []
        self.cnt = {e: 0 for e in ENGINES}
        self.blk_idx = 0
        self.n_idx = 0
        self.res_w = {}
        self.res_r = {}
        self.chan_sem = {}
        self.chan_cnt = {}
        self.pending_fence = []

    def fence(self, *chs):
        self.pending_fence = [("ch", ch, self.chan_cnt[ch])
                              for ch in chs if ch in self.chan_cnt]

    def _chan(self, name):
        if name not in self.chan_sem:
            self.chan_sem[name] = self.ctx.enter_context(
                self.nc.semaphore(f"ch_{name}"))
            self.chan_cnt[name] = 0
        return self.chan_sem[name]

    def step(self, eng, fn, rd=(), wr=(), extra=(), tag="", wb=False, ch=None):
        raw = []
        for n in rd:
            w = self.res_w.get(n)
            if w is not None:
                raw.append(w)
        for n in wr:
            w = self.res_w.get(n)
            if w is not None:
                raw.append(w)
            raw.extend(self.res_r.get(n, {}).values())
        raw.extend(d for d in extra if d is not None)
        raw.extend(self.pending_fence)
        deps = []
        seen = set()
        for d in raw:
            if isinstance(d, tuple):
                if d[1] in self.chan_sem and d not in seen:
                    seen.add(d)
                    deps.append(d)
            elif d.eng in DMA_Q:
                key = ("ch", d.tag_ch)
                if key not in seen:
                    seen.add(key)
                    deps.append(("ch", d.tag_ch, self.chan_cnt[d.tag_ch]))
            else:
                if id(d) not in seen:
                    seen.add(id(d))
                    deps.append(d)
        if eng in DMA_Q:
            assert ch is not None, f"DMA step needs ch=: {tag}"
            self._chan(ch)
            self.chan_cnt[ch] += 16
            s = Step(eng, fn, deps, 16, self.blk_idx, tag, self.n_idx, wb)
            s.tag_ch = ch
            s.count = self.chan_cnt[ch]
        else:
            s = Step(eng, fn, deps, 1, self.blk_idx, tag, self.n_idx, wb)
            s.tag_ch = None
        self.n_idx += 1
        self.steps.append(s)
        for n in rd:
            self.res_r.setdefault(n, {})[eng] = s
        for n in wr:
            self.res_w[n] = s
            self.res_r[n] = {}
        return s

    def barrier(self):
        if not self.steps:
            self.blk_idx += 1
            return
        steps, self.steps = self.steps, []
        cnt = dict(self.cnt)
        for s in steps:
            if s.eng in DMA_Q:
                continue
            cnt[s.eng] += s.inc
            s.count = cnt[s.eng]
        by_body = {}
        for s in steps:
            by_body.setdefault(ENG_OF[s.eng], []).append(s)
        nc = self.nc
        with nc.Block() as blk:
            for body_name, seq in by_body.items():
                seq.sort(key=lambda s: s.idx)

                def make_body(seq):
                    def body(engobj):
                        hw = {}
                        for s in seq:
                            for d in s.deps:
                                if isinstance(d, tuple):
                                    _, chn, val = d
                                    if val <= 0 or hw.get(("c", chn), 0) >= val:
                                        continue
                                    engobj.wait_ge(self.chan_sem[chn], val)
                                    hw[("c", chn)] = val
                                    continue
                                if d.count is None:
                                    raise RuntimeError(
                                        f"dep not emitted: {d.tag} <- {s.tag}")
                                if (not d.wb) and (
                                        d.block < s.block or
                                        (d.eng == s.eng and d.eng == "pe")):
                                    continue
                                if hw.get(d.eng, 0) >= d.count:
                                    continue
                                engobj.wait_ge(self.sem[d.eng], d.count)
                                hw[d.eng] = d.count
                            instr = s.fn(engobj)
                            if instr is None:
                                raise RuntimeError(f"step returned None: {s.tag}")
                            if s.eng in DMA_Q:
                                instr.then_inc(self.chan_sem[s.tag_ch], 16)
                            else:
                                instr.then_inc(self.sem[s.eng], s.inc)
                    return body
                getattr(blk, body_name)(make_body(seq))
        self.cnt = cnt
        self.blk_idx += 1
        self.pending_fence = []

    def final_wait(self, deps):
        assert not self.steps
        nc = self.nc
        chans = {}
        for d in deps:
            assert d.eng in DMA_Q
            chans[d.tag_ch] = self.chan_cnt[d.tag_ch]
        with nc.Block() as blk:
            @blk.sync
            def _(sy):
                for chn, val in chans.items():
                    sy.wait_ge(self.chan_sem[chn], val)
        self.blk_idx += 1


def _poff_vec(half):
    offs = np.zeros(80, dtype=np.int64)
    for t3 in range(3):
        for r in range(2):
            offs[t3 * 2 + r] = r * 6144 * 1024 + (t3 * 2048 + half * 1024) * 1024
    for r in range(2):
        offs[6 + r] = r * 1024 * 2048 + half * 1024
    return offs.astype(np.int32).reshape(1, 80)


def host_inputs(inputs, core):
    b, half = core // 2, core % 2
    x = np.ascontiguousarray(
        np.asarray(inputs["L0_attn_x"], dtype=np.float32)[b, half * T:(half + 1) * T, :])
    eye_b = np.eye(128, dtype=ml_dtypes.bfloat16)
    eye_f = np.eye(128, dtype=np.float32)
    jj, ii = np.meshgrid(np.arange(128), np.arange(128), indexing="ij")
    maskS = np.where(ii > jj, 0.0, -1e30).astype(np.float32)
    maskI = np.where(ii >= jj, 0.0, -1e30).astype(np.float32)
    selb = np.zeros((8, 8 * 128), dtype=ml_dtypes.bfloat16)
    for e in range(8):
        selb[e, e * 128:(e + 1) * 128] = 1.0
    f32 = lambda a: np.asarray(a, dtype=np.float32)
    cont = np.ascontiguousarray
    return dict(
        x_in=x,
        wqkvT=cont(f32(inputs["L0_attn_W_qkv"]).T),
        conv_w=cont(f32(inputs["L0_attn_conv_w"]).reshape(48, 128, 4)
                    .transpose(1, 0, 2).reshape(128, 192)),
        wabT=cont(f32(inputs["L0_attn_W_ab"]).T.reshape(16, 128, 32)
                  .transpose(1, 0, 2).reshape(128, 512)),
        b_ab=cont(np.broadcast_to(f32(inputs["L0_attn_b_ab"]).reshape(1, 32),
                                  (128, 32))),
        wgT=cont(f32(inputs["L0_attn_W_g"]).T),
        woT=cont(f32(inputs["L0_attn_W_o"]).T),
        gate_wT=cont(f32(inputs["L0_ffn_gate_w"]).T.reshape(16, 128, 8)
                     .transpose(1, 0, 2).reshape(128, 128)),
        gate_b=cont(np.broadcast_to(f32(inputs["L0_ffn_gate_b"]).reshape(1, 8),
                                    (128, 8))),
        gate_bias=cont(np.broadcast_to(f32(inputs["L0_ffn_bias"]).reshape(1, 8),
                                       (128, 8))),
        w1T=cont(f32(inputs["L0_ffn_w1"]).transpose(0, 2, 1)),
        b1T=cont(f32(inputs["L0_ffn_b1"]).T.reshape(11, 128, 8)
                 .transpose(1, 0, 2).reshape(128, 88)),
        w2R=cont(f32(inputs["L0_ffn_w2"]).transpose(0, 2, 1)
                 .reshape(NE, 11, 128, 16, 128).transpose(0, 3, 2, 1, 4)
                 .reshape(NE, 16, 128, 11 * 128)),
        b2=cont(f32(inputs["L0_ffn_b2"])),
        idb=eye_b, idf=eye_f, maskS=maskS, maskI=maskI,
        onesb=np.ones((8, 128), dtype=ml_dtypes.bfloat16),
        onescolb=np.ones((128, 1), dtype=ml_dtypes.bfloat16),
        poff=_poff_vec(half),
        onesf=np.ones((1, 128), dtype=np.float32),
        ones8=np.ones((8, 128), dtype=np.float32),
        selb=selb,
        m3=np.full((128, 1), float(half), dtype=np.float32),
        epscol=np.full((128, 1), 1e-5, dtype=np.float32),
        m3w=np.full((128, 128), float(half), dtype=np.float32),
    )


def build_program():
    nc = bass.Bass("TRN2", target_bir_lowering=False)

    def din(name, shape, dt=F32):
        return nc.dram_tensor(name, list(shape), dt, kind="ExternalInput").ap()

    x_in = din("x_in", [T, D])
    wqkvT = din("wqkvT", [D, 3 * D])
    conv_w = din("conv_w", [128, 192])
    wabT = din("wabT", [128, 512])
    b_ab = din("b_ab", [128, 32])
    wgT = din("wgT", [D, D])
    woT = din("woT", [D, D])
    gate_wT = din("gate_wT", [128, 128])
    gate_b = din("gate_b", [128, NE])
    gate_bias = din("gate_bias", [128, NE])
    w1T = din("w1T", [NE, D, HF])
    b1T = din("b1T", [128, 88])
    w2R = din("w2R", [NE, 16, 128, 11 * 128])
    b2 = din("b2", [NE, D])
    idb_i = din("idb", [128, 128], BF16)
    idf_i = din("idf", [128, 128])
    maskS_i = din("maskS", [128, 128])
    maskI_i = din("maskI", [128, 128])
    onesb_i = din("onesb", [8, 128], BF16)
    onescolb_i = din("onescolb", [128, 1], BF16)
    poff_i = din("poff", [1, 80], mybir.dt.int32)
    onesf_i = din("onesf", [1, 128])
    ones8_i = din("ones8", [8, 128])
    selb_i = din("selb", [8, 8 * 128], BF16)
    m3_i = din("m3", [128, 1])
    eps_i = din("epscol", [128, 1])
    m3w_i = din("m3w", [128, 128])

    y_out = nc.dram_tensor("y_out", [T, D], F32, kind="ExternalOutput").ap()

    def dint(name, shape, dt=F32):
        return nc.dram_tensor(name, list(shape), dt, kind="Internal").ap()

    x1d = dint("x1d", [T, D])
    x2d = dint("x2d", [T, D])
    w1b = dint("w1b", [NE, D, HF], BF16)
    w2b = dint("w2b", [NE, HF, D], BF16)
    hdram = dint("hdram", [NE, 128, 11 * 512], BF16)
    zpre = dint("zpre", [3 * D, T], BF16)
    lnfm2d = dint("lnfm2d", [128, 16 * 1024], BF16)
    qkv_out_t = nc.dram_tensor("qkv_out", [3 * D, T], BF16, kind="Internal")
    qkv_gat_t = nc.dram_tensor("qkv_gat", [2, 3 * D, T], BF16, kind="Internal")
    qkv_out = qkv_out_t.ap()
    qkv_gat = qkv_gat_t.ap()
    ab_out = dint("ab_out", [T, 32])
    ab_gat = dint("ab_gat", [2, T, 32])
    ztail_in = dint("ztail_in", [128, 144], BF16)
    ztail_gat = dint("ztail_gat", [2, 128, 144], BF16)
    o_out = dint("o_out", [T, D], BF16)
    o_gat_t = nc.dram_tensor("o_gat", [2, T, D], BF16, kind="Internal")
    o_gat = o_gat_t.ap()

    final_deps = []

    ctx = ExitStack()
    with ctx:
        p = Prog(nc, ctx)
        sb = lambda name, shape, dt=F32: ctx.enter_context(
            nc.sbuf_tensor(name, list(shape), dt))
        c_idb = sb("c_idb", [128, 128], BF16)
        c_idf = sb("c_idf", [128, 128])
        c_maskS = sb("c_maskS", [128, 128])
        c_maskI = sb("c_maskI", [128, 128])
        c_onesb = sb("c_onesb", [8, 128], BF16)
        c_onescolb = sb("c_onescolb", [128, 1], BF16)
        c_poff = sb("c_poff", [1, 80], mybir.dt.int32)
        c_onesf = sb("c_onesf", [1, 128])
        c_ones8 = sb("c_ones8", [8, 128])
        c_selb = sb("c_selb", [8, 8 * 128], BF16)
        c_m3 = sb("c_m3", [128, 1])
        c_eps = sb("c_eps", [128, 1])
        c_m3w = sb("c_m3w", [128, 128])
        c_cw = sb("c_cw", [128, 48 * 4])
        c_b1T = sb("c_b1T", [128, 11 * 8])
        c_b2b = sb("c_b2b", [8, D], BF16)
        c_gwT = sb("c_gwT", [128, 16 * 8], BF16)
        c_wabT = sb("c_wabT", [128, 16 * 32], BF16)
        c_gbias = sb("c_gbias", [128, 8])
        c_abbias = sb("c_abbias", [128, 32])
        c_scr = sb("c_scr", [128, 1024])

        PB = [ctx.enter_context(nc.psum_tensor(f"pb{i}", [128, 512], F32))
              for i in range(8)]

        for dst, src in [
            (c_idb[:], idb_i), (c_idf[:], idf_i), (c_maskS[:], maskS_i),
            (c_maskI[:], maskI_i), (c_onesb[:], onesb_i), (c_onesf[:], onesf_i),
            (c_ones8[:], ones8_i), (c_selb[:], selb_i), (c_m3[:], m3_i),
            (c_eps[:], eps_i), (c_onescolb[:], onescolb_i), (c_poff[:], poff_i),
            (c_m3w[:], m3w_i),
            (c_cw[:], conv_w),
            (c_b1T[:], b1T),
        ]:
            p.step("syq", lambda sy, d=dst, s=src: sy.dma_start(out=d, in_=s), tag="cload", ch="c")
        if CUT == "c1":
            p.barrier()
            p.final_wait(final_deps)
            return nc
        for h2 in range(2):
            p.step("syq", lambda sy, h2=h2: sy.dma_start(
                out=c_scr[0:8, 0:1024], in_=b2[:, h2 * 1024:(h2 + 1) * 1024]),
                wr=["scr"], tag="b2", ch="c")
            p.step("dve", lambda v, h2=h2: v.tensor_copy(
                c_b2b[:, h2 * 1024:(h2 + 1) * 1024], c_scr[0:8, 0:1024]), rd=["scr"])
        p.barrier()
        if CUT == "c2":
            p.final_wait(final_deps)
            return nc
        p.step("syq", lambda sy: sy.dma_start(
            out=c_scr[:, 0:128], in_=gate_wT),
            wr=["scr1"], tag="gw", ch="c")
        p.step("syq", lambda sy: sy.dma_start(
            out=c_scr[:, 128:640], in_=wabT),
            wr=["scr2"], tag="ab", ch="c")
        p.step("syq", lambda sy: sy.dma_start(out=c_gbias[:], in_=gate_b),
               wr=["scr3"], ch="c")
        p.step("syq", lambda sy: sy.dma_start(out=c_scr[:, 640:648], in_=gate_bias),
               wr=["scr4"], ch="c")
        p.step("syq", lambda sy: sy.dma_start(out=c_abbias[:], in_=b_ab),
               wr=["scr5"], ch="c")
        p.step("dve", lambda v: v.tensor_copy(c_gwT[:], c_scr[:, 0:128]), rd=["scr1"])
        p.step("dve", lambda v: v.tensor_copy(c_wabT[:], c_scr[:, 128:640]), rd=["scr2"])
        p.step("dve", lambda v: v.tensor_add(
            c_gbias[:], c_gbias[:], c_scr[:, 640:648]),
            rd=["scr4"], wr=["scr3"], tag="gbsum")
        p.barrier()

        if CUT == "consts":
            p.final_wait(final_deps)
            return nc

        def emit_ffn(call, src_x, dst_x, src_res, dst_res):
            f32w = (call == 1)
            with ExitStack() as fs:
                fsb = lambda name, shape, dt=F32: fs.enter_context(
                    nc.sbuf_tensor(f"f{call}{name}", list(shape), dt))
                lnfm = fsb("lnfm", [128, 16 * 512], BF16)
                lnv = lnfm[:].rearrange("p (k n) -> p k n", k=16)
                xt = [fsb(f"xt{i}", [128, D]) for i in range(2)]
                st24 = fsb("st24", [128, 24])
                mv = fsb("mv", [128, 2])
                rstd = fsb("rstd", [128, 2])
                xnb = fsb("xnb", [128, D], BF16)
                s_sc = fsb("s_sc", [128, 8])
                s_e1 = fsb("s_e1", [128, 8])
                s_mk = fsb("s_mk", [128, 8])
                s_e2 = fsb("s_e2", [128, 8])
                s_m1 = fsb("s_m1", [128, 1])
                s_m2 = fsb("s_m2", [128, 1])
                s_g1 = fsb("s_g1", [128, 1])
                s_g2 = fsb("s_g2", [128, 1])
                s_t1 = fsb("s_t1", [128, 8])
                combb = fsb("combb", [128, 32], BF16)
                comb_fm = fsb("comb_fm", [8, 512], BF16)
                comb_bc = [fsb(f"cbc{i}", [128, 512], BF16) for i in range(2)]
                w1sb = [fsb(f"w1sb{i}", [128, 16 * HF], BF16) for i in range(2)]
                wstage = fsb("wstage", [128, 2816]) if f32w else None
                hstage = fsb("hstage", [128, 11 * 512], BF16)
                hbuf = [fsb(f"hbuf{i}", [128, 11 * 512], BF16) for i in range(2)]
                w2sb = [fsb(f"w2sb{i}", [128, 1024], BF16) for i in range(2)]
                ystage = fsb("ystage", [128, 4 * 512])
                xr = [fsb(f"xr{i}", [128, 512]) for i in range(2)]

                for tt in range(2):
                    tok0 = tt * 512
                    for ttt in range(4):
                        r0 = tok0 + ttt * 128
                        xb = f"xt{ttt % 2}"
                        p.step("scq", lambda s, r0=r0, ttt=ttt: s.dma_start(
                            out=xt[ttt % 2][:], in_=src_x[r0:r0 + 128, :]),
                            rd=[f"{src_res}.{r0}.0.0", f"{src_res}.{r0}.0.1",
                                f"{src_res}.{r0}.1.0", f"{src_res}.{r0}.1.1"],
                            wr=[xb], tag=f"ldx{tt}.{ttt}", ch=f"xt{ttt % 2}")
                        for q in range(4):
                            p.step("dve", lambda v, q=q, ttt=ttt: v.bn_stats(
                                st24[:, q * 6:(q + 1) * 6],
                                xt[ttt % 2][:, q * 512:(q + 1) * 512]),
                                rd=[xb], wr=[f"st24.{q}"], tag=f"bn{ttt}.{q}", wb=True)
                        p.step("dve", lambda v: v.bn_aggr(mv[:], st24[:]),
                               rd=["st24.0", "st24.1", "st24.2", "st24.3"],
                               wr=["mv"], tag="aggr", wb=True)
                        p.step("act", lambda s: s.activation(
                            rstd[:, 0:1], mv[:, 1:2], AF.Ln, bias=c_eps[:, :]),
                            rd=["mv"], wr=["rstd0"])
                        p.step("act", lambda s: s.activation(
                            rstd[:, 1:2], rstd[:, 0:1], AF.Exp, scale=-0.5),
                            rd=["rstd0"], wr=["rstd"])
                        p.step("dve", lambda v, ttt=ttt: v.tensor_scalar(
                            xnb[:], xt[ttt % 2][:], mv[:, 0:1], rstd[:, 1:2],
                            ALU.subtract, ALU.mult),
                            rd=[xb, "mv", "rstd"], wr=["xnb"], tag=f"nrm{ttt}")
                        for bg in range(4):
                            for q in range(4):
                                k = bg * 4 + q
                                p.step("pe", lambda t, k=k, bg=bg, q=q: t.matmul(
                                    PB[bg][:, q * 128:(q + 1) * 128],
                                    xnb[:, k * 128:(k + 1) * 128], c_idb[:, :],
                                    start=True, stop=True),
                                    rd=["xnb"], wr=[f"pb{bg}"], tag=f"trmm{ttt}.{bg}")
                            p.step("act", lambda s, bg=bg, ttt=ttt: s.activation(
                                lnv[:, bg * 4:(bg + 1) * 4, ttt * 128:(ttt + 1) * 128],
                                PB[bg][:, 0:512].rearrange("p (k n) -> p k n", k=4),
                                AF.Copy),
                                rd=[f"pb{bg}"], wr=[f"lnfm.{bg}.{ttt}"],
                                tag=f"trev{ttt}.{bg}")
                    if CUT == "ln1" and tt == 0:
                        p.barrier()
                        return
                    if CUT == "ln":
                        p.barrier()
                        return
                    for ttt in range(4):
                        for k in range(16):
                            p.step("pe", lambda t, k=k, ttt=ttt: t.matmul(
                                PB[4][:, 0:8],
                                lnv[:, k, ttt * 128:(ttt + 1) * 128],
                                c_gwT[:, k * 8:(k + 1) * 8],
                                start=(k == 0), stop=(k == 15)),
                                rd=[f"lnfm.{k // 4}.{ttt}"],
                                wr=["pb4"] if k in (0, 15) else [],
                                tag=f"scmm{ttt}.{k}")
                        p.step("dve", lambda v: v.tensor_add(
                            s_sc[:], PB[4][:, 0:8], c_gbias[:]),
                            rd=["pb4"], wr=["s_sc"])
                        p.step("dve", lambda v: v.reduce_max(s_m1[:], s_sc[:], axis=AX),
                               rd=["s_sc"], wr=["s_m1"], wb=True)
                        p.step("dve", lambda v: v.tensor_scalar(
                            s_e1[:], s_sc[:], s_m1[:, :], None, ALU.is_equal),
                            rd=["s_sc", "s_m1"], wr=["s_e1"])
                        p.step("dve", lambda v: v.scalar_tensor_tensor(
                            s_mk[:], s_e1[:], -1e30, s_sc[:], ALU.mult, ALU.add),
                            rd=["s_e1", "s_sc"], wr=["s_mk"])
                        p.step("dve", lambda v: v.reduce_max(s_m2[:], s_mk[:], axis=AX),
                               rd=["s_mk"], wr=["s_m2"], wb=True)
                        p.step("dve", lambda v: v.tensor_scalar(
                            s_e2[:], s_mk[:], s_m2[:, :], None, ALU.is_equal),
                            rd=["s_mk", "s_m2"], wr=["s_e2"])
                        p.step("dve", lambda v: v.tensor_sub(s_m1[:], s_m1[:], s_m2[:]),
                               rd=["s_m2"], wr=["s_m1"])
                        p.step("act", lambda s: s.activation(
                            s_g1[:], s_m1[:, :], AF.Sigmoid), rd=["s_m1"], wr=["s_g1"])
                        p.step("act", lambda s: s.activation(
                            s_g2[:], s_m1[:, :], AF.Sigmoid, scale=-1.0),
                            rd=["s_m1"], wr=["s_g2"])
                        p.step("dve", lambda v: v.tensor_scalar(
                            s_t1[:], s_e1[:], s_g1[:, :], None, ALU.mult),
                            rd=["s_e1", "s_g1"], wr=["s_t1"])
                        p.step("dve", lambda v, ttt=ttt: v.scalar_tensor_tensor(
                            combb[:, ttt * 8:(ttt + 1) * 8], s_e2[:], s_g2[:, :], s_t1[:],
                            ALU.mult, ALU.add),
                            rd=["s_e2", "s_g2", "s_t1"], wr=[f"combb.{ttt}"], tag="comb")
                        p.step("pe", lambda t, ttt=ttt: t.matmul(
                            PB[5][0:8, ttt * 128:(ttt + 1) * 128],
                            combb[:, ttt * 8:(ttt + 1) * 8], c_idb[:, :],
                            start=True, stop=True),
                            rd=[f"combb.{ttt}"], wr=["pb5"], tag="cmbT")
                        p.step("dve", lambda v, ttt=ttt: v.tensor_copy(
                            comb_fm[:, ttt * 128:(ttt + 1) * 128],
                            PB[5][0:8, ttt * 128:(ttt + 1) * 128]),
                            rd=["pb5"], wr=[f"comb_fm.{ttt}"], tag="cfm")
                    cfm_all = ["comb_fm.0", "comb_fm.1", "comb_fm.2", "comb_fm.3"]
                    if CUT == "scores":
                        p.barrier()
                        return
                    p.barrier()

                    for e in range(NE):
                        wbuf = f"w1sb{e % 2}"
                        for k in range(16):
                            if f32w and tt == 0:
                                sl = k % 2
                                p.step("syq", lambda sy, e=e, k=k, sl=sl: sy.dma_start(
                                    out=wstage[:, sl * 1408:(sl + 1) * 1408],
                                    in_=w1T[e, k * 128:(k + 1) * 128, :]),
                                    wr=[f"wst{sl}"], tag=f"w1ld{e}.{k}", ch=f"wst{sl}")
                                p.step("dve", lambda v, e=e, k=k, sl=sl: v.tensor_copy(
                                    w1sb[e % 2][:, k * HF:(k + 1) * HF],
                                    wstage[:, sl * 1408:(sl + 1) * 1408]),
                                    rd=[f"wst{sl}"],
                                    wr=([wbuf] if k in (0, 15) else []) + [f"{wbuf}.k{k}"],
                                    tag=f"w1cast{e}.{k}")
                                p.step("gpq", lambda g, e=e, k=k: g.dma_start(
                                    out=w1b[e, k * 128:(k + 1) * 128, :],
                                    in_=w1sb[e % 2][:, k * HF:(k + 1) * HF]),
                                    rd=[f"{wbuf}.k{k}"], wr=[f"w1b{e}"] if k == 15 else [],
                                    tag=f"w1wb{e}.{k}", ch="wb1")
                            else:
                                p.step("syq", lambda sy, e=e, k=k: sy.dma_start(
                                    out=w1sb[e % 2][:, k * HF:(k + 1) * HF],
                                    in_=w1b[e, k * 128:(k + 1) * 128, :]),
                                    rd=[f"w1b{e}"] if k == 0 else [],
                                    wr=[wbuf] if k in (0, 15) else [],
                                    tag=f"w1ld{e}.{k}", ch=f"w1{e % 2}")
                        p.step("pe", lambda t, e=e: t.matmul(
                            PB[4][:, 0:512], c_selb[0:8, e * 128:(e + 1) * 128],
                            comb_fm[0:8, :], start=True, stop=True),
                            rd=cfm_all, wr=["pb4"], tag=f"cbcmm{e}")
                        p.step("dve", lambda v, e=e: v.tensor_copy(
                            comb_bc[e % 2][:], PB[4][:, 0:512]),
                            rd=["pb4"], wr=[f"cbc{e % 2}"], tag=f"cbcev{e}")
                        for hc in range(11):
                            bank = 6 + (e * 11 + hc) % 2
                            for k in range(16):
                                p.step("pe", lambda t, k=k, hc=hc, e=e, bank=bank:
                                       t.matmul(PB[bank][:, 0:512],
                                                w1sb[e % 2][:, k * HF + hc * 128:
                                                            k * HF + (hc + 1) * 128],
                                                lnv[:, k, :],
                                                start=(k == 0), stop=(k == 15)),
                                       rd=[wbuf] if k in (0, 15) else [],
                                       wr=[f"pb{bank}"] if k in (0, 15) else [],
                                       tag=f"hmm{e}.{hc}.{k}")
                            p.step("act", lambda s, e=e, hc=hc, bank=bank: s.activation(
                                hstage[:, hc * 512:(hc + 1) * 512], PB[bank][:, 0:512],
                                AF.Silu, bias=c_b1T[:, hc * 8 + e: hc * 8 + e + 1]),
                                rd=[f"pb{bank}"], wr=[f"hstage.{hc}"], tag=f"sil{e}.{hc}")
                            p.step("dve", lambda v, hc=hc, e=e: v.tensor_mul(
                                hstage[:, hc * 512:(hc + 1) * 512],
                                hstage[:, hc * 512:(hc + 1) * 512], comb_bc[e % 2][:]),
                                rd=[f"cbc{e % 2}"], wr=[f"hstage.{hc}"],
                                tag=f"hm{e}.{hc}")
                        p.step("gpq", lambda g, e=e: g.dma_start(
                            out=hdram[e], in_=hstage[:]),
                            rd=[f"hstage.{hc2}" for hc2 in range(11)],
                            wr=[f"hdram{e}"], tag=f"hst{e}", ch="hd")
                    if CUT == "h":
                        p.barrier()
                        return
                    p.barrier()

                    for ocg in range(2):
                        ymm_last = None
                        for oc in range(8):
                            goc = ocg * 8 + oc
                            p.step("pe", lambda t, oc=oc, goc=goc: t.matmul(
                                PB[oc][:, 0:512], c_b2b[0:8, goc * 128:(goc + 1) * 128],
                                comb_fm[0:8, :], start=True, stop=False,
                                skip_group_check=True),
                                rd=cfm_all, wr=[f"pb{oc}"], tag=f"yb2.{ocg}.{oc}")
                        for e in range(NE):
                            p.step("gpq", lambda g, e=e: g.dma_start(
                                out=hbuf[e % 2][:], in_=hdram[e]),
                                rd=[f"hdram{e}"], wr=[f"hbuf{e % 2}"],
                                tag=f"hld{ocg}.{e}", ch=f"hb{e % 2}")
                            for hc in range(11):
                                wi = (e * 11 + hc) % 2
                                if f32w and tt == 0:
                                    p.step("syq", lambda sy, e=e, hc=hc, ocg=ocg, wi=wi:
                                           sy.dma_start(
                                               out=wstage[:, wi * 1408:wi * 1408 + 1024],
                                               in_=w2T[e, hc * 128:(hc + 1) * 128,
                                                       ocg * 1024:(ocg + 1) * 1024]),
                                           wr=[f"wst{wi}"], tag=f"w2ld{ocg}.{e}.{hc}",
                                           ch=f"wst{wi}")
                                    p.step("dve", lambda v, wi=wi: v.tensor_copy(
                                        w2sb[wi][:], wstage[:, wi * 1408:wi * 1408 + 1024]),
                                        rd=[f"wst{wi}"], wr=[f"w2sb{wi}"],
                                        tag=f"w2cast{ocg}.{e}.{hc}")
                                    p.step("gpq", lambda g, e=e, hc=hc, ocg=ocg, wi=wi:
                                           g.dma_start(
                                               out=w2b[e, hc * 128:(hc + 1) * 128,
                                                       ocg * 1024:(ocg + 1) * 1024],
                                               in_=w2sb[wi][:]),
                                           rd=[f"w2sb{wi}"], wr=[f"w2b{e}.{ocg}"],
                                           tag=f"w2wb{ocg}.{e}.{hc}", ch="wb2")
                                else:
                                    p.step("syq", lambda sy, e=e, hc=hc, ocg=ocg, wi=wi:
                                           sy.dma_start(
                                               out=w2sb[wi][:],
                                               in_=w2b[e, hc * 128:(hc + 1) * 128,
                                                       ocg * 1024:(ocg + 1) * 1024]),
                                           rd=[f"w2b{e}.{ocg}"] if hc == 0 else [],
                                           wr=[f"w2sb{wi}"],
                                           tag=f"w2ld{ocg}.{e}.{hc}", ch=f"w2{wi}")
                                for oc in range(8):
                                    s = p.step("pe", lambda t, oc=oc, hc=hc, wi=wi, e=e:
                                               t.matmul(PB[oc][:, 0:512],
                                                        w2sb[wi][:, oc * 128:(oc + 1) * 128],
                                                        hbuf[e % 2][:, hc * 512:(hc + 1) * 512],
                                                        start=False,
                                                        stop=(e == NE - 1 and hc == 10),
                                                        skip_group_check=True),
                                               rd=([f"w2sb{wi}", f"hbuf{e % 2}"]
                                                   if oc == 0 else []) +
                                               ([f"w2sb{wi}", f"hbuf{e % 2}"]
                                                if oc == 7 else []),
                                               wr=[f"pb{oc}"] if (e == NE - 1 and hc == 10)
                                               else [],
                                               tag=f"ymm{ocg}.{e}.{hc}.{oc}")
                                    ymm_last = s
                        for half in range(2):
                            for q4 in range(4):
                                oc = half * 4 + q4
                                p.step("act", lambda s, oc=oc, q4=q4: s.activation(
                                    ystage[:, q4 * 512:(q4 + 1) * 512], PB[oc][:, 0:512],
                                    AF.Copy, scale=0.5),
                                    rd=[f"pb{oc}"], extra=[ymm_last],
                                    wr=[f"ystage.{q4}"], tag=f"yev{ocg}.{oc}")
                            for ttt in range(4):
                                r0 = tok0 + ttt * 128
                                c0 = ocg * 1024 + half * 512
                                bank = ttt if half == 0 else 4 + ttt
                                xb2 = f"xr{ttt % 2}"
                                p.step("scq", lambda s, r0=r0, ttt=ttt, c0=c0: s.dma_start(
                                    out=xr[ttt % 2][:], in_=src_x[r0:r0 + 128, c0:c0 + 512]),
                                    rd=[f"{src_res}.{r0}.{ocg}.0",
                                        f"{src_res}.{r0}.{ocg}.1"],
                                    wr=[xb2], tag=f"ldxr{ocg}.{half}.{ttt}",
                                    ch=f"xr{ttt % 2}")
                                for q in range(4):
                                    p.step("pe", lambda t, q=q, ttt=ttt, bank=bank:
                                           t.transpose(PB[bank][:, q * 128:(q + 1) * 128],
                                                       ystage[:, q * 512 + ttt * 128:
                                                               q * 512 + (ttt + 1) * 128],
                                                       c_idf[:, :]),
                                           rd=[f"ystage.{q}"], wr=[f"pb{bank}"],
                                           tag=f"ytr{ocg}.{ttt}.{half}.{q}")
                                p.step("dve", lambda v, ttt=ttt, bank=bank:
                                       v.tensor_add(xr[ttt % 2][:], xr[ttt % 2][:],
                                                    PB[bank][:, 0:512]),
                                       rd=[f"pb{bank}"], wr=[xb2],
                                       tag=f"yadd{ocg}.{ttt}.{half}")
                                stx = p.step("gpq", lambda g, r0=r0, ttt=ttt, c0=c0:
                                             g.dma_start(
                                                 out=dst_x[r0:r0 + 128, c0:c0 + 512],
                                                 in_=xr[ttt % 2][:]),
                                             rd=[xb2],
                                             wr=[f"{dst_res}.{r0}.{ocg}.{half}"],
                                             tag=f"stx{ocg}.{half}.{ttt}",
                                             ch=f"st_{dst_res}")
                                if dst_x is y_out:
                                    final_deps.append(stx)
                    if CUT == "y":
                        p.barrier()
                        return
                    p.barrier()
            return


        def emit_attn(src_x, dst_x, src_res, dst_res):
            NH = 8
            with ExitStack() as s1:
                asb = lambda name, shape, dt=F32: s1.enter_context(
                    nc.sbuf_tensor(f"a1{name}", list(shape), dt))
                lnfm2 = asb("lnfm2", [128, 16 * 1024], BF16)
                ln2v = lnfm2[:].rearrange("p (k n) -> p k n", k=16)
                xt = [asb(f"xt{i}", [128, D]) for i in range(2)]
                st24 = asb("st24", [128, 24])
                mv = asb("mv", [128, 2])
                rstd = asb("rstd", [128, 2])
                xnb = asb("xnb", [128, D], BF16)
                wqb = [asb(f"wqb{i}", [128, 16 * 1024], BF16) for i in range(2)]
                wstg = [asb(f"wstg{i}", [128, 1024]) for i in range(2)]
                zstage = [asb(f"zst{i}", [128, 512], BF16) for i in range(2)]
                ztail_sb = asb("ztail", [128, 144], BF16)
                abt = asb("abt", [128, 8 * 32])

                for ttt in range(8):
                    r0 = ttt * 128
                    xb = f"axt{ttt % 2}"
                    p.step("scq", lambda s, r0=r0, ttt=ttt: s.dma_start(
                        out=xt[ttt % 2][:], in_=src_x[r0:r0 + 128, :]),
                        rd=[f"{src_res}.{r0}.0.0", f"{src_res}.{r0}.0.1",
                            f"{src_res}.{r0}.1.0", f"{src_res}.{r0}.1.1"],
                        wr=[xb], tag=f"aldx{ttt}", ch=f"axt{ttt % 2}")
                    for q in range(4):
                        p.step("dve", lambda v, q=q, ttt=ttt: v.bn_stats(
                            st24[:, q * 6:(q + 1) * 6],
                            xt[ttt % 2][:, q * 512:(q + 1) * 512]),
                            rd=[xb], wr=[f"ast24.{q}"], tag=f"abn{ttt}.{q}", wb=True)
                    p.step("dve", lambda v: v.bn_aggr(mv[:], st24[:]),
                           rd=["ast24.0", "ast24.1", "ast24.2", "ast24.3"],
                           wr=["amv"], tag="aaggr", wb=True)
                    p.step("act", lambda s: s.activation(
                        rstd[:, 0:1], mv[:, 1:2], AF.Ln, bias=c_eps[:, :]),
                        rd=["amv"], wr=["arstd0"])
                    p.step("act", lambda s: s.activation(
                        rstd[:, 1:2], rstd[:, 0:1], AF.Exp, scale=-0.5),
                        rd=["arstd0"], wr=["arstd"])
                    p.step("dve", lambda v, ttt=ttt: v.tensor_scalar(
                        xnb[:], xt[ttt % 2][:], mv[:, 0:1], rstd[:, 1:2],
                        ALU.subtract, ALU.mult),
                        rd=[xb, "amv", "arstd"], wr=["axnb"], tag=f"anrm{ttt}")
                    for bg in range(4):
                        for q in range(4):
                            k = bg * 4 + q
                            p.step("pe", lambda t, k=k, bg=bg, q=q: t.matmul(
                                PB[bg][:, q * 128:(q + 1) * 128],
                                xnb[:, k * 128:(k + 1) * 128], c_idb[:, :],
                                start=True, stop=True),
                                rd=["axnb"], wr=[f"pb{bg}"], tag=f"atrmm{ttt}.{bg}")
                        p.step("act", lambda s, bg=bg, ttt=ttt: s.activation(
                            ln2v[:, bg * 4:(bg + 1) * 4, ttt * 128:(ttt + 1) * 128],
                            PB[bg][:, 0:512].rearrange("p (k n) -> p k n", k=4),
                            AF.Copy),
                            rd=[f"pb{bg}"], wr=[f"aln.{bg}.{ttt}"],
                            tag=f"atrev{ttt}.{bg}")
                    for k in range(16):
                        p.step("pe", lambda t, k=k, ttt=ttt: t.matmul(
                            PB[4][:, 0:32],
                            ln2v[:, k, ttt * 128:(ttt + 1) * 128],
                            c_wabT[:, k * 32:(k + 1) * 32],
                            start=(k == 0), stop=(k == 15)),
                            rd=[f"aln.{k // 4}.{ttt}"],
                            wr=["pb4"] if k in (0, 15) else [],
                            tag=f"abmm{ttt}.{k}")
                    p.step("dve", lambda v, ttt=ttt: v.tensor_add(
                        abt[:, ttt * 32:(ttt + 1) * 32], PB[4][:, 0:32], c_abbias[:]),
                        rd=["pb4"], wr=[f"aabt.{ttt}"], tag=f"aabadd{ttt}")
                    p.step("gpq", lambda g, ttt=ttt, r0=r0: g.dma_start(
                        out=ab_out[r0:r0 + 128, :], in_=abt[:, ttt * 32:(ttt + 1) * 32]),
                        rd=[f"aabt.{ttt}"], wr=["ab_out_d"], tag=f"abst{ttt}",
                        ch="abst")
                for k in range(16):
                    p.step("gpq", lambda g, k=k: g.dma_start(
                        out=lnfm2d[:, k * 1024:(k + 1) * 1024],
                        in_=lnfm2[:, k * 1024:(k + 1) * 1024]),
                        rd=[f"aln.{k // 4}.{ttt2}" for ttt2 in range(8)],
                        wr=["lnfm2d_d"], tag=f"lnst{k}", ch="lnst")
                p.barrier()

                for j in range(6):
                    jp = j % 2
                    for k in range(16):
                        if f32w_attn:
                            sl = k % 2
                            p.step("syq", lambda sy, j=j, k=k, sl=sl: sy.dma_start(
                                out=wstg[sl][:],
                                in_=wqkvT[k * 128:(k + 1) * 128,
                                          j * 1024:(j + 1) * 1024]),
                                wr=[f"awst{sl}"], tag=f"wqld{j}.{k}", ch=f"awst{sl}")
                            p.step("dve", lambda v, j=j, k=k, sl=sl, jp=jp: v.tensor_copy(
                                wqb[jp][:, k * 1024:(k + 1) * 1024], wstg[sl][:]),
                                rd=[f"awst{sl}"],
                                wr=([f"awqb{jp}"] if k in (0, 15) else []) +
                                [f"awqb{jp}.k{k}"],
                                tag=f"wqcast{j}.{k}")
                    for oc8 in range(8):
                        bank = 5 + (j * 8 + oc8) % 2
                        for tt2 in range(2):
                            for k in range(16):
                                p.step("pe", lambda t, k=k, oc8=oc8, tt2=tt2, jp=jp, bank=bank:
                                       t.matmul(PB[bank][:, 0:512],
                                                wqb[jp][:, k * 1024 + oc8 * 128:
                                                         k * 1024 + (oc8 + 1) * 128],
                                                ln2v[:, k, tt2 * 512:(tt2 + 1) * 512],
                                                start=(k == 0), stop=(k == 15)),
                                       rd=[f"awqb{jp}"] if k in (0, 15) else [],
                                       wr=[f"pb{bank}"] if k in (0, 15) else [],
                                       tag=f"qmm{j}.{oc8}.{tt2}.{k}")
                            p.step("act", lambda s, tt2=tt2, bank=bank: s.activation(
                                zstage[tt2][:], PB[bank][:, 0:512], AF.Copy),
                                rd=[f"pb{bank}"], wr=[f"azst{tt2}"],
                                tag=f"qev{j}.{oc8}.{tt2}")
                            if tt2 == 1:
                                g_ = j * 8 + oc8
                                p.step("dve", lambda v, g_=g_: v.tensor_copy(
                                    ztail_sb[:, g_ * 3:(g_ + 1) * 3],
                                    zstage[1][:, 509:512]),
                                    rd=["azst1"], wr=[f"aztl.{g_}"],
                                    tag=f"ztl{g_}")
                            p.step("gpq", lambda g, g2=j * 8 + oc8, tt2=tt2: g.dma_start(
                                out=zpre[g2 * 128:(g2 + 1) * 128,
                                         tt2 * 512:(tt2 + 1) * 512],
                                in_=zstage[tt2][:]),
                                rd=[f"azst{tt2}"] + ([f"aztl.{j * 8 + oc8}"]
                                                     if tt2 == 1 else []),
                                wr=["zpre_d"], tag=f"zst{j}.{oc8}.{tt2}", ch="zpst")
                p.step("gpq", lambda g: g.dma_start(out=ztail_in, in_=ztail_sb[:]),
                       rd=[f"aztl.{g_}" for g_ in range(48)], wr=["ztail_in_d"],
                       tag="ztlst", ch="ztl")
                p.barrier()

            p.step("pool", lambda g: g.collective_compute(
                "AllGather", ALU.bypass, replica_groups=PAIRS,
                ins=[ztail_in], outs=[ztail_gat]),
                rd=["ztail_in_d"], wr=["ztail_gat_d"], tag="cc_ztl", wb=True)
            p.step("pool", lambda g: g.collective_compute(
                "AllGather", ALU.bypass, replica_groups=PAIRS,
                ins=[ab_out], outs=[ab_gat]),
                rd=["ab_out_d"], wr=["ab_gat_d"], tag="cc_ab", wb=True)
            p.barrier()

            p.fence("lnst", "zpst", "abst", "ztl")
            with ExitStack() as s2:
                asb = lambda name, shape, dt=F32: s2.enter_context(
                    nc.sbuf_tensor(f"a2{name}", list(shape), dt))
                halo = asb("halo", [128, 144], BF16)
                zw = asb("zw", [128, 8 * 1027], BF16)
                zw3 = zw[:].rearrange("p (g n) -> p g n", g=8)
                facc = asb("facc", [128, 1024])
                znorm = asb("znorm", [128, 8 * 1024], BF16)
                zn3 = znorm[:].rearrange("p (g n) -> p g n", g=8)
                sqb = asb("sqb", [128, 1024], BF16)
                rnr = asb("rnr", [8, 1024])
                rnb = asb("rnb", [8, 1024], BF16)

                p.step("gpq", lambda g: g.dma_start(out=halo[:], in_=ztail_gat[0]),
                       rd=["ztail_gat_d"], wr=["ahalo"], tag="halold", ch="halold")
                p.step("dve", lambda v: v.tensor_scalar(
                    halo[:], halo[:], c_m3[:, :], None, ALU.mult),
                    wr=["ahalo"], tag="halomask")
                for j in range(6):
                    for gg in range(8):
                        g_ = j * 8 + gg
                        p.step("gpq", lambda g, g_=g_, gg=gg: g.dma_start(
                            out=zw3[:, gg, 3:1027],
                            in_=zpre[g_ * 128:(g_ + 1) * 128, :]),
                            rd=["zpre_d"], wr=[f"azw.{gg}"], tag=f"zwld{g_}",
                            ch="zwld")
                    for gg in range(8):
                        g_ = j * 8 + gg
                        p.step("dve", lambda v, g_=g_, gg=gg: v.tensor_copy(
                            zw3[:, gg, 0:3], halo[:, g_ * 3:(g_ + 1) * 3]),
                            rd=["ahalo"], wr=[f"azw.{gg}"], tag=f"zwh{g_}")
                    for gg in range(8):
                        g_ = j * 8 + gg
                        p.step("dve", lambda v, gg=gg, g_=g_: v.tensor_scalar(
                            facc[:], zw3[:, gg, 0:1024],
                            c_cw[:, g_ * 4:g_ * 4 + 1], None, ALU.mult),
                            rd=[f"azw.{gg}"], wr=["afacc"], tag=f"cv0.{g_}")
                        for j2 in range(1, 4):
                            p.step("dve", lambda v, gg=gg, g_=g_, j2=j2:
                                   v.scalar_tensor_tensor(
                                       facc[:], zw3[:, gg, j2:j2 + 1024],
                                       c_cw[:, g_ * 4 + j2:g_ * 4 + j2 + 1],
                                       facc[:], ALU.mult, ALU.add),
                                   rd=[f"azw.{gg}"], wr=["afacc"], tag=f"cv{j2}.{g_}")
                        p.step("act", lambda s, gg=gg: s.activation(
                            zn3[:, gg, :], facc[:], AF.Silu),
                            rd=["afacc"], wr=[f"azn.{gg}"], tag=f"cvsilu.{g_}")
                        if j < 4:
                            p.step("act", lambda s, gg=gg: s.activation(
                                sqb[:], zn3[:, gg, :], AF.Square),
                                rd=[f"azn.{gg}"], wr=["asqb"], tag=f"nsq.{g_}")
                            for hh in range(2):
                                p.step("pe", lambda t, hh=hh: t.matmul(
                                    PB[4 + hh][0:1, 0:512], c_onescolb[:, :],
                                    sqb[:, hh * 512:(hh + 1) * 512],
                                    start=True, stop=True),
                                    rd=["asqb"], wr=[f"pb{4 + hh}"],
                                    tag=f"nsum.{g_}.{hh}")
                                p.step("act", lambda s, hh=hh: s.activation(
                                    rnr[0:1, hh * 512:(hh + 1) * 512],
                                    PB[4 + hh][0:1, 0:512], AF.Ln),
                                    rd=[f"pb{4 + hh}"], wr=[f"arnr{hh}"],
                                    tag=f"nln.{g_}.{hh}")
                                p.step("act", lambda s, hh=hh: s.activation(
                                    rnr[0:1, hh * 512:(hh + 1) * 512],
                                    rnr[0:1, hh * 512:(hh + 1) * 512],
                                    AF.Exp, scale=-0.5),
                                    wr=[f"arnr{hh}"], tag=f"nexp.{g_}.{hh}")
                                p.step("dve", lambda v, hh=hh: v.tensor_scalar(
                                    rnb[0:1, hh * 512:(hh + 1) * 512],
                                    rnr[0:1, hh * 512:(hh + 1) * 512],
                                    1e12, None, ALU.min),
                                    rd=[f"arnr{hh}"], wr=[f"arnb{hh}"],
                                    tag=f"nmin.{g_}.{hh}")
                                p.step("pe", lambda t, hh=hh: t.matmul(
                                    PB[6 + hh][:, 0:512], c_onesb[0:1, :],
                                    rnb[0:1, hh * 512:(hh + 1) * 512],
                                    start=True, stop=True),
                                    rd=[f"arnb{hh}"], wr=[f"pb{6 + hh}"],
                                    tag=f"nbc.{g_}.{hh}")
                                p.step("dve", lambda v, gg=gg, hh=hh: v.tensor_mul(
                                    zn3[:, gg, hh * 512:(hh + 1) * 512],
                                    zn3[:, gg, hh * 512:(hh + 1) * 512],
                                    PB[6 + hh][:, 0:512]),
                                    rd=[f"pb{6 + hh}"], wr=[f"azn.{gg}"],
                                    tag=f"nmul.{g_}.{hh}")
                        p.step("gpq", lambda g, g_=g_, gg=gg: g.dma_start(
                            out=qkv_out[g_ * 128:(g_ + 1) * 128, :],
                            in_=zn3[:, gg, :]),
                            rd=[f"azn.{gg}"], wr=["qkv_out_d"], tag=f"qkst{g_}",
                            ch="qkst")
                p.barrier()

            p.step("pool", lambda g: g.collective_compute(
                "AllGather", ALU.bypass, replica_groups=PAIRS,
                ins=[qkv_out], outs=[qkv_gat]),
                rd=["qkv_out_d"], wr=["qkv_gat_d"], tag="cc_qkv", wb=True)
            p.barrier()

            p.fence("qkst", "zwld")
            sA = ExitStack()
            asb = lambda name, shape, dt=F32: sA.enter_context(
                nc.sbuf_tensor(f"a6{name}", list(shape), dt))
            qkvh = asb("qkvh", [128, 24 * 2048], BF16)
            ab_all = asb("aball", [128, 16 * 32])
            alpha_s = asb("alphas", [128, 128])
            beta_s = asb("betas", [128, 128])
            ell_tm = asb("elltm", [128, 128])
            negb_tm = asb("negbtm", [128, 128])
            O_fm = asb("Ofm", [128, 8 * 2048], BF16)
            S0f = asb("S0f", [128, 8 * 128])
            S0b = asb("S0b", [128, 8 * 128], BF16)

            rt = nc.gpsimd.alloc_register("rt")
            qkvh3 = qkvh[:].rearrange("p (s n) -> p s n", s=24)
            for t3 in range(3):
                for r in range(2):
                    i3 = t3 * 2 + r
                    def fnq(g, i3=i3, t3=t3, r=r):
                        g.reg_load(rt, c_poff[0:1, i3:i3 + 1])
                        return g.dma_start(
                            out=qkvh3[:, t3 * 8:(t3 + 1) * 8,
                                      r * 1024:(r + 1) * 1024],
                            in_=bass.AP(qkv_gat_t, rt,
                                        [[T, 128], [128 * T, 8], [1, 1024]]))
                    p.step("gpq", fnq, rd=["qkv_gat_d", "c_poff"], wr=["aqkvh"],
                           tag=f"qhl{t3}.{r}", ch="qhl")
            for tt3 in range(16):
                r, tl = tt3 // 8, tt3 % 8
                p.step("gpq", lambda g, r=r, tl=tl, tt3=tt3: g.dma_start(
                    out=ab_all[:, tt3 * 32:(tt3 + 1) * 32],
                    in_=ab_gat[r, tl * 128:(tl + 1) * 128, :]),
                    rd=["ab_gat_d"], wr=["aaball"], tag=f"abl{tt3}", ch="abl")
            abv = ab_all[:].rearrange("p (t c) -> p t c", t=16)
            alp3 = alpha_s[:].rearrange("p (t c) -> p t c", t=16)
            bet3 = beta_s[:].rearrange("p (t c) -> p t c", t=16)
            p.step("dve", lambda v: v.tensor_sub(
                alp3, abv[:, :, 8:16], abv[:, :, 0:8]),
                rd=["aaball"], wr=["aalpha"], tag="sel_a1")
            p.step("dve", lambda v: v.scalar_tensor_tensor(
                alp3, alp3, c_m3[:, :], abv[:, :, 0:8], ALU.mult, ALU.add),
                rd=["aaball"], wr=["aalpha"], tag="sel_a2")
            p.step("dve", lambda v: v.tensor_sub(
                bet3, abv[:, :, 24:32], abv[:, :, 16:24]),
                rd=["aaball"], wr=["abeta"], tag="sel_b1")
            p.step("dve", lambda v: v.scalar_tensor_tensor(
                bet3, bet3, c_m3[:, :], abv[:, :, 16:24], ALU.mult, ALU.add),
                rd=["aaball"], wr=["abeta"], tag="sel_b2")
            p.step("act", lambda s: s.activation(alpha_s[:], alpha_s[:], AF.Sigmoid),
                   wr=["aalpha"], tag="sig_a")
            p.step("act", lambda s: s.activation(beta_s[:], beta_s[:], AF.Sigmoid),
                   wr=["abeta"], tag="sig_b")
            p.step("act", lambda s: s.activation(ell_tm[:], alpha_s[:], AF.Ln),
                   rd=["aalpha"], wr=["aell"], tag="log_a")
            p.step("dve", lambda v: v.tensor_scalar(
                ell_tm[:], ell_tm[:], -60.0, None, ALU.max),
                wr=["aell"], tag="clamp_a")
            p.step("dve", lambda v: v.tensor_scalar(
                negb_tm[:], beta_s[:], -1.0, None, ALU.mult),
                rd=["abeta"], wr=["anegb"], tag="negb")
            p.step("dve", lambda v: v.memset(S0f[:], 0.0), wr=["aS0f"], tag="s0z")
            p.step("dve", lambda v: v.memset(S0b[:], 0.0), wr=["aS0b"], tag="s0bz")
            p.barrier()

            with ExitStack() as s6:
                csb = lambda name, shape, dt=F32: s6.enter_context(
                    nc.sbuf_tensor(f"a6c{name}", list(shape), dt))
                lell = csb("lell", [8, 128])
                lrow = csb("lrow", [8, 128])
                dtmp = csb("dtmp", [8, 128])
                drow = csb("drow", [8, 128])
                arow = csb("arow", [8, 128])
                acbr = csb("acbr", [8, 128])
                sc_tm = csb("sctm", [128, 32])
                negA8 = csb("negA8", [128, 8])
                lrbuf = csb("lrbuf", [8, 512])
                rsS = csb("rsS", [128, 512])
                rsI = csb("rsI", [128, 512])
                RtS = csb("RtS", [128, 512], BF16)
                RtI = csb("RtI", [128, 512], BF16)
                tDt = csb("tDt", [128, 512], BF16)
                tD = csb("tD", [128, 512], BF16)
                tW = csb("tW", [128, 512], BF16)
                tMt = csb("tMt", [128, 512], BF16)
                tVt = csb("tVt", [128, 512], BF16)
                tO2 = csb("tO2", [128, 512], BF16)
                tOt = csb("tOt", [128, 512], BF16)
                tKb = csb("tKb", [128, 512], BF16)

                bankctr = [0]

                def nb():
                    bankctr[0] = (bankctr[0] + 1) % 8
                    return bankctr[0]

                for c in range(16):
                    p.barrier()
                    b0 = nb()
                    p.step("pe", lambda t, c=c, b0=b0: t.transpose(
                        PB[b0][0:8, 0:128], ell_tm[:, c * 8:(c + 1) * 8], c_idf[:, :]),
                        rd=["aell"], wr=[f"pb{b0}"], tag=f"ellT{c}")
                    p.step("act", lambda s, b0=b0: s.activation(
                        lell[:], PB[b0][0:8, 0:128], AF.Copy),
                        rd=[f"pb{b0}"], wr=["clell"], tag=f"ellcp{c}")
                    p.step("dve", lambda v: v.tensor_tensor_scan(
                        lrow[:], lell[:], lell[:], 0.0, ALU.add, ALU.bypass),
                        rd=["clell"], wr=["clrow"], tag=f"scan{c}", wb=True)
                    p.step("dve", lambda v: v.tensor_scalar(
                        dtmp[:], lrow[:], lrow[:, 127:128], None, ALU.subtract),
                        rd=["clrow"], wr=["cdtmp"], tag=f"dsub{c}")
                    p.step("act", lambda s: s.activation(drow[:], dtmp[:], AF.Exp,
                                                         scale=-1.0),
                           rd=["cdtmp"], wr=["cdrow"], tag=f"dexp{c}")
                    p.step("act", lambda s: s.activation(arow[:], lrow[:], AF.Exp),
                           rd=["clrow"], wr=["carow"], tag=f"aexp{c}")
                    p.step("dve", lambda v: v.tensor_scalar(
                        acbr[:], c_ones8[:], arow[:, 127:128], None, ALU.mult),
                        rd=["carow"], wr=["cacb"], tag=f"acb{c}")
                    b1 = nb()
                    for qi, (rowt, rname) in enumerate(
                            [(drow, "cdrow"), (arow, "carow"),
                             (acbr, "cacb"), (lrow, "clrow")]):
                        p.step("pe", lambda t, rowt=rowt, qi=qi, b1=b1: t.transpose(
                            PB[b1][:, qi * 8:(qi + 1) * 8], rowt[:, :],
                            c_idf[0:8, 0:8]),
                            rd=[rname], wr=[f"pb{b1}"], tag=f"scT{c}.{qi}")
                    p.step("act", lambda s, b1=b1: s.activation(
                        sc_tm[:], PB[b1][:, 0:32], AF.Copy),
                        rd=[f"pb{b1}"], wr=["csc"], tag=f"scev{c}")
                    p.step("dve", lambda v: v.tensor_scalar(
                        negA8[:], sc_tm[:, 8:16], -1.0, None, ALU.mult),
                        rd=["csc"], wr=["cnegA"], tag=f"negA{c}")

                    if A6STOP and A6STOP == (c, -1, 0):
                        p.barrier()
                        s6.pop_all().close()
                        sA.close()
                        return
                    for g2 in range(2):
                        hs = [g2 * 4 + q for q in range(4)]
                        sl = lambda buf, q: buf[:, q * 128:(q + 1) * 128]
                        qf = lambda h, c=c: qkvh[:, (0 * 8 + h) * 2048 + c * 128:
                                                 (0 * 8 + h) * 2048 + (c + 1) * 128]
                        kf = lambda h, c=c: qkvh[:, (1 * 8 + h) * 2048 + c * 128:
                                                 (1 * 8 + h) * 2048 + (c + 1) * 128]
                        vf = lambda h, c=c: qkvh[:, (2 * 8 + h) * 2048 + c * 128:
                                                 (2 * 8 + h) * 2048 + (c + 1) * 128]
                        bkk = nb()
                        bkq = nb()
                        for q, h in enumerate(hs):
                            p.step("pe", lambda t, h=h, q=q, bkk=bkk: t.matmul(
                                PB[bkk][:, q * 128:(q + 1) * 128], kf(h), kf(h),
                                start=True, stop=True),
                                rd=["aqkvh"], wr=[f"pb{bkk}"], tag=f"gkk{c}.{h}")
                            p.step("pe", lambda t, h=h, q=q, bkq=bkq: t.matmul(
                                PB[bkq][:, q * 128:(q + 1) * 128], kf(h), qf(h),
                                start=True, stop=True),
                                rd=["aqkvh"], wr=[f"pb{bkq}"], tag=f"gkq{c}.{h}")
                        if A6STOP and A6STOP == (c, g2, 9):
                            p.barrier()
                            s6.pop_all().close()
                            sA.close()
                            return
                        brow = nb()
                        bbc = nb()
                        for q, h in enumerate(hs):
                            p.step("pe", lambda t, h=h, q=q, brow=brow: t.transpose(
                                PB[brow][0:1, q * 128:(q + 1) * 128],
                                sc_tm[:, 24 + h:25 + h], c_idf[:, :]),
                                rd=["csc"], wr=[f"pb{brow}"], tag=f"lrT{c}.{h}")
                            p.step("act", lambda s, q=q, brow=brow: s.activation(
                                lrbuf[0:1, q * 128:(q + 1) * 128],
                                PB[brow][0:1, q * 128:(q + 1) * 128], AF.Copy),
                                rd=[f"pb{brow}"], wr=[f"clr{q}"], tag=f"lrcp{c}.{q}")
                            p.step("pe", lambda t, q=q, bbc=bbc: t.matmul(
                                PB[bbc][:, q * 128:(q + 1) * 128],
                                c_ones8[0:1, :], lrbuf[0:1, q * 128:(q + 1) * 128],
                                start=True, stop=True),
                                rd=[f"clr{q}"], wr=[f"pb{bbc}"], tag=f"lbc{c}.{q}")
                        for q, h in enumerate(hs):
                            p.step("dve", lambda v, h=h, q=q, bbc=bbc: v.scalar_tensor_tensor(
                                sl(rsS, q), PB[bbc][:, q * 128:(q + 1) * 128],
                                sc_tm[:, 24 + h:25 + h], c_maskS[:, :],
                                ALU.subtract, ALU.add),
                                rd=[f"pb{bbc}", "csc"], wr=["crsS"], tag=f"rsS{c}.{h}")
                            p.step("dve", lambda v, h=h, q=q, bbc=bbc: v.scalar_tensor_tensor(
                                sl(rsI, q), PB[bbc][:, q * 128:(q + 1) * 128],
                                sc_tm[:, 24 + h:25 + h], c_maskI[:, :],
                                ALU.subtract, ALU.add),
                                rd=[f"pb{bbc}", "csc"], wr=["crsI"], tag=f"rsI{c}.{h}")
                        p.step("act", lambda s: s.activation(RtS[:], rsS[:], AF.Exp),
                               rd=["crsS"], wr=["cRtS"], tag=f"expS{c}.{g2}")
                        p.step("act", lambda s: s.activation(RtI[:], rsI[:], AF.Exp),
                               rd=["crsI"], wr=["cRtI"], tag=f"expI{c}.{g2}")
                        for q, h in enumerate(hs):
                            p.step("dve", lambda v, h=h, q=q, c=c, bkk=bkk:
                                   v.scalar_tensor_tensor(
                                       sl(tDt, q), PB[bkk][:, q * 128:(q + 1) * 128],
                                       negb_tm[:, c * 8 + h:c * 8 + h + 1],
                                       sl(RtS, q), ALU.mult, ALU.mult),
                                   rd=[f"pb{bkk}", "cRtS", "anegb"], wr=["cDt"],
                                   tag=f"bt{c}.{h}")
                            p.step("dve", lambda v, q=q, bkq=bkq: v.tensor_mul(
                                sl(tMt, q), PB[bkq][:, q * 128:(q + 1) * 128],
                                sl(RtI, q)),
                                rd=[f"pb{bkq}", "cRtI"], wr=["cMt"], tag=f"mt{c}.{h}")
                        if A6STOP and A6STOP == (c, g2, 1):
                            p.barrier()
                            s6.pop_all().close()
                            sA.close()
                            return
                        bD0 = nb()
                        bVt = nb()
                        bW0 = nb()
                        for q, h in enumerate(hs):
                            p.step("pe", lambda t, q=q, bD0=bD0: t.matmul(
                                PB[bD0][:, q * 128:(q + 1) * 128], sl(tDt, q),
                                c_idb[:, :], start=True, stop=True),
                                rd=["cDt"], wr=[f"pb{bD0}"], tag=f"d0T{c}.{h}")
                            p.step("pe", lambda t, h=h, q=q, bVt=bVt: t.matmul(
                                PB[bVt][:, q * 128:(q + 1) * 128], vf(h),
                                c_idb[:, :], start=True, stop=True),
                                rd=["aqkvh"], wr=[f"pb{bVt}"], tag=f"vT{c}.{h}")
                            p.step("pe", lambda t, h=h, q=q, bW0=bW0: t.matmul(
                                PB[bW0][:, q * 128:(q + 1) * 128], kf(h),
                                S0b[:, h * 128:(h + 1) * 128],
                                start=True, stop=True),
                                rd=["aqkvh", "aS0b"], wr=[f"pb{bW0}"],
                                tag=f"w0{c}.{h}")
                        p.step("dve", lambda v, bD0=bD0: v.tensor_copy(
                            tD[:], PB[bD0][:, 0:512]),
                            rd=[f"pb{bD0}"], wr=["cD"], tag=f"d0ev{c}.{g2}")
                        p.step("act", lambda s, bVt=bVt: s.activation(
                            tVt[:], PB[bVt][:, 0:512], AF.Copy),
                            rd=[f"pb{bVt}"], wr=["cVt"], tag=f"vev{c}.{g2}")
                        for q, h in enumerate(hs):
                            p.step("dve", lambda v, h=h, q=q, bW0=bW0:
                                   v.scalar_tensor_tensor(
                                       sl(tW, q), PB[bW0][:, q * 128:(q + 1) * 128],
                                       negA8[:, h:h + 1], sl(tVt, q),
                                       ALU.mult, ALU.add),
                                   rd=[f"pb{bW0}", "cVt", "cnegA"], wr=["cW"],
                                   tag=f"vw{c}.{h}")
                        for kk in range(7):
                            ba = nb()
                            for q, h in enumerate(hs):
                                p.step("pe", lambda t, q=q, ba=ba: t.matmul(
                                    PB[ba][:, q * 128:(q + 1) * 128], sl(tDt, q),
                                    sl(tW, q), start=True, stop=False,
                                    skip_group_check=True),
                                    rd=["cDt", "cW"], wr=[f"pb{ba}"],
                                    tag=f"ap{c}.{g2}.{kk}.{q}a")
                                p.step("pe", lambda t, q=q, ba=ba: t.matmul(
                                    PB[ba][:, q * 128:(q + 1) * 128], c_idb[:, :],
                                    sl(tW, q), start=False, stop=True,
                                    skip_group_check=True),
                                    rd=["cW"], wr=[f"pb{ba}"],
                                    tag=f"ap{c}.{g2}.{kk}.{q}b")
                            if kk < 6:
                                bb = nb()
                                bc = nb()
                                for q in range(4):
                                    p.step("pe", lambda t, q=q, bb=bb: t.matmul(
                                        PB[bb][:, q * 128:(q + 1) * 128], sl(tDt, q),
                                        sl(tD, q), start=True, stop=True),
                                        rd=["cDt", "cD"], wr=[f"pb{bb}"],
                                        tag=f"sqD{c}.{g2}.{kk}.{q}")
                                    p.step("pe", lambda t, q=q, bc=bc: t.matmul(
                                        PB[bc][:, q * 128:(q + 1) * 128], sl(tD, q),
                                        sl(tDt, q), start=True, stop=True),
                                        rd=["cDt", "cD"], wr=[f"pb{bc}"],
                                        tag=f"sqDt{c}.{g2}.{kk}.{q}")
                                p.step("act", lambda s, bb=bb: s.activation(
                                    tD[:], PB[bb][:, 0:512], AF.Copy),
                                    rd=[f"pb{bb}"], wr=["cD"], tag=f"evD{c}.{g2}.{kk}")
                                p.step("dve", lambda v, bc=bc: v.tensor_copy(
                                    tDt[:], PB[bc][:, 0:512]),
                                    rd=[f"pb{bc}"], wr=["cDt"],
                                    tag=f"evDt{c}.{g2}.{kk}")
                            if kk < 6:
                                p.step("act", lambda s, ba=ba: s.activation(
                                    tW[:], PB[ba][:, 0:512], AF.Copy),
                                    rd=[f"pb{ba}"], wr=["cW"], tag=f"evW{c}.{g2}.{kk}")
                            else:
                                for q, h in enumerate(hs):
                                    p.step("act", lambda s, q=q, h=h, c=c, ba=ba:
                                           s.activation(
                                               sl(tW, q),
                                               PB[ba][:, q * 128:(q + 1) * 128],
                                               AF.Copy,
                                               scale=beta_s[:, c * 8 + h:
                                                            c * 8 + h + 1]),
                                           rd=[f"pb{ba}"], wr=["cW"],
                                           tag=f"evU{c}.{g2}.{q}")
                        if A6STOP and A6STOP == (c, g2, 2):
                            p.barrier()
                            s6.pop_all().close()
                            sA.close()
                            return
                        bO = nb()
                        bO2 = nb()
                        for q, h in enumerate(hs):
                            p.step("pe", lambda t, q=q, bO=bO: t.matmul(
                                PB[bO][:, q * 128:(q + 1) * 128], sl(tMt, q),
                                sl(tW, q), start=True, stop=True),
                                rd=["cMt", "cW"], wr=[f"pb{bO}"], tag=f"o1{c}.{h}")
                            p.step("pe", lambda t, h=h, q=q, bO2=bO2: t.matmul(
                                PB[bO2][:, q * 128:(q + 1) * 128], qf(h),
                                S0b[:, h * 128:(h + 1) * 128],
                                start=True, stop=True),
                                rd=["aqkvh", "aS0b"], wr=[f"pb{bO2}"],
                                tag=f"o2{c}.{h}")
                        for q, h in enumerate(hs):
                            p.step("act", lambda s, q=q, h=h, bO2=bO2: s.activation(
                                sl(tO2, q), PB[bO2][:, q * 128:(q + 1) * 128],
                                AF.Copy, scale=sc_tm[:, 8 + h:9 + h]),
                                rd=[f"pb{bO2}", "csc"], wr=["cO2"],
                                tag=f"o2ev{c}.{h}")
                        p.step("dve", lambda v, bO=bO: v.tensor_copy(
                            tOt[:], PB[bO][:, 0:512]),
                            rd=[f"pb{bO}"], wr=["cOt"], tag=f"oev{c}.{g2}")
                        p.step("dve", lambda v: v.tensor_add(
                            tOt[:], tOt[:], tO2[:]),
                            rd=["cO2"], wr=["cOt"], tag=f"oadd{c}.{g2}")
                        if A6STOP and A6STOP == (c, g2, 3):
                            p.barrier()
                            s6.pop_all().close()
                            sA.close()
                            return
                        bK = nb()
                        bS = nb()
                        for q, h in enumerate(hs):
                            p.step("pe", lambda t, h=h, q=q, bK=bK: t.matmul(
                                PB[bK][:, q * 128:(q + 1) * 128], kf(h),
                                c_idb[:, :], start=True, stop=True),
                                rd=["aqkvh"], wr=[f"pb{bK}"], tag=f"kT{c}.{h}")
                        for q, h in enumerate(hs):
                            p.step("act", lambda s, q=q, h=h, bK=bK: s.activation(
                                sl(tKb, q), PB[bK][:, q * 128:(q + 1) * 128],
                                AF.Copy, scale=sc_tm[:, h:h + 1]),
                                rd=[f"pb{bK}", "csc"], wr=["cKb"], tag=f"kbev{c}.{h}")
                        for q, h in enumerate(hs):
                            p.step("pe", lambda t, q=q, bS=bS: t.matmul(
                                PB[bS][:, q * 128:(q + 1) * 128], sl(tKb, q),
                                sl(tW, q), start=True, stop=True),
                                rd=["cKb", "cW"], wr=[f"pb{bS}"], tag=f"smm{c}.{h}")
                        for q, h in enumerate(hs):
                            p.step("dve", lambda v, q=q, h=h, bS=bS:
                                   v.scalar_tensor_tensor(
                                       S0f[:, h * 128:(h + 1) * 128],
                                       S0f[:, h * 128:(h + 1) * 128],
                                       sc_tm[:, 16 + h:17 + h],
                                       PB[bS][:, q * 128:(q + 1) * 128],
                                       ALU.mult, ALU.add),
                                   rd=[f"pb{bS}", "csc"], wr=["aS0f"],
                                   tag=f"sup{c}.{h}")
                        p.step("pool", lambda g, g2=g2: g.tensor_copy(
                            S0b[:, g2 * 512:(g2 + 1) * 512],
                            S0f[:, g2 * 512:(g2 + 1) * 512]),
                            rd=["aS0f"], wr=["aS0b"], tag=f"s0b{c}.{g2}")
                        if A6STOP and A6STOP == (c, g2, 4):
                            p.barrier()
                            s6.pop_all().close()
                            sA.close()
                            return
                        bT = nb()
                        for q, h in enumerate(hs):
                            p.step("pe", lambda t, q=q, bT=bT: t.matmul(
                                PB[bT][:, q * 128:(q + 1) * 128], sl(tOt, q),
                                c_idb[:, :], start=True, stop=True),
                                rd=["cOt"], wr=[f"pb{bT}"], tag=f"oT{c}.{h}")
                        g0 = hs[0]
                        p.step("act", lambda s, c=c, g0=g0, bT=bT: s.activation(
                            O_fm[:].rearrange("p (g n) -> p g n", g=8)[
                                :, g0:g0 + 4, c * 128:(c + 1) * 128],
                            PB[bT][:, 0:512].rearrange("p (q n) -> p q n", q=4),
                            AF.Copy),
                            rd=[f"pb{bT}"], wr=[f"aOfm.{c}.{g2}"],
                            tag=f"ofmev{c}.{g2}")
                        if A6STOP and (c, g2) == A6STOP:
                            p.barrier()
                            s6.pop_all().close()
                            sA.close()
                            return
                if CUT == "a6":
                    p.barrier()
                    sA.close()
                    return
                p.barrier()

            for g_ in range(NH):
                p.step("gpq", lambda g, g_=g_: g.dma_start(
                    out=o_out[g_ * 128:(g_ + 1) * 128, :],
                    in_=O_fm[:, g_ * 2048:(g_ + 1) * 2048]),
                    rd=[f"aOfm.{c2}.{g22}" for c2 in range(16) for g22 in range(2)],
                    wr=["o_out_d"], tag=f"ost{g_}", ch="ost")
            p.barrier()
            p.step("pool", lambda g: g.collective_compute(
                "AllGather", ALU.bypass, replica_groups=PAIRS,
                ins=[o_out], outs=[o_gat]),
                rd=["o_out_d"], wr=["o_gat_d"], tag="cc_o", wb=True)
            p.barrier()
            sA.close()

            p.fence("ost")
            with ExitStack() as s8:
                asb = lambda name, shape, dt=F32: s8.enter_context(
                    nc.sbuf_tensor(f"a8{name}", list(shape), dt))
                Ofull = asb("Ofull", [128, 16 * 1024], BF16)
                lnfm2b = asb("lnfm2b", [128, 16 * 1024], BF16)
                ln2vb = lnfm2b[:].rearrange("p (k n) -> p k n", k=16)
                wgb = [asb(f"wgb{i}", [128, 16 * 1024], BF16) for i in range(2)]
                wstg8 = [asb(f"wstg{i}", [128, 1024]) for i in range(2)]
                gstage = asb("gstage", [128, 512], BF16)
                ystage = asb("ystage", [128, 4 * 512])
                xr = [asb(f"xr{i}", [128, 512]) for i in range(2)]

                Ofull3 = Ofull[:].rearrange("p (s n) -> p s n", s=16)
                for r in range(2):
                    def fno(g, r=r):
                        g.reg_load(rt, c_poff[0:1, 6 + r:7 + r])
                        return g.dma_start(
                            out=Ofull3[:, r * 8:(r + 1) * 8, :],
                            in_=bass.AP(o_gat_t, rt,
                                        [[D, 128], [128 * D, 8], [1, 1024]]))
                    p.step("gpq", fno, rd=["o_gat_d", "c_poff"], wr=["aOfull"],
                           tag=f"ofl{r}", ch="ofl")
                for k in range(16):
                    p.step("syq", lambda sy, k=k: sy.dma_start(
                        out=lnfm2b[:, k * 1024:(k + 1) * 1024],
                        in_=lnfm2d[:, k * 1024:(k + 1) * 1024]),
                        rd=["lnfm2d_d"], wr=["aln2b"], tag=f"lnld{k}", ch="lnld")
                for halfg in range(2):
                    jp = halfg % 2
                    for k in range(16):
                        sl2 = k % 2
                        p.step("syq", lambda sy, halfg=halfg, k=k, sl2=sl2: sy.dma_start(
                            out=wstg8[sl2][:],
                            in_=wgT[k * 128:(k + 1) * 128,
                                    halfg * 1024:(halfg + 1) * 1024]),
                            wr=[f"a8wst{sl2}"], tag=f"wgld{halfg}.{k}",
                            ch=f"a8wst{sl2}")
                        p.step("dve", lambda v, k=k, sl2=sl2, jp=jp: v.tensor_copy(
                            wgb[jp][:, k * 1024:(k + 1) * 1024], wstg8[sl2][:]),
                            rd=[f"a8wst{sl2}"],
                            wr=([f"awgb{jp}"] if k in (0, 15) else []),
                            tag=f"wgcast{halfg}.{k}")
                    for oc8 in range(8):
                        cc = halfg * 8 + oc8
                        for tt2 in range(2):
                            bank = (oc8 * 2 + tt2) % 8
                            for k in range(16):
                                p.step("pe", lambda t, k=k, oc8=oc8, tt2=tt2, jp=jp,
                                       bank=bank: t.matmul(
                                           PB[bank][:, 0:512],
                                           wgb[jp][:, k * 1024 + oc8 * 128:
                                                    k * 1024 + (oc8 + 1) * 128],
                                           ln2vb[:, k, tt2 * 512:(tt2 + 1) * 512],
                                           start=(k == 0), stop=(k == 15)),
                                       rd=(["awgb" + str(jp), "aln2b"]
                                           if k in (0, 15) else []),
                                       wr=[f"pb{bank}"] if k in (0, 15) else [],
                                       tag=f"gmm{halfg}.{oc8}.{tt2}.{k}")
                            p.step("act", lambda s, bank=bank: s.activation(
                                gstage[:], PB[bank][:, 0:512], AF.Silu),
                                rd=[f"pb{bank}"], wr=["agst"],
                                tag=f"gsilu{halfg}.{oc8}.{tt2}")
                            p.step("dve", lambda v, cc=cc, tt2=tt2: v.tensor_mul(
                                Ofull[:, cc * 1024 + tt2 * 512:
                                      cc * 1024 + (tt2 + 1) * 512],
                                Ofull[:, cc * 1024 + tt2 * 512:
                                      cc * 1024 + (tt2 + 1) * 512],
                                gstage[:]),
                                rd=["agst", "aOfull"], wr=["aOfull"],
                                tag=f"gate{halfg}.{oc8}.{tt2}")
                p.barrier()
                for ocg in range(2):
                    jp = ocg % 2
                    for k in range(16):
                        sl2 = k % 2
                        p.step("syq", lambda sy, ocg=ocg, k=k, sl2=sl2: sy.dma_start(
                            out=wstg8[sl2][:],
                            in_=woT[k * 128:(k + 1) * 128,
                                    ocg * 1024:(ocg + 1) * 1024]),
                            wr=[f"a8wst{sl2}"], tag=f"wold{ocg}.{k}",
                            ch=f"a8wst{sl2}")
                        p.step("dve", lambda v, k=k, sl2=sl2, jp=jp: v.tensor_copy(
                            wgb[jp][:, k * 1024:(k + 1) * 1024], wstg8[sl2][:]),
                            rd=[f"a8wst{sl2}"],
                            wr=([f"awgb{jp}"] if k in (0, 15) else []),
                            tag=f"wocast{ocg}.{k}")
                    for tt2 in range(2):
                        tok0 = tt2 * 512
                        for oc8 in range(8):
                            for k in range(16):
                                p.step("pe", lambda t, k=k, oc8=oc8, tt2=tt2, jp=jp:
                                       t.matmul(
                                           PB[oc8][:, 0:512],
                                           wgb[jp][:, k * 1024 + oc8 * 128:
                                                    k * 1024 + (oc8 + 1) * 128],
                                           Ofull[:, k * 1024 + tt2 * 512:
                                                 k * 1024 + (tt2 + 1) * 512],
                                           start=(k == 0), stop=(k == 15)),
                                       rd=(["awgb" + str(jp), "aOfull"]
                                           if k in (0, 15) else []),
                                       wr=[f"pb{oc8}"] if k in (0, 15) else [],
                                       tag=f"womm{ocg}.{tt2}.{oc8}.{k}")
                        for half in range(2):
                            for q4 in range(4):
                                oc = half * 4 + q4
                                p.step("act", lambda s, oc=oc, q4=q4: s.activation(
                                    ystage[:, q4 * 512:(q4 + 1) * 512],
                                    PB[oc][:, 0:512], AF.Copy),
                                    rd=[f"pb{oc}"], wr=[f"a8ys.{q4}"],
                                    tag=f"woev{ocg}.{tt2}.{oc}")
                            for ttt in range(4):
                                r0 = tok0 + ttt * 128
                                c0 = ocg * 1024 + half * 512
                                bank = ttt if half == 0 else 4 + ttt
                                xb2 = f"a8xr{ttt % 2}"
                                p.step("scq", lambda s, r0=r0, ttt=ttt, c0=c0:
                                       s.dma_start(out=xr[ttt % 2][:],
                                                   in_=src_x[r0:r0 + 128,
                                                             c0:c0 + 512]),
                                       rd=[f"{src_res}.{r0}.{ocg}.0",
                                           f"{src_res}.{r0}.{ocg}.1"],
                                       wr=[xb2], tag=f"aldxr{ocg}.{tt2}.{half}.{ttt}",
                                       ch=f"a8xr{ttt % 2}")
                                for q in range(4):
                                    p.step("pe", lambda t, q=q, ttt=ttt, bank=bank:
                                           t.transpose(
                                               PB[bank][:, q * 128:(q + 1) * 128],
                                               ystage[:, q * 512 + ttt * 128:
                                                       q * 512 + (ttt + 1) * 128],
                                               c_idf[:, :]),
                                           rd=[f"a8ys.{q}"], wr=[f"pb{bank}"],
                                           tag=f"aytr{ocg}.{tt2}.{ttt}.{half}.{q}")
                                p.step("dve", lambda v, ttt=ttt, bank=bank:
                                       v.tensor_add(xr[ttt % 2][:], xr[ttt % 2][:],
                                                    PB[bank][:, 0:512]),
                                       rd=[f"pb{bank}"], wr=[xb2],
                                       tag=f"ayadd{ocg}.{tt2}.{ttt}.{half}")
                                stx = p.step("gpq", lambda g, r0=r0, ttt=ttt, c0=c0:
                                             g.dma_start(
                                                 out=dst_x[r0:r0 + 128, c0:c0 + 512],
                                                 in_=xr[ttt % 2][:]),
                                             rd=[xb2],
                                             wr=[f"{dst_res}.{r0}.{ocg}.{half}"],
                                             tag=f"astx{ocg}.{tt2}.{half}.{ttt}",
                                             ch=f"st_{dst_res}")
                                if dst_x is y_out:
                                    final_deps.append(stx)
                    p.barrier()
            return

        def emit_ffn(call, src_x, dst_x, src_res, dst_res):
            with ExitStack() as fs:
                fsb = lambda name, shape, dt=F32: fs.enter_context(
                    nc.sbuf_tensor(f"f{call}{name}", list(shape), dt))
                lnfm = fsb("lnfm", [128, 16 * 1024], BF16)
                lnv = lnfm[:].rearrange("p (k n) -> p k n", k=16)
                xt = fsb("xt", [128, D])
                st24 = fsb("st24", [128, 24])
                mv = fsb("mv", [128, 2])
                rstd = fsb("rstd", [128, 2])
                xnb = fsb("xnb", [128, D], BF16)
                s_sc = fsb("s_sc", [128, 8])
                s_e1 = fsb("s_e1", [128, 8])
                s_mk = fsb("s_mk", [128, 8])
                s_e2 = fsb("s_e2", [128, 8])
                s_m1 = fsb("s_m1", [128, 1])
                s_m2 = fsb("s_m2", [128, 1])
                s_g1 = fsb("s_g1", [128, 1])
                s_g2 = fsb("s_g2", [128, 1])
                s_t1 = fsb("s_t1", [128, 8])
                combb = fsb("combb", [128, 64], BF16)
                comb_fm = fsb("comb_fm", [8, 1024], BF16)
                comb_bc = fsb("cbc", [128, 1024], BF16)
                w1sb = fsb("w1sb", [128, 16 * HF], BF16)
                wst = fsb("wst", [128, 2 * 704])
                hstage = fsb("hst", [128, 11 * 1024], BF16)
                hsv = hstage[:].rearrange("p (hc n) -> p hc n", hc=11)
                w2sb = [fsb(f"w2sb{i}", [128, 1408], BF16) for i in range(2)]
                y_acc = fsb("yac", [128, 16 * 1024])

                wslot = [0]

                def w_load(src_ap, tag):
                    sl = wslot[0] % 2
                    wslot[0] += 1
                    p.step("syq", lambda sy, src_ap=src_ap, sl=sl: sy.dma_start(
                        out=wst[:, sl * 704:(sl + 1) * 704], in_=src_ap),
                        rd=[], wr=[f"wst{sl}"], tag=tag, ch=f"wst{call}{sl}")
                    return sl

                def w2_pair(e, ot):
                    wi = ot % 2
                    for hf in range(2):
                        sl = w_load(w2R[e, ot, :, hf * 704:(hf + 1) * 704],
                                    f"w2ld{e}.{ot}.{hf}")
                        p.step("dve", lambda v, sl=sl, wi=wi, hf=hf: v.tensor_copy(
                            w2sb[wi][:, hf * 704:(hf + 1) * 704],
                            wst[:, sl * 704:(sl + 1) * 704]),
                            rd=[f"wst{sl}"], wr=[f"w2sb{wi}"],
                            tag=f"w2cast{e}.{ot}.{hf}")

                def w1_pair(e, k, extra=()):
                    for hf in range(2):
                        sl = w_load(w1T[e, k * 128:(k + 1) * 128,
                                        hf * 704:(hf + 1) * 704],
                                    f"w1ld{e}.{k}.{hf}")
                        p.step("dve", lambda v, k=k, sl=sl, hf=hf: v.tensor_copy(
                            w1sb[:, k * HF + hf * 704:k * HF + (hf + 1) * 704],
                            wst[:, sl * 704:(sl + 1) * 704]),
                            rd=[f"wst{sl}"],
                            wr=["w1rdy"] if (k == 15 and hf == 1) else [f"w1c{k}"],
                            extra=list(extra) if hf == 0 else [],
                            tag=f"w1cast{e}.{k}.{hf}")

                for ttt in range(8):
                    r0 = ttt * 128
                    p.step("scq", lambda s, r0=r0: s.dma_start(
                        out=xt[:], in_=src_x[r0:r0 + 128, :]),
                        rd=[f"{src_res}.{r0}.0.0", f"{src_res}.{r0}.0.1",
                            f"{src_res}.{r0}.1.0", f"{src_res}.{r0}.1.1"],
                        wr=["xtb"], tag=f"ldx{ttt}", ch=f"xt{call}")
                    for q in range(4):
                        p.step("dve", lambda v, q=q: v.bn_stats(
                            st24[:, q * 6:(q + 1) * 6],
                            xt[:, q * 512:(q + 1) * 512]),
                            rd=["xtb"], wr=[f"st24.{q}"], tag=f"bn{ttt}.{q}", wb=True)
                    p.step("dve", lambda v: v.bn_aggr(mv[:], st24[:]),
                           rd=["st24.0", "st24.1", "st24.2", "st24.3"],
                           wr=["mv"], tag="aggr", wb=True)
                    p.step("act", lambda s: s.activation(
                        rstd[:, 0:1], mv[:, 1:2], AF.Ln, bias=c_eps[:, :]),
                        rd=["mv"], wr=["rstd0"])
                    p.step("act", lambda s: s.activation(
                        rstd[:, 1:2], rstd[:, 0:1], AF.Exp, scale=-0.5),
                        rd=["rstd0"], wr=["rstd"])
                    p.step("dve", lambda v: v.tensor_scalar(
                        xnb[:], xt[:], mv[:, 0:1], rstd[:, 1:2],
                        ALU.subtract, ALU.mult),
                        rd=["xtb", "mv", "rstd"], wr=["xnb"], tag=f"nrm{ttt}")
                    for bg in range(4):
                        for q in range(4):
                            k = bg * 4 + q
                            p.step("pe", lambda t, k=k, bg=bg, q=q: t.matmul(
                                PB[bg][:, q * 128:(q + 1) * 128],
                                xnb[:, k * 128:(k + 1) * 128], c_idb[:, :],
                                start=True, stop=True),
                                rd=["xnb"], wr=[f"pb{bg}"], tag=f"trmm{ttt}.{bg}")
                        p.step("act", lambda s, bg=bg, ttt=ttt: s.activation(
                            lnv[:, bg * 4:(bg + 1) * 4, ttt * 128:(ttt + 1) * 128],
                            PB[bg][:, 0:512].rearrange("p (k n) -> p k n", k=4),
                            AF.Copy),
                            rd=[f"pb{bg}"], wr=[f"lnfm.{bg}.{ttt}"],
                            tag=f"trev{ttt}.{bg}")
                for ttt in range(8):
                    for k in range(16):
                        p.step("pe", lambda t, k=k, ttt=ttt: t.matmul(
                            PB[4][:, 0:8],
                            lnv[:, k, ttt * 128:(ttt + 1) * 128],
                            c_gwT[:, k * 8:(k + 1) * 8],
                            start=(k == 0), stop=(k == 15)),
                            rd=[f"lnfm.{k // 4}.{ttt}"],
                            wr=["pb4"] if k in (0, 15) else [],
                            tag=f"scmm{ttt}.{k}")
                    p.step("dve", lambda v: v.tensor_add(
                        s_sc[:], PB[4][:, 0:8], c_gbias[:]),
                        rd=["pb4"], wr=["s_sc"])
                    p.step("dve", lambda v: v.reduce_max(s_m1[:], s_sc[:], axis=AX),
                           rd=["s_sc"], wr=["s_m1"], wb=True)
                    p.step("dve", lambda v: v.tensor_scalar(
                        s_e1[:], s_sc[:], s_m1[:, :], None, ALU.is_equal),
                        rd=["s_sc", "s_m1"], wr=["s_e1"])
                    p.step("dve", lambda v: v.scalar_tensor_tensor(
                        s_mk[:], s_e1[:], -1e30, s_sc[:], ALU.mult, ALU.add),
                        rd=["s_e1", "s_sc"], wr=["s_mk"])
                    p.step("dve", lambda v: v.reduce_max(s_m2[:], s_mk[:], axis=AX),
                           rd=["s_mk"], wr=["s_m2"], wb=True)
                    p.step("dve", lambda v: v.tensor_scalar(
                        s_e2[:], s_mk[:], s_m2[:, :], None, ALU.is_equal),
                        rd=["s_mk", "s_m2"], wr=["s_e2"])
                    p.step("dve", lambda v: v.tensor_sub(s_m1[:], s_m1[:], s_m2[:]),
                           rd=["s_m2"], wr=["s_m1"])
                    p.step("act", lambda s: s.activation(
                        s_g1[:], s_m1[:, :], AF.Sigmoid), rd=["s_m1"], wr=["s_g1"])
                    p.step("act", lambda s: s.activation(
                        s_g2[:], s_m1[:, :], AF.Sigmoid, scale=-1.0),
                        rd=["s_m1"], wr=["s_g2"])
                    p.step("dve", lambda v: v.tensor_scalar(
                        s_t1[:], s_e1[:], s_g1[:, :], None, ALU.mult),
                        rd=["s_e1", "s_g1"], wr=["s_t1"])
                    p.step("dve", lambda v, ttt=ttt: v.scalar_tensor_tensor(
                        combb[:, ttt * 8:(ttt + 1) * 8], s_e2[:], s_g2[:, :], s_t1[:],
                        ALU.mult, ALU.add),
                        rd=["s_e2", "s_g2", "s_t1"], wr=[f"combb.{ttt}"], tag="comb")
                    pbx = 5 + ttt // 4
                    p.step("pe", lambda t, ttt=ttt, pbx=pbx: t.matmul(
                        PB[pbx][0:8, (ttt % 4) * 128:(ttt % 4 + 1) * 128],
                        combb[:, ttt * 8:(ttt + 1) * 8], c_idb[:, :],
                        start=True, stop=True),
                        rd=[f"combb.{ttt}"], wr=[f"pb{pbx}"], tag="cmbT")
                    p.step("dve", lambda v, ttt=ttt, pbx=pbx: v.tensor_copy(
                        comb_fm[:, ttt * 128:(ttt + 1) * 128],
                        PB[pbx][0:8, (ttt % 4) * 128:(ttt % 4 + 1) * 128]),
                        rd=[f"pb{pbx}"], wr=[f"comb_fm.{ttt}"], tag="cfm")
                cfm_all = [f"comb_fm.{i}" for i in range(8)]
                p.barrier()

                for k in range(16):
                    w1_pair(0, k)

                for e in range(NE):
                    p.step("pe", lambda t, e=e: t.matmul(
                        PB[4][:, 0:512], c_selb[0:8, e * 128:(e + 1) * 128],
                        comb_fm[0:8, 0:512], start=True, stop=True),
                        rd=cfm_all[:4], wr=["pb4"], tag=f"cbmm{e}.0")
                    p.step("pe", lambda t, e=e: t.matmul(
                        PB[5][:, 0:512], c_selb[0:8, e * 128:(e + 1) * 128],
                        comb_fm[0:8, 512:1024], start=True, stop=True),
                        rd=cfm_all[4:], wr=["pb5"], tag=f"cbmm{e}.1")
                    p.step("dve", lambda v: v.tensor_copy(
                        comb_bc[:, 0:512], PB[4][:, 0:512]),
                        rd=["pb4"], wr=["cbc0"], tag=f"cbev{e}.0")
                    p.step("dve", lambda v: v.tensor_copy(
                        comb_bc[:, 512:1024], PB[5][:, 0:512]),
                        rd=["pb5"], wr=["cbc1"], tag=f"cbev{e}.1")
                    h_done = None
                    for hc in range(11):
                        for th in range(2):
                            b = (hc * 2 + th) % 2
                            for k in range(16):
                                s = p.step("pe", lambda t, k=k, hc=hc, th=th, b=b:
                                           t.matmul(
                                               PB[b][:, 0:512],
                                               w1sb[:, k * HF + hc * 128:
                                                    k * HF + (hc + 1) * 128],
                                               lnv[:, k, th * 512:(th + 1) * 512],
                                               start=(k == 0), stop=(k == 15)),
                                           rd=["w1rdy"] if k == 0 else [],
                                           wr=[f"pb{b}"] if k in (0, 15) else [],
                                           tag=f"hmm{e}.{hc}.{th}.{k}")
                                if hc == 10 and th == 1 and k == 15:
                                    h_done = s
                            p.step("act", lambda s, e=e, hc=hc, th=th, b=b:
                                   s.activation(
                                       hsv[:, hc, th * 512:(th + 1) * 512],
                                       PB[b][:, 0:512], AF.Silu,
                                       bias=c_b1T[:, hc * 8 + e:hc * 8 + e + 1]),
                                   rd=[f"pb{b}"],
                                   wr=[f"hsi.{hc}.{th}", f"hst.{hc}.{th}"],
                                   tag=f"sil{e}.{hc}.{th}")
                            p.step("dve", lambda v, hc=hc, th=th: v.tensor_mul(
                                hsv[:, hc, th * 512:(th + 1) * 512],
                                hsv[:, hc, th * 512:(th + 1) * 512],
                                comb_bc[:, th * 512:(th + 1) * 512]),
                                rd=[f"hsi.{hc}.{th}", f"cbc{th}"],
                                wr=[f"hst.{hc}.{th}"] +
                                   (["hrdy"] if hc == 10 and th == 1 else []),
                                tag=f"hm{e}.{hc}.{th}")
                    w2_pair(e, 0)
                    w2_pair(e, 1)
                    for ot in range(16):
                        wi = ot % 2
                        for th in range(2):
                            b = 2 + (ot * 2 + th) % 2
                            if e == 0:
                                p.step("pe", lambda t, ot=ot, th=th, b=b, wi=wi:
                                       t.matmul(
                                           PB[b][:, 0:512],
                                           c_b2b[0:8, ot * 128:(ot + 1) * 128],
                                           comb_fm[0:8, th * 512:(th + 1) * 512],
                                           start=True, stop=False),
                                       rd=["hrdy", f"w2sb{wi}"],
                                       wr=[f"pb{b}"], tag=f"ymm{e}.{ot}.{th}.s")
                            for hc in range(11):
                                rdl = []
                                if hc == 0 and e > 0:
                                    rdl += ["hrdy", f"w2sb{wi}"]
                                if hc == 10:
                                    rdl.append(f"w2sb{wi}")
                                if ot == 15:
                                    rdl.append(f"hst.{hc}.{th}")
                                p.step("pe", lambda t, hc=hc, th=th, b=b, wi=wi, e=e:
                                       t.matmul(
                                           PB[b][:, 0:512],
                                           w2sb[wi][:, hc * 128:(hc + 1) * 128],
                                           hsv[:, hc, th * 512:(th + 1) * 512],
                                           start=(hc == 0 and e > 0),
                                           stop=(hc == 10)),
                                       rd=rdl,
                                       wr=[f"pb{b}"] if hc in (0, 10) else [],
                                       tag=f"ymm{e}.{ot}.{th}.{hc}")
                            ysl = y_acc[:, ot * 1024 + th * 512:
                                        ot * 1024 + (th + 1) * 512]
                            if e == 0:
                                p.step("dve", lambda v, ysl=ysl, b=b: v.tensor_copy(
                                    ysl, PB[b][:, 0:512]),
                                    rd=[f"pb{b}"], wr=[f"yac.{ot}.{th}"],
                                    tag=f"yev{e}.{ot}.{th}")
                            else:
                                p.step("dve", lambda v, ysl=ysl, b=b: v.tensor_add(
                                    ysl, ysl, PB[b][:, 0:512]),
                                    rd=[f"pb{b}", f"yac.{ot}.{th}"],
                                    wr=[f"yac.{ot}.{th}"],
                                    tag=f"yev{e}.{ot}.{th}")
                        if ot + 2 < 16:
                            w2_pair(e, ot + 2)
                        if e + 1 < NE:
                            w1_pair(e + 1, ot,
                                    [h_done] if ot == 0 else [])
                for tv in range(8):
                    r0 = tv * 128
                    p.step("scq", lambda s, r0=r0: s.dma_start(
                        out=xt[:], in_=src_x[r0:r0 + 128, :]),
                        rd=[f"{src_res}.{r0}.0.0", f"{src_res}.{r0}.0.1",
                            f"{src_res}.{r0}.1.0", f"{src_res}.{r0}.1.1"],
                        wr=["xtb"], tag=f"ldxr{tv}", ch=f"xt{call}")
                    bks = [4, 5, 6, 7] if tv % 2 == 0 else [0, 1, 2, 3]
                    for g in range(4):
                        b = bks[g]
                        for oq in range(4):
                            ot = g * 4 + oq
                            p.step("pe", lambda t, ot=ot, tv=tv, b=b, oq=oq:
                                   t.transpose(
                                       PB[b][:, oq * 128:(oq + 1) * 128],
                                       y_acc[:, ot * 1024 + tv * 128:
                                             ot * 1024 + tv * 128 + 128],
                                       c_idf[:, :]),
                                   rd=[f"yac.{ot}.{tv // 4}"], wr=[f"pb{b}"],
                                   tag=f"ytr{tv}.{ot}")
                        p.step("dve", lambda v, g=g, b=b: v.scalar_tensor_tensor(
                            xt[:, g * 512:(g + 1) * 512], PB[b][:, 0:512], 0.5,
                            xt[:, g * 512:(g + 1) * 512], ALU.mult, ALU.add),
                            rd=[f"pb{b}", "xtb"], wr=["xtb"], tag=f"yfin{tv}.{g}")
                    stx = p.step("gpq", lambda gq, r0=r0: gq.dma_start(
                        out=dst_x[r0:r0 + 128, :], in_=xt[:]),
                        rd=["xtb"],
                        wr=[f"{dst_res}.{r0}.{a}.{bb}"
                            for a in range(2) for bb in range(2)],
                        tag=f"stx{tv}", ch=f"st_{dst_res}")
                    if dst_x is y_out:
                        final_deps.append(stx)
                p.barrier()
            return

        f32w_attn = True
        emit_ffn(1, x_in, x1d if PHASES > 1 else y_out, "xin", "x1")
        if PHASES > 1:
            p.fence("st_x1")
            emit_attn(x1d, x2d if PHASES > 2 else y_out, "x1", "x2")
        if PHASES > 2:
            p.fence("st_x2", "lnld", "ofl")
            emit_ffn(2, x2d, y_out, "x2", "y")

        p.final_wait(final_deps)
    return nc


def kernel(**inputs):
    in_maps = [host_inputs(inputs, c) for c in range(8)]
    nc = build_program()
    res = run_bass_kernel_spmd(nc, in_maps, core_ids=list(range(8)))
    out = np.zeros((4, 2048, D), dtype=np.float32)
    for c in range(8):
        b, half = c // 2, c % 2
        out[b, half * T:(half + 1) * T, :] = res.results[c]["y_out"]
    return out


# version 04
# speedup vs baseline: 1.2603x; 1.0000x over previous
import numpy as np
import ml_dtypes
import concourse.bass as bass
import concourse.mybir as mybir
from concourse.bass_utils import run_bass_kernel_spmd
from contextlib import ExitStack

F32 = mybir.dt.float32
BF16 = mybir.dt.bfloat16
AF = mybir.ActivationFunctionType
ALU = mybir.AluOpType
AX = mybir.AxisListType.X
PAIRS = [[0, 1], [2, 3], [4, 5], [6, 7]]

T = 1024
D = 2048
HF = 1408
NE = 8

PHASES = 99
A6STOP = False
CUT = None
DBG = False

ENGINES = ["pe", "act", "dve", "pool", "syq", "scq", "gpq"]
DMA_Q = {"syq", "scq", "gpq"}
ENG_OF = {"pe": "tensor", "act": "scalar", "dve": "vector", "pool": "gpsimd",
          "syq": "sync", "scq": "scalar", "gpq": "gpsimd"}


class Step:
    __slots__ = ["eng", "fn", "deps", "inc", "count", "block", "tag", "idx", "wb",
                 "tag_ch"]

    def __init__(self, eng, fn, deps, inc, block, tag, idx, wb=False):
        self.eng, self.fn, self.deps, self.inc = eng, fn, deps, inc
        self.count = None
        self.block = block
        self.tag = tag
        self.idx = idx
        self.wb = wb


class Prog:
    def __init__(self, nc, ctx):
        self.nc = nc
        self.ctx = ctx
        self.sem = {e: ctx.enter_context(nc.semaphore(f"sem_{e}")) for e in ENGINES}
        self.steps = []
        self.cnt = {e: 0 for e in ENGINES}
        self.blk_idx = 0
        self.n_idx = 0
        self.res_w = {}
        self.res_r = {}
        self.chan_sem = {}
        self.chan_cnt = {}
        self.pending_fence = []

    def fence(self, *chs):
        self.pending_fence = [("ch", ch, self.chan_cnt[ch])
                              for ch in chs if ch in self.chan_cnt]

    def _chan(self, name):
        if name not in self.chan_sem:
            self.chan_sem[name] = self.ctx.enter_context(
                self.nc.semaphore(f"ch_{name}"))
            self.chan_cnt[name] = 0
        return self.chan_sem[name]

    def step(self, eng, fn, rd=(), wr=(), extra=(), tag="", wb=False, ch=None):
        raw = []
        for n in rd:
            w = self.res_w.get(n)
            if w is not None:
                raw.append(w)
        for n in wr:
            w = self.res_w.get(n)
            if w is not None:
                raw.append(w)
            raw.extend(self.res_r.get(n, {}).values())
        raw.extend(d for d in extra if d is not None)
        raw.extend(self.pending_fence)
        deps = []
        seen = set()
        for d in raw:
            if isinstance(d, tuple):
                if d[1] in self.chan_sem and d not in seen:
                    seen.add(d)
                    deps.append(d)
            elif d.eng in DMA_Q:
                key = ("ch", d.tag_ch)
                if key not in seen:
                    seen.add(key)
                    deps.append(("ch", d.tag_ch, self.chan_cnt[d.tag_ch]))
            else:
                if id(d) not in seen:
                    seen.add(id(d))
                    deps.append(d)
        if eng in DMA_Q:
            assert ch is not None, f"DMA step needs ch=: {tag}"
            self._chan(ch)
            self.chan_cnt[ch] += 16
            s = Step(eng, fn, deps, 16, self.blk_idx, tag, self.n_idx, wb)
            s.tag_ch = ch
            s.count = self.chan_cnt[ch]
        else:
            s = Step(eng, fn, deps, 1, self.blk_idx, tag, self.n_idx, wb)
            s.tag_ch = None
        self.n_idx += 1
        self.steps.append(s)
        for n in rd:
            self.res_r.setdefault(n, {})[eng] = s
        for n in wr:
            self.res_w[n] = s
            self.res_r[n] = {}
        return s

    def barrier(self):
        if not self.steps:
            self.blk_idx += 1
            return
        steps, self.steps = self.steps, []
        cnt = dict(self.cnt)
        for s in steps:
            if s.eng in DMA_Q:
                continue
            cnt[s.eng] += s.inc
            s.count = cnt[s.eng]
        by_body = {}
        for s in steps:
            by_body.setdefault(ENG_OF[s.eng], []).append(s)
        nc = self.nc
        with nc.Block() as blk:
            for body_name, seq in by_body.items():
                seq.sort(key=lambda s: s.idx)

                def make_body(seq):
                    def body(engobj):
                        hw = {}
                        for s in seq:
                            for d in s.deps:
                                if isinstance(d, tuple):
                                    _, chn, val = d
                                    if val <= 0 or hw.get(("c", chn), 0) >= val:
                                        continue
                                    engobj.wait_ge(self.chan_sem[chn], val)
                                    hw[("c", chn)] = val
                                    continue
                                if d.count is None:
                                    raise RuntimeError(
                                        f"dep not emitted: {d.tag} <- {s.tag}")
                                if (not d.wb) and (
                                        d.block < s.block or
                                        (d.eng == s.eng and d.eng == "pe")):
                                    continue
                                if hw.get(d.eng, 0) >= d.count:
                                    continue
                                engobj.wait_ge(self.sem[d.eng], d.count)
                                hw[d.eng] = d.count
                            instr = s.fn(engobj)
                            if instr is None:
                                raise RuntimeError(f"step returned None: {s.tag}")
                            if s.eng in DMA_Q:
                                instr.then_inc(self.chan_sem[s.tag_ch], 16)
                            else:
                                instr.then_inc(self.sem[s.eng], s.inc)
                    return body
                getattr(blk, body_name)(make_body(seq))
        self.cnt = cnt
        self.blk_idx += 1
        self.pending_fence = []

    def final_wait(self, deps):
        assert not self.steps
        nc = self.nc
        chans = {}
        for d in deps:
            assert d.eng in DMA_Q
            chans[d.tag_ch] = self.chan_cnt[d.tag_ch]
        with nc.Block() as blk:
            @blk.sync
            def _(sy):
                for chn, val in chans.items():
                    sy.wait_ge(self.chan_sem[chn], val)
        self.blk_idx += 1


def _poff_vec(half):
    offs = np.zeros(80, dtype=np.int64)
    for t3 in range(3):
        for r in range(2):
            offs[t3 * 2 + r] = r * 6144 * 1024 + (t3 * 2048 + half * 1024) * 1024
    for r in range(2):
        offs[6 + r] = r * 1024 * 2048 + half * 1024
    return offs.astype(np.int32).reshape(1, 80)


def host_inputs(inputs, core):
    b, half = core // 2, core % 2
    x = np.ascontiguousarray(
        np.asarray(inputs["L0_attn_x"], dtype=np.float32)[b, half * T:(half + 1) * T, :])
    eye_b = np.eye(128, dtype=ml_dtypes.bfloat16)
    eye_f = np.eye(128, dtype=np.float32)
    jj, ii = np.meshgrid(np.arange(128), np.arange(128), indexing="ij")
    maskS = np.where(ii > jj, 0.0, -1e30).astype(np.float32)
    maskI = np.where(ii >= jj, 0.0, -1e30).astype(np.float32)
    selb = np.zeros((8, 8 * 128), dtype=ml_dtypes.bfloat16)
    for e in range(8):
        selb[e, e * 128:(e + 1) * 128] = 1.0
    f32 = lambda a: np.asarray(a, dtype=np.float32)
    cont = np.ascontiguousarray
    return dict(
        x_in=x,
        wqkvT=cont(f32(inputs["L0_attn_W_qkv"]).T),
        conv_w=cont(f32(inputs["L0_attn_conv_w"]).reshape(48, 128, 4)
                    .transpose(1, 0, 2).reshape(128, 192)),
        wabT=cont(f32(inputs["L0_attn_W_ab"]).T.reshape(16, 128, 32)
                  .transpose(1, 0, 2).reshape(128, 512)),
        b_ab=cont(np.broadcast_to(f32(inputs["L0_attn_b_ab"]).reshape(1, 32),
                                  (128, 32))),
        wgT=cont(f32(inputs["L0_attn_W_g"]).T),
        woT=cont(f32(inputs["L0_attn_W_o"]).T),
        gate_wT=cont(f32(inputs["L0_ffn_gate_w"]).T.reshape(16, 128, 8)
                     .transpose(1, 0, 2).reshape(128, 128)),
        gate_b=cont(np.broadcast_to(f32(inputs["L0_ffn_gate_b"]).reshape(1, 8),
                                    (128, 8))),
        gate_bias=cont(np.broadcast_to(f32(inputs["L0_ffn_bias"]).reshape(1, 8),
                                       (128, 8))),
        w1T=cont(f32(inputs["L0_ffn_w1"]).transpose(0, 2, 1)),
        b1T=cont(f32(inputs["L0_ffn_b1"]).T.reshape(11, 128, 8)
                 .transpose(1, 0, 2).reshape(128, 88)),
        w2R=cont(f32(inputs["L0_ffn_w2"]).transpose(0, 2, 1)
                 .reshape(NE, 11, 128, 16, 128).transpose(0, 3, 2, 1, 4)
                 .reshape(NE, 16, 128, 11 * 128)),
        b2=cont(f32(inputs["L0_ffn_b2"])),
        idb=eye_b, idf=eye_f, maskS=maskS, maskI=maskI,
        onesb=np.ones((8, 128), dtype=ml_dtypes.bfloat16),
        onescolb=np.ones((128, 1), dtype=ml_dtypes.bfloat16),
        poff=_poff_vec(half),
        onesf=np.ones((1, 128), dtype=np.float32),
        ones8=np.ones((8, 128), dtype=np.float32),
        selb=selb,
        m3=np.full((128, 1), float(half), dtype=np.float32),
        epscol=np.full((128, 1), 1e-5, dtype=np.float32),
        m3w=np.full((128, 128), float(half), dtype=np.float32),
    )


def build_program():
    nc = bass.Bass("TRN2", target_bir_lowering=False)

    def din(name, shape, dt=F32):
        return nc.dram_tensor(name, list(shape), dt, kind="ExternalInput").ap()

    x_in = din("x_in", [T, D])
    wqkvT = din("wqkvT", [D, 3 * D])
    conv_w = din("conv_w", [128, 192])
    wabT = din("wabT", [128, 512])
    b_ab = din("b_ab", [128, 32])
    wgT = din("wgT", [D, D])
    woT = din("woT", [D, D])
    gate_wT = din("gate_wT", [128, 128])
    gate_b = din("gate_b", [128, NE])
    gate_bias = din("gate_bias", [128, NE])
    w1T = din("w1T", [NE, D, HF])
    b1T = din("b1T", [128, 88])
    w2R = din("w2R", [NE, 16, 128, 11 * 128])
    b2 = din("b2", [NE, D])
    idb_i = din("idb", [128, 128], BF16)
    idf_i = din("idf", [128, 128])
    maskS_i = din("maskS", [128, 128])
    maskI_i = din("maskI", [128, 128])
    onesb_i = din("onesb", [8, 128], BF16)
    onescolb_i = din("onescolb", [128, 1], BF16)
    poff_i = din("poff", [1, 80], mybir.dt.int32)
    onesf_i = din("onesf", [1, 128])
    ones8_i = din("ones8", [8, 128])
    selb_i = din("selb", [8, 8 * 128], BF16)
    m3_i = din("m3", [128, 1])
    eps_i = din("epscol", [128, 1])
    m3w_i = din("m3w", [128, 128])

    y_out = nc.dram_tensor("y_out", [T, D], F32, kind="ExternalOutput").ap()

    def dint(name, shape, dt=F32):
        return nc.dram_tensor(name, list(shape), dt, kind="Internal").ap()

    x1d = dint("x1d", [T, D])
    x2d = dint("x2d", [T, D])
    w1b = dint("w1b", [NE, D, HF], BF16)
    w2b = dint("w2b", [NE, HF, D], BF16)
    hdram = dint("hdram", [NE, 128, 11 * 512], BF16)
    zpre = dint("zpre", [3 * D, T], BF16)
    lnfm2d = dint("lnfm2d", [128, 16 * 1024], BF16)
    qkv_out_t = nc.dram_tensor("qkv_out", [3 * D, T], BF16, kind="Internal")
    qkv_gat_t = nc.dram_tensor("qkv_gat", [2, 3 * D, T], BF16, kind="Internal")
    qkv_out = qkv_out_t.ap()
    qkv_gat = qkv_gat_t.ap()
    ab_out = dint("ab_out", [T, 32])
    ab_gat = dint("ab_gat", [2, T, 32])
    ztail_in = dint("ztail_in", [128, 144], BF16)
    ztail_gat = dint("ztail_gat", [2, 128, 144], BF16)
    o_out = dint("o_out", [T, D], BF16)
    o_gat_t = nc.dram_tensor("o_gat", [2, T, D], BF16, kind="Internal")
    o_gat = o_gat_t.ap()

    final_deps = []

    ctx = ExitStack()
    with ctx:
        p = Prog(nc, ctx)
        sb = lambda name, shape, dt=F32: ctx.enter_context(
            nc.sbuf_tensor(name, list(shape), dt))
        c_idb = sb("c_idb", [128, 128], BF16)
        c_idf = sb("c_idf", [128, 128])
        c_maskS = sb("c_maskS", [128, 128])
        c_maskI = sb("c_maskI", [128, 128])
        c_onesb = sb("c_onesb", [8, 128], BF16)
        c_onescolb = sb("c_onescolb", [128, 1], BF16)
        c_poff = sb("c_poff", [1, 80], mybir.dt.int32)
        c_onesf = sb("c_onesf", [1, 128])
        c_ones8 = sb("c_ones8", [8, 128])
        c_selb = sb("c_selb", [8, 8 * 128], BF16)
        c_m3 = sb("c_m3", [128, 1])
        c_eps = sb("c_eps", [128, 1])
        c_m3w = sb("c_m3w", [128, 128])
        c_cw = sb("c_cw", [128, 48 * 4])
        c_b1T = sb("c_b1T", [128, 11 * 8])
        c_b2b = sb("c_b2b", [8, D], BF16)
        c_gwT = sb("c_gwT", [128, 16 * 8], BF16)
        c_wabT = sb("c_wabT", [128, 16 * 32], BF16)
        c_gbias = sb("c_gbias", [128, 8])
        c_abbias = sb("c_abbias", [128, 32])
        c_scr = sb("c_scr", [128, 1024])

        PB = [ctx.enter_context(nc.psum_tensor(f"pb{i}", [128, 512], F32))
              for i in range(8)]

        for dst, src in [
            (c_idb[:], idb_i), (c_idf[:], idf_i), (c_maskS[:], maskS_i),
            (c_maskI[:], maskI_i), (c_onesb[:], onesb_i), (c_onesf[:], onesf_i),
            (c_ones8[:], ones8_i), (c_selb[:], selb_i), (c_m3[:], m3_i),
            (c_eps[:], eps_i), (c_onescolb[:], onescolb_i), (c_poff[:], poff_i),
            (c_m3w[:], m3w_i),
            (c_cw[:], conv_w),
            (c_b1T[:], b1T),
        ]:
            p.step("syq", lambda sy, d=dst, s=src: sy.dma_start(out=d, in_=s), tag="cload", ch="c")
        if CUT == "c1":
            p.barrier()
            p.final_wait(final_deps)
            return nc
        for h2 in range(2):
            p.step("syq", lambda sy, h2=h2: sy.dma_start(
                out=c_scr[0:8, 0:1024], in_=b2[:, h2 * 1024:(h2 + 1) * 1024]),
                wr=["scr"], tag="b2", ch="c")
            p.step("dve", lambda v, h2=h2: v.tensor_copy(
                c_b2b[:, h2 * 1024:(h2 + 1) * 1024], c_scr[0:8, 0:1024]), rd=["scr"])
        p.barrier()
        if CUT == "c2":
            p.final_wait(final_deps)
            return nc
        p.step("syq", lambda sy: sy.dma_start(
            out=c_scr[:, 0:128], in_=gate_wT),
            wr=["scr1"], tag="gw", ch="c")
        p.step("syq", lambda sy: sy.dma_start(
            out=c_scr[:, 128:640], in_=wabT),
            wr=["scr2"], tag="ab", ch="c")
        p.step("syq", lambda sy: sy.dma_start(out=c_gbias[:], in_=gate_b),
               wr=["scr3"], ch="c")
        p.step("syq", lambda sy: sy.dma_start(out=c_scr[:, 640:648], in_=gate_bias),
               wr=["scr4"], ch="c")
        p.step("syq", lambda sy: sy.dma_start(out=c_abbias[:], in_=b_ab),
               wr=["scr5"], ch="c")
        p.step("dve", lambda v: v.tensor_copy(c_gwT[:], c_scr[:, 0:128]), rd=["scr1"])
        p.step("dve", lambda v: v.tensor_copy(c_wabT[:], c_scr[:, 128:640]), rd=["scr2"])
        p.step("dve", lambda v: v.tensor_add(
            c_gbias[:], c_gbias[:], c_scr[:, 640:648]),
            rd=["scr4"], wr=["scr3"], tag="gbsum")
        p.barrier()

        if CUT == "consts":
            p.final_wait(final_deps)
            return nc

        def emit_ffn(call, src_x, dst_x, src_res, dst_res):
            f32w = (call == 1)
            with ExitStack() as fs:
                fsb = lambda name, shape, dt=F32: fs.enter_context(
                    nc.sbuf_tensor(f"f{call}{name}", list(shape), dt))
                lnfm = fsb("lnfm", [128, 16 * 512], BF16)
                lnv = lnfm[:].rearrange("p (k n) -> p k n", k=16)
                xt = [fsb(f"xt{i}", [128, D]) for i in range(2)]
                st24 = fsb("st24", [128, 24])
                mv = fsb("mv", [128, 2])
                rstd = fsb("rstd", [128, 2])
                xnb = fsb("xnb", [128, D], BF16)
                s_sc = fsb("s_sc", [128, 8])
                s_e1 = fsb("s_e1", [128, 8])
                s_mk = fsb("s_mk", [128, 8])
                s_e2 = fsb("s_e2", [128, 8])
                s_m1 = fsb("s_m1", [128, 1])
                s_m2 = fsb("s_m2", [128, 1])
                s_g1 = fsb("s_g1", [128, 1])
                s_g2 = fsb("s_g2", [128, 1])
                s_t1 = fsb("s_t1", [128, 8])
                combb = fsb("combb", [128, 32], BF16)
                comb_fm = fsb("comb_fm", [8, 512], BF16)
                comb_bc = [fsb(f"cbc{i}", [128, 512], BF16) for i in range(2)]
                w1sb = [fsb(f"w1sb{i}", [128, 16 * HF], BF16) for i in range(2)]
                wstage = fsb("wstage", [128, 2816]) if f32w else None
                hstage = fsb("hstage", [128, 11 * 512], BF16)
                hbuf = [fsb(f"hbuf{i}", [128, 11 * 512], BF16) for i in range(2)]
                w2sb = [fsb(f"w2sb{i}", [128, 1024], BF16) for i in range(2)]
                ystage = fsb("ystage", [128, 4 * 512])
                xr = [fsb(f"xr{i}", [128, 512]) for i in range(2)]

                for tt in range(2):
                    tok0 = tt * 512
                    for ttt in range(4):
                        r0 = tok0 + ttt * 128
                        xb = f"xt{ttt % 2}"
                        p.step("scq", lambda s, r0=r0, ttt=ttt: s.dma_start(
                            out=xt[ttt % 2][:], in_=src_x[r0:r0 + 128, :]),
                            rd=[f"{src_res}.{r0}.0.0", f"{src_res}.{r0}.0.1",
                                f"{src_res}.{r0}.1.0", f"{src_res}.{r0}.1.1"],
                            wr=[xb], tag=f"ldx{tt}.{ttt}", ch=f"xt{ttt % 2}")
                        for q in range(4):
                            p.step("dve", lambda v, q=q, ttt=ttt: v.bn_stats(
                                st24[:, q * 6:(q + 1) * 6],
                                xt[ttt % 2][:, q * 512:(q + 1) * 512]),
                                rd=[xb], wr=[f"st24.{q}"], tag=f"bn{ttt}.{q}", wb=True)
                        p.step("dve", lambda v: v.bn_aggr(mv[:], st24[:]),
                               rd=["st24.0", "st24.1", "st24.2", "st24.3"],
                               wr=["mv"], tag="aggr", wb=True)
                        p.step("act", lambda s: s.activation(
                            rstd[:, 0:1], mv[:, 1:2], AF.Ln, bias=c_eps[:, :]),
                            rd=["mv"], wr=["rstd0"])
                        p.step("act", lambda s: s.activation(
                            rstd[:, 1:2], rstd[:, 0:1], AF.Exp, scale=-0.5),
                            rd=["rstd0"], wr=["rstd"])
                        p.step("dve", lambda v, ttt=ttt: v.tensor_scalar(
                            xnb[:], xt[ttt % 2][:], mv[:, 0:1], rstd[:, 1:2],
                            ALU.subtract, ALU.mult),
                            rd=[xb, "mv", "rstd"], wr=["xnb"], tag=f"nrm{ttt}")
                        for bg in range(4):
                            for q in range(4):
                                k = bg * 4 + q
                                p.step("pe", lambda t, k=k, bg=bg, q=q: t.matmul(
                                    PB[bg][:, q * 128:(q + 1) * 128],
                                    xnb[:, k * 128:(k + 1) * 128], c_idb[:, :],
                                    start=True, stop=True),
                                    rd=["xnb"], wr=[f"pb{bg}"], tag=f"trmm{ttt}.{bg}")
                            p.step("act", lambda s, bg=bg, ttt=ttt: s.activation(
                                lnv[:, bg * 4:(bg + 1) * 4, ttt * 128:(ttt + 1) * 128],
                                PB[bg][:, 0:512].rearrange("p (k n) -> p k n", k=4),
                                AF.Copy),
                                rd=[f"pb{bg}"], wr=[f"lnfm.{bg}.{ttt}"],
                                tag=f"trev{ttt}.{bg}")
                    if CUT == "ln1" and tt == 0:
                        p.barrier()
                        return
                    if CUT == "ln":
                        p.barrier()
                        return
                    for ttt in range(4):
                        for k in range(16):
                            p.step("pe", lambda t, k=k, ttt=ttt: t.matmul(
                                PB[4][:, 0:8],
                                lnv[:, k, ttt * 128:(ttt + 1) * 128],
                                c_gwT[:, k * 8:(k + 1) * 8],
                                start=(k == 0), stop=(k == 15)),
                                rd=[f"lnfm.{k // 4}.{ttt}"],
                                wr=["pb4"] if k in (0, 15) else [],
                                tag=f"scmm{ttt}.{k}")
                        p.step("dve", lambda v: v.tensor_add(
                            s_sc[:], PB[4][:, 0:8], c_gbias[:]),
                            rd=["pb4"], wr=["s_sc"])
                        p.step("dve", lambda v: v.reduce_max(s_m1[:], s_sc[:], axis=AX),
                               rd=["s_sc"], wr=["s_m1"], wb=True)
                        p.step("dve", lambda v: v.tensor_scalar(
                            s_e1[:], s_sc[:], s_m1[:, :], None, ALU.is_equal),
                            rd=["s_sc", "s_m1"], wr=["s_e1"])
                        p.step("dve", lambda v: v.scalar_tensor_tensor(
                            s_mk[:], s_e1[:], -1e30, s_sc[:], ALU.mult, ALU.add),
                            rd=["s_e1", "s_sc"], wr=["s_mk"])
                        p.step("dve", lambda v: v.reduce_max(s_m2[:], s_mk[:], axis=AX),
                               rd=["s_mk"], wr=["s_m2"], wb=True)
                        p.step("dve", lambda v: v.tensor_scalar(
                            s_e2[:], s_mk[:], s_m2[:, :], None, ALU.is_equal),
                            rd=["s_mk", "s_m2"], wr=["s_e2"])
                        p.step("dve", lambda v: v.tensor_sub(s_m1[:], s_m1[:], s_m2[:]),
                               rd=["s_m2"], wr=["s_m1"])
                        p.step("act", lambda s: s.activation(
                            s_g1[:], s_m1[:, :], AF.Sigmoid), rd=["s_m1"], wr=["s_g1"])
                        p.step("act", lambda s: s.activation(
                            s_g2[:], s_m1[:, :], AF.Sigmoid, scale=-1.0),
                            rd=["s_m1"], wr=["s_g2"])
                        p.step("dve", lambda v: v.tensor_scalar(
                            s_t1[:], s_e1[:], s_g1[:, :], None, ALU.mult),
                            rd=["s_e1", "s_g1"], wr=["s_t1"])
                        p.step("dve", lambda v, ttt=ttt: v.scalar_tensor_tensor(
                            combb[:, ttt * 8:(ttt + 1) * 8], s_e2[:], s_g2[:, :], s_t1[:],
                            ALU.mult, ALU.add),
                            rd=["s_e2", "s_g2", "s_t1"], wr=[f"combb.{ttt}"], tag="comb")
                        p.step("pe", lambda t, ttt=ttt: t.matmul(
                            PB[5][0:8, ttt * 128:(ttt + 1) * 128],
                            combb[:, ttt * 8:(ttt + 1) * 8], c_idb[:, :],
                            start=True, stop=True),
                            rd=[f"combb.{ttt}"], wr=["pb5"], tag="cmbT")
                        p.step("dve", lambda v, ttt=ttt: v.tensor_copy(
                            comb_fm[:, ttt * 128:(ttt + 1) * 128],
                            PB[5][0:8, ttt * 128:(ttt + 1) * 128]),
                            rd=["pb5"], wr=[f"comb_fm.{ttt}"], tag="cfm")
                    cfm_all = ["comb_fm.0", "comb_fm.1", "comb_fm.2", "comb_fm.3"]
                    if CUT == "scores":
                        p.barrier()
                        return
                    p.barrier()

                    for e in range(NE):
                        wbuf = f"w1sb{e % 2}"
                        for k in range(16):
                            if f32w and tt == 0:
                                sl = k % 2
                                p.step("syq", lambda sy, e=e, k=k, sl=sl: sy.dma_start(
                                    out=wstage[:, sl * 1408:(sl + 1) * 1408],
                                    in_=w1T[e, k * 128:(k + 1) * 128, :]),
                                    wr=[f"wst{sl}"], tag=f"w1ld{e}.{k}", ch=f"wst{sl}")
                                p.step("dve", lambda v, e=e, k=k, sl=sl: v.tensor_copy(
                                    w1sb[e % 2][:, k * HF:(k + 1) * HF],
                                    wstage[:, sl * 1408:(sl + 1) * 1408]),
                                    rd=[f"wst{sl}"],
                                    wr=([wbuf] if k in (0, 15) else []) + [f"{wbuf}.k{k}"],
                                    tag=f"w1cast{e}.{k}")
                                p.step("gpq", lambda g, e=e, k=k: g.dma_start(
                                    out=w1b[e, k * 128:(k + 1) * 128, :],
                                    in_=w1sb[e % 2][:, k * HF:(k + 1) * HF]),
                                    rd=[f"{wbuf}.k{k}"], wr=[f"w1b{e}"] if k == 15 else [],
                                    tag=f"w1wb{e}.{k}", ch="wb1")
                            else:
                                p.step("syq", lambda sy, e=e, k=k: sy.dma_start(
                                    out=w1sb[e % 2][:, k * HF:(k + 1) * HF],
                                    in_=w1b[e, k * 128:(k + 1) * 128, :]),
                                    rd=[f"w1b{e}"] if k == 0 else [],
                                    wr=[wbuf] if k in (0, 15) else [],
                                    tag=f"w1ld{e}.{k}", ch=f"w1{e % 2}")
                        p.step("pe", lambda t, e=e: t.matmul(
                            PB[4][:, 0:512], c_selb[0:8, e * 128:(e + 1) * 128],
                            comb_fm[0:8, :], start=True, stop=True),
                            rd=cfm_all, wr=["pb4"], tag=f"cbcmm{e}")
                        p.step("dve", lambda v, e=e: v.tensor_copy(
                            comb_bc[e % 2][:], PB[4][:, 0:512]),
                            rd=["pb4"], wr=[f"cbc{e % 2}"], tag=f"cbcev{e}")
                        for hc in range(11):
                            bank = 6 + (e * 11 + hc) % 2
                            for k in range(16):
                                p.step("pe", lambda t, k=k, hc=hc, e=e, bank=bank:
                                       t.matmul(PB[bank][:, 0:512],
                                                w1sb[e % 2][:, k * HF + hc * 128:
                                                            k * HF + (hc + 1) * 128],
                                                lnv[:, k, :],
                                                start=(k == 0), stop=(k == 15)),
                                       rd=[wbuf] if k in (0, 15) else [],
                                       wr=[f"pb{bank}"] if k in (0, 15) else [],
                                       tag=f"hmm{e}.{hc}.{k}")
                            p.step("act", lambda s, e=e, hc=hc, bank=bank: s.activation(
                                hstage[:, hc * 512:(hc + 1) * 512], PB[bank][:, 0:512],
                                AF.Silu, bias=c_b1T[:, hc * 8 + e: hc * 8 + e + 1]),
                                rd=[f"pb{bank}"], wr=[f"hstage.{hc}"], tag=f"sil{e}.{hc}")
                            p.step("dve", lambda v, hc=hc, e=e: v.tensor_mul(
                                hstage[:, hc * 512:(hc + 1) * 512],
                                hstage[:, hc * 512:(hc + 1) * 512], comb_bc[e % 2][:]),
                                rd=[f"cbc{e % 2}"], wr=[f"hstage.{hc}"],
                                tag=f"hm{e}.{hc}")
                        p.step("gpq", lambda g, e=e: g.dma_start(
                            out=hdram[e], in_=hstage[:]),
                            rd=[f"hstage.{hc2}" for hc2 in range(11)],
                            wr=[f"hdram{e}"], tag=f"hst{e}", ch="hd")
                    if CUT == "h":
                        p.barrier()
                        return
                    p.barrier()

                    for ocg in range(2):
                        ymm_last = None
                        for oc in range(8):
                            goc = ocg * 8 + oc
                            p.step("pe", lambda t, oc=oc, goc=goc: t.matmul(
                                PB[oc][:, 0:512], c_b2b[0:8, goc * 128:(goc + 1) * 128],
                                comb_fm[0:8, :], start=True, stop=False,
                                skip_group_check=True),
                                rd=cfm_all, wr=[f"pb{oc}"], tag=f"yb2.{ocg}.{oc}")
                        for e in range(NE):
                            p.step("gpq", lambda g, e=e: g.dma_start(
                                out=hbuf[e % 2][:], in_=hdram[e]),
                                rd=[f"hdram{e}"], wr=[f"hbuf{e % 2}"],
                                tag=f"hld{ocg}.{e}", ch=f"hb{e % 2}")
                            for hc in range(11):
                                wi = (e * 11 + hc) % 2
                                if f32w and tt == 0:
                                    p.step("syq", lambda sy, e=e, hc=hc, ocg=ocg, wi=wi:
                                           sy.dma_start(
                                               out=wstage[:, wi * 1408:wi * 1408 + 1024],
                                               in_=w2T[e, hc * 128:(hc + 1) * 128,
                                                       ocg * 1024:(ocg + 1) * 1024]),
                                           wr=[f"wst{wi}"], tag=f"w2ld{ocg}.{e}.{hc}",
                                           ch=f"wst{wi}")
                                    p.step("dve", lambda v, wi=wi: v.tensor_copy(
                                        w2sb[wi][:], wstage[:, wi * 1408:wi * 1408 + 1024]),
                                        rd=[f"wst{wi}"], wr=[f"w2sb{wi}"],
                                        tag=f"w2cast{ocg}.{e}.{hc}")
                                    p.step("gpq", lambda g, e=e, hc=hc, ocg=ocg, wi=wi:
                                           g.dma_start(
                                               out=w2b[e, hc * 128:(hc + 1) * 128,
                                                       ocg * 1024:(ocg + 1) * 1024],
                                               in_=w2sb[wi][:]),
                                           rd=[f"w2sb{wi}"], wr=[f"w2b{e}.{ocg}"],
                                           tag=f"w2wb{ocg}.{e}.{hc}", ch="wb2")
                                else:
                                    p.step("syq", lambda sy, e=e, hc=hc, ocg=ocg, wi=wi:
                                           sy.dma_start(
                                               out=w2sb[wi][:],
                                               in_=w2b[e, hc * 128:(hc + 1) * 128,
                                                       ocg * 1024:(ocg + 1) * 1024]),
                                           rd=[f"w2b{e}.{ocg}"] if hc == 0 else [],
                                           wr=[f"w2sb{wi}"],
                                           tag=f"w2ld{ocg}.{e}.{hc}", ch=f"w2{wi}")
                                for oc in range(8):
                                    s = p.step("pe", lambda t, oc=oc, hc=hc, wi=wi, e=e:
                                               t.matmul(PB[oc][:, 0:512],
                                                        w2sb[wi][:, oc * 128:(oc + 1) * 128],
                                                        hbuf[e % 2][:, hc * 512:(hc + 1) * 512],
                                                        start=False,
                                                        stop=(e == NE - 1 and hc == 10),
                                                        skip_group_check=True),
                                               rd=([f"w2sb{wi}", f"hbuf{e % 2}"]
                                                   if oc == 0 else []) +
                                               ([f"w2sb{wi}", f"hbuf{e % 2}"]
                                                if oc == 7 else []),
                                               wr=[f"pb{oc}"] if (e == NE - 1 and hc == 10)
                                               else [],
                                               tag=f"ymm{ocg}.{e}.{hc}.{oc}")
                                    ymm_last = s
                        for half in range(2):
                            for q4 in range(4):
                                oc = half * 4 + q4
                                p.step("act", lambda s, oc=oc, q4=q4: s.activation(
                                    ystage[:, q4 * 512:(q4 + 1) * 512], PB[oc][:, 0:512],
                                    AF.Copy, scale=0.5),
                                    rd=[f"pb{oc}"], extra=[ymm_last],
                                    wr=[f"ystage.{q4}"], tag=f"yev{ocg}.{oc}")
                            for ttt in range(4):
                                r0 = tok0 + ttt * 128
                                c0 = ocg * 1024 + half * 512
                                bank = ttt if half == 0 else 4 + ttt
                                xb2 = f"xr{ttt % 2}"
                                p.step("scq", lambda s, r0=r0, ttt=ttt, c0=c0: s.dma_start(
                                    out=xr[ttt % 2][:], in_=src_x[r0:r0 + 128, c0:c0 + 512]),
                                    rd=[f"{src_res}.{r0}.{ocg}.0",
                                        f"{src_res}.{r0}.{ocg}.1"],
                                    wr=[xb2], tag=f"ldxr{ocg}.{half}.{ttt}",
                                    ch=f"xr{ttt % 2}")
                                for q in range(4):
                                    p.step("pe", lambda t, q=q, ttt=ttt, bank=bank:
                                           t.transpose(PB[bank][:, q * 128:(q + 1) * 128],
                                                       ystage[:, q * 512 + ttt * 128:
                                                               q * 512 + (ttt + 1) * 128],
                                                       c_idf[:, :]),
                                           rd=[f"ystage.{q}"], wr=[f"pb{bank}"],
                                           tag=f"ytr{ocg}.{ttt}.{half}.{q}")
                                p.step("dve", lambda v, ttt=ttt, bank=bank:
                                       v.tensor_add(xr[ttt % 2][:], xr[ttt % 2][:],
                                                    PB[bank][:, 0:512]),
                                       rd=[f"pb{bank}"], wr=[xb2],
                                       tag=f"yadd{ocg}.{ttt}.{half}")
                                stx = p.step("gpq", lambda g, r0=r0, ttt=ttt, c0=c0:
                                             g.dma_start(
                                                 out=dst_x[r0:r0 + 128, c0:c0 + 512],
                                                 in_=xr[ttt % 2][:]),
                                             rd=[xb2],
                                             wr=[f"{dst_res}.{r0}.{ocg}.{half}"],
                                             tag=f"stx{ocg}.{half}.{ttt}",
                                             ch=f"st_{dst_res}")
                                if dst_x is y_out:
                                    final_deps.append(stx)
                    if CUT == "y":
                        p.barrier()
                        return
                    p.barrier()
            return


        def emit_attn(src_x, dst_x, src_res, dst_res):
            NH = 8
            with ExitStack() as s1:
                asb = lambda name, shape, dt=F32: s1.enter_context(
                    nc.sbuf_tensor(f"a1{name}", list(shape), dt))
                lnfm2 = asb("lnfm2", [128, 16 * 1024], BF16)
                ln2v = lnfm2[:].rearrange("p (k n) -> p k n", k=16)
                xt = [asb(f"xt{i}", [128, D]) for i in range(2)]
                st24 = asb("st24", [128, 24])
                mv = asb("mv", [128, 2])
                rstd = asb("rstd", [128, 2])
                xnb = asb("xnb", [128, D], BF16)
                wqb = [asb(f"wqb{i}", [128, 16 * 1024], BF16) for i in range(2)]
                wstg = [asb(f"wstg{i}", [128, 1024]) for i in range(2)]
                zstage = [asb(f"zst{i}", [128, 512], BF16) for i in range(2)]
                ztail_sb = asb("ztail", [128, 144], BF16)
                abt = asb("abt", [128, 8 * 32])

                for ttt in range(8):
                    r0 = ttt * 128
                    xb = f"axt{ttt % 2}"
                    p.step("scq", lambda s, r0=r0, ttt=ttt: s.dma_start(
                        out=xt[ttt % 2][:], in_=src_x[r0:r0 + 128, :]),
                        rd=[f"{src_res}.{r0}.0.0", f"{src_res}.{r0}.0.1",
                            f"{src_res}.{r0}.1.0", f"{src_res}.{r0}.1.1"],
                        wr=[xb], tag=f"aldx{ttt}", ch=f"axt{ttt % 2}")
                    for q in range(4):
                        p.step("dve", lambda v, q=q, ttt=ttt: v.bn_stats(
                            st24[:, q * 6:(q + 1) * 6],
                            xt[ttt % 2][:, q * 512:(q + 1) * 512]),
                            rd=[xb], wr=[f"ast24.{q}"], tag=f"abn{ttt}.{q}", wb=True)
                    p.step("dve", lambda v: v.bn_aggr(mv[:], st24[:]),
                           rd=["ast24.0", "ast24.1", "ast24.2", "ast24.3"],
                           wr=["amv"], tag="aaggr", wb=True)
                    p.step("act", lambda s: s.activation(
                        rstd[:, 0:1], mv[:, 1:2], AF.Ln, bias=c_eps[:, :]),
                        rd=["amv"], wr=["arstd0"])
                    p.step("act", lambda s: s.activation(
                        rstd[:, 1:2], rstd[:, 0:1], AF.Exp, scale=-0.5),
                        rd=["arstd0"], wr=["arstd"])
                    p.step("dve", lambda v, ttt=ttt: v.tensor_scalar(
                        xnb[:], xt[ttt % 2][:], mv[:, 0:1], rstd[:, 1:2],
                        ALU.subtract, ALU.mult),
                        rd=[xb, "amv", "arstd"], wr=["axnb"], tag=f"anrm{ttt}")
                    for bg in range(4):
                        for q in range(4):
                            k = bg * 4 + q
                            p.step("pe", lambda t, k=k, bg=bg, q=q: t.matmul(
                                PB[bg][:, q * 128:(q + 1) * 128],
                                xnb[:, k * 128:(k + 1) * 128], c_idb[:, :],
                                start=True, stop=True),
                                rd=["axnb"], wr=[f"pb{bg}"], tag=f"atrmm{ttt}.{bg}")
                        p.step("act", lambda s, bg=bg, ttt=ttt: s.activation(
                            ln2v[:, bg * 4:(bg + 1) * 4, ttt * 128:(ttt + 1) * 128],
                            PB[bg][:, 0:512].rearrange("p (k n) -> p k n", k=4),
                            AF.Copy),
                            rd=[f"pb{bg}"], wr=[f"aln.{bg}.{ttt}"],
                            tag=f"atrev{ttt}.{bg}")
                    for k in range(16):
                        p.step("pe", lambda t, k=k, ttt=ttt: t.matmul(
                            PB[4][:, 0:32],
                            ln2v[:, k, ttt * 128:(ttt + 1) * 128],
                            c_wabT[:, k * 32:(k + 1) * 32],
                            start=(k == 0), stop=(k == 15)),
                            rd=[f"aln.{k // 4}.{ttt}"],
                            wr=["pb4"] if k in (0, 15) else [],
                            tag=f"abmm{ttt}.{k}")
                    p.step("dve", lambda v, ttt=ttt: v.tensor_add(
                        abt[:, ttt * 32:(ttt + 1) * 32], PB[4][:, 0:32], c_abbias[:]),
                        rd=["pb4"], wr=[f"aabt.{ttt}"], tag=f"aabadd{ttt}")
                    p.step("gpq", lambda g, ttt=ttt, r0=r0: g.dma_start(
                        out=ab_out[r0:r0 + 128, :], in_=abt[:, ttt * 32:(ttt + 1) * 32]),
                        rd=[f"aabt.{ttt}"], wr=["ab_out_d"], tag=f"abst{ttt}",
                        ch="abst")
                for k in range(16):
                    p.step("gpq", lambda g, k=k: g.dma_start(
                        out=lnfm2d[:, k * 1024:(k + 1) * 1024],
                        in_=lnfm2[:, k * 1024:(k + 1) * 1024]),
                        rd=[f"aln.{k // 4}.{ttt2}" for ttt2 in range(8)],
                        wr=["lnfm2d_d"], tag=f"lnst{k}", ch="lnst")
                p.barrier()

                for j in range(6):
                    jp = j % 2
                    for k in range(16):
                        if f32w_attn:
                            sl = k % 2
                            p.step("syq", lambda sy, j=j, k=k, sl=sl: sy.dma_start(
                                out=wstg[sl][:],
                                in_=wqkvT[k * 128:(k + 1) * 128,
                                          j * 1024:(j + 1) * 1024]),
                                wr=[f"awst{sl}"], tag=f"wqld{j}.{k}", ch=f"awst{sl}")
                            p.step("dve", lambda v, j=j, k=k, sl=sl, jp=jp: v.tensor_copy(
                                wqb[jp][:, k * 1024:(k + 1) * 1024], wstg[sl][:]),
                                rd=[f"awst{sl}"],
                                wr=([f"awqb{jp}"] if k in (0, 15) else []) +
                                [f"awqb{jp}.k{k}"],
                                tag=f"wqcast{j}.{k}")
                    for oc8 in range(8):
                        bank = 5 + (j * 8 + oc8) % 2
                        for tt2 in range(2):
                            for k in range(16):
                                p.step("pe", lambda t, k=k, oc8=oc8, tt2=tt2, jp=jp, bank=bank:
                                       t.matmul(PB[bank][:, 0:512],
                                                wqb[jp][:, k * 1024 + oc8 * 128:
                                                         k * 1024 + (oc8 + 1) * 128],
                                                ln2v[:, k, tt2 * 512:(tt2 + 1) * 512],
                                                start=(k == 0), stop=(k == 15)),
                                       rd=[f"awqb{jp}"] if k in (0, 15) else [],
                                       wr=[f"pb{bank}"] if k in (0, 15) else [],
                                       tag=f"qmm{j}.{oc8}.{tt2}.{k}")
                            p.step("act", lambda s, tt2=tt2, bank=bank: s.activation(
                                zstage[tt2][:], PB[bank][:, 0:512], AF.Copy),
                                rd=[f"pb{bank}"], wr=[f"azst{tt2}"],
                                tag=f"qev{j}.{oc8}.{tt2}")
                            if tt2 == 1:
                                g_ = j * 8 + oc8
                                p.step("dve", lambda v, g_=g_: v.tensor_copy(
                                    ztail_sb[:, g_ * 3:(g_ + 1) * 3],
                                    zstage[1][:, 509:512]),
                                    rd=["azst1"], wr=[f"aztl.{g_}"],
                                    tag=f"ztl{g_}")
                            p.step("gpq", lambda g, g2=j * 8 + oc8, tt2=tt2: g.dma_start(
                                out=zpre[g2 * 128:(g2 + 1) * 128,
                                         tt2 * 512:(tt2 + 1) * 512],
                                in_=zstage[tt2][:]),
                                rd=[f"azst{tt2}"] + ([f"aztl.{j * 8 + oc8}"]
                                                     if tt2 == 1 else []),
                                wr=["zpre_d"], tag=f"zst{j}.{oc8}.{tt2}", ch="zpst")
                p.step("gpq", lambda g: g.dma_start(out=ztail_in, in_=ztail_sb[:]),
                       rd=[f"aztl.{g_}" for g_ in range(48)], wr=["ztail_in_d"],
                       tag="ztlst", ch="ztl")
                p.barrier()

            p.step("pool", lambda g: g.collective_compute(
                "AllGather", ALU.bypass, replica_groups=PAIRS,
                ins=[ztail_in], outs=[ztail_gat]),
                rd=["ztail_in_d"], wr=["ztail_gat_d"], tag="cc_ztl", wb=True)
            p.step("pool", lambda g: g.collective_compute(
                "AllGather", ALU.bypass, replica_groups=PAIRS,
                ins=[ab_out], outs=[ab_gat]),
                rd=["ab_out_d"], wr=["ab_gat_d"], tag="cc_ab", wb=True)
            p.barrier()

            p.fence("lnst", "zpst", "abst", "ztl")
            with ExitStack() as s2:
                asb = lambda name, shape, dt=F32: s2.enter_context(
                    nc.sbuf_tensor(f"a2{name}", list(shape), dt))
                halo = asb("halo", [128, 144], BF16)
                zw = asb("zw", [128, 8 * 1027], BF16)
                zw3 = zw[:].rearrange("p (g n) -> p g n", g=8)
                facc = asb("facc", [128, 1024])
                znorm = asb("znorm", [128, 8 * 1024], BF16)
                zn3 = znorm[:].rearrange("p (g n) -> p g n", g=8)
                sqb = asb("sqb", [128, 1024], BF16)
                rnr = asb("rnr", [8, 1024])
                rnb = asb("rnb", [8, 1024], BF16)

                p.step("gpq", lambda g: g.dma_start(out=halo[:], in_=ztail_gat[0]),
                       rd=["ztail_gat_d"], wr=["ahalo"], tag="halold", ch="halold")
                p.step("dve", lambda v: v.tensor_scalar(
                    halo[:], halo[:], c_m3[:, :], None, ALU.mult),
                    wr=["ahalo"], tag="halomask")
                for j in range(6):
                    for gg in range(8):
                        g_ = j * 8 + gg
                        p.step("gpq", lambda g, g_=g_, gg=gg: g.dma_start(
                            out=zw3[:, gg, 3:1027],
                            in_=zpre[g_ * 128:(g_ + 1) * 128, :]),
                            rd=["zpre_d"], wr=[f"azw.{gg}"], tag=f"zwld{g_}",
                            ch="zwld")
                    for gg in range(8):
                        g_ = j * 8 + gg
                        p.step("dve", lambda v, g_=g_, gg=gg: v.tensor_copy(
                            zw3[:, gg, 0:3], halo[:, g_ * 3:(g_ + 1) * 3]),
                            rd=["ahalo"], wr=[f"azw.{gg}"], tag=f"zwh{g_}")
                    for gg in range(8):
                        g_ = j * 8 + gg
                        p.step("dve", lambda v, gg=gg, g_=g_: v.tensor_scalar(
                            facc[:], zw3[:, gg, 0:1024],
                            c_cw[:, g_ * 4:g_ * 4 + 1], None, ALU.mult),
                            rd=[f"azw.{gg}"], wr=["afacc"], tag=f"cv0.{g_}")
                        for j2 in range(1, 4):
                            p.step("dve", lambda v, gg=gg, g_=g_, j2=j2:
                                   v.scalar_tensor_tensor(
                                       facc[:], zw3[:, gg, j2:j2 + 1024],
                                       c_cw[:, g_ * 4 + j2:g_ * 4 + j2 + 1],
                                       facc[:], ALU.mult, ALU.add),
                                   rd=[f"azw.{gg}"], wr=["afacc"], tag=f"cv{j2}.{g_}")
                        p.step("act", lambda s, gg=gg: s.activation(
                            zn3[:, gg, :], facc[:], AF.Silu),
                            rd=["afacc"], wr=[f"azn.{gg}"], tag=f"cvsilu.{g_}")
                        if j < 4:
                            p.step("act", lambda s, gg=gg: s.activation(
                                sqb[:], zn3[:, gg, :], AF.Square),
                                rd=[f"azn.{gg}"], wr=["asqb"], tag=f"nsq.{g_}")
                            for hh in range(2):
                                p.step("pe", lambda t, hh=hh: t.matmul(
                                    PB[4 + hh][0:1, 0:512], c_onescolb[:, :],
                                    sqb[:, hh * 512:(hh + 1) * 512],
                                    start=True, stop=True),
                                    rd=["asqb"], wr=[f"pb{4 + hh}"],
                                    tag=f"nsum.{g_}.{hh}")
                                p.step("act", lambda s, hh=hh: s.activation(
                                    rnr[0:1, hh * 512:(hh + 1) * 512],
                                    PB[4 + hh][0:1, 0:512], AF.Ln),
                                    rd=[f"pb{4 + hh}"], wr=[f"arnr{hh}"],
                                    tag=f"nln.{g_}.{hh}")
                                p.step("act", lambda s, hh=hh: s.activation(
                                    rnr[0:1, hh * 512:(hh + 1) * 512],
                                    rnr[0:1, hh * 512:(hh + 1) * 512],
                                    AF.Exp, scale=-0.5),
                                    wr=[f"arnr{hh}"], tag=f"nexp.{g_}.{hh}")
                                p.step("dve", lambda v, hh=hh: v.tensor_scalar(
                                    rnb[0:1, hh * 512:(hh + 1) * 512],
                                    rnr[0:1, hh * 512:(hh + 1) * 512],
                                    1e12, None, ALU.min),
                                    rd=[f"arnr{hh}"], wr=[f"arnb{hh}"],
                                    tag=f"nmin.{g_}.{hh}")
                                p.step("pe", lambda t, hh=hh: t.matmul(
                                    PB[6 + hh][:, 0:512], c_onesb[0:1, :],
                                    rnb[0:1, hh * 512:(hh + 1) * 512],
                                    start=True, stop=True),
                                    rd=[f"arnb{hh}"], wr=[f"pb{6 + hh}"],
                                    tag=f"nbc.{g_}.{hh}")
                                p.step("dve", lambda v, gg=gg, hh=hh: v.tensor_mul(
                                    zn3[:, gg, hh * 512:(hh + 1) * 512],
                                    zn3[:, gg, hh * 512:(hh + 1) * 512],
                                    PB[6 + hh][:, 0:512]),
                                    rd=[f"pb{6 + hh}"], wr=[f"azn.{gg}"],
                                    tag=f"nmul.{g_}.{hh}")
                        p.step("gpq", lambda g, g_=g_, gg=gg: g.dma_start(
                            out=qkv_out[g_ * 128:(g_ + 1) * 128, :],
                            in_=zn3[:, gg, :]),
                            rd=[f"azn.{gg}"], wr=["qkv_out_d"], tag=f"qkst{g_}",
                            ch="qkst")
                p.barrier()

            p.step("pool", lambda g: g.collective_compute(
                "AllGather", ALU.bypass, replica_groups=PAIRS,
                ins=[qkv_out], outs=[qkv_gat]),
                rd=["qkv_out_d"], wr=["qkv_gat_d"], tag="cc_qkv", wb=True)
            p.barrier()

            p.fence("qkst", "zwld")
            sA = ExitStack()
            asb = lambda name, shape, dt=F32: sA.enter_context(
                nc.sbuf_tensor(f"a6{name}", list(shape), dt))
            qkvh = asb("qkvh", [128, 24 * 2048], BF16)
            ab_all = asb("aball", [128, 16 * 32])
            alpha_s = asb("alphas", [128, 128])
            beta_s = asb("betas", [128, 128])
            ell_tm = asb("elltm", [128, 128])
            negb_tm = asb("negbtm", [128, 128])
            O_fm = asb("Ofm", [128, 8 * 2048], BF16)
            S0f = asb("S0f", [128, 8 * 128])
            S0b = asb("S0b", [128, 8 * 128], BF16)

            rt = nc.gpsimd.alloc_register("rt")
            qkvh3 = qkvh[:].rearrange("p (s n) -> p s n", s=24)
            for t3 in range(3):
                for r in range(2):
                    i3 = t3 * 2 + r
                    def fnq(g, i3=i3, t3=t3, r=r):
                        g.reg_load(rt, c_poff[0:1, i3:i3 + 1])
                        return g.dma_start(
                            out=qkvh3[:, t3 * 8:(t3 + 1) * 8,
                                      r * 1024:(r + 1) * 1024],
                            in_=bass.AP(qkv_gat_t, rt,
                                        [[T, 128], [128 * T, 8], [1, 1024]]))
                    p.step("gpq", fnq, rd=["qkv_gat_d", "c_poff"], wr=["aqkvh"],
                           tag=f"qhl{t3}.{r}", ch="qhl")
            for tt3 in range(16):
                r, tl = tt3 // 8, tt3 % 8
                p.step("gpq", lambda g, r=r, tl=tl, tt3=tt3: g.dma_start(
                    out=ab_all[:, tt3 * 32:(tt3 + 1) * 32],
                    in_=ab_gat[r, tl * 128:(tl + 1) * 128, :]),
                    rd=["ab_gat_d"], wr=["aaball"], tag=f"abl{tt3}", ch="abl")
            abv = ab_all[:].rearrange("p (t c) -> p t c", t=16)
            alp3 = alpha_s[:].rearrange("p (t c) -> p t c", t=16)
            bet3 = beta_s[:].rearrange("p (t c) -> p t c", t=16)
            p.step("dve", lambda v: v.tensor_sub(
                alp3, abv[:, :, 8:16], abv[:, :, 0:8]),
                rd=["aaball"], wr=["aalpha"], tag="sel_a1")
            p.step("dve", lambda v: v.scalar_tensor_tensor(
                alp3, alp3, c_m3[:, :], abv[:, :, 0:8], ALU.mult, ALU.add),
                rd=["aaball"], wr=["aalpha"], tag="sel_a2")
            p.step("dve", lambda v: v.tensor_sub(
                bet3, abv[:, :, 24:32], abv[:, :, 16:24]),
                rd=["aaball"], wr=["abeta"], tag="sel_b1")
            p.step("dve", lambda v: v.scalar_tensor_tensor(
                bet3, bet3, c_m3[:, :], abv[:, :, 16:24], ALU.mult, ALU.add),
                rd=["aaball"], wr=["abeta"], tag="sel_b2")
            p.step("act", lambda s: s.activation(alpha_s[:], alpha_s[:], AF.Sigmoid),
                   wr=["aalpha"], tag="sig_a")
            p.step("act", lambda s: s.activation(beta_s[:], beta_s[:], AF.Sigmoid),
                   wr=["abeta"], tag="sig_b")
            p.step("act", lambda s: s.activation(ell_tm[:], alpha_s[:], AF.Ln),
                   rd=["aalpha"], wr=["aell"], tag="log_a")
            p.step("dve", lambda v: v.tensor_scalar(
                ell_tm[:], ell_tm[:], -60.0, None, ALU.max),
                wr=["aell"], tag="clamp_a")
            p.step("dve", lambda v: v.tensor_scalar(
                negb_tm[:], beta_s[:], -1.0, None, ALU.mult),
                rd=["abeta"], wr=["anegb"], tag="negb")
            p.step("dve", lambda v: v.memset(S0f[:], 0.0), wr=["aS0f"], tag="s0z")
            p.step("dve", lambda v: v.memset(S0b[:], 0.0), wr=["aS0b"], tag="s0bz")
            p.barrier()

            with ExitStack() as s6:
                csb = lambda name, shape, dt=F32: s6.enter_context(
                    nc.sbuf_tensor(f"a6c{name}", list(shape), dt))
                lell = csb("lell", [8, 128])
                lrow = csb("lrow", [8, 128])
                dtmp = csb("dtmp", [8, 128])
                drow = csb("drow", [8, 128])
                arow = csb("arow", [8, 128])
                acbr = csb("acbr", [8, 128])
                sc_tm = csb("sctm", [128, 32])
                negA8 = csb("negA8", [128, 8])
                lrbuf = csb("lrbuf", [8, 512])
                rsS = csb("rsS", [128, 512])
                rsI = csb("rsI", [128, 512])
                RtS = csb("RtS", [128, 512], BF16)
                RtI = csb("RtI", [128, 512], BF16)
                tDt = csb("tDt", [128, 512], BF16)
                tD = csb("tD", [128, 512], BF16)
                tW = csb("tW", [128, 512], BF16)
                tMt = csb("tMt", [128, 512], BF16)
                tVt = csb("tVt", [128, 512], BF16)
                tO2 = csb("tO2", [128, 512], BF16)
                tOt = csb("tOt", [128, 512], BF16)
                tKb = csb("tKb", [128, 512], BF16)

                bankctr = [0]

                def nb():
                    bankctr[0] = (bankctr[0] + 1) % 8
                    return bankctr[0]

                for c in range(16):
                    p.barrier()
                    b0 = nb()
                    p.step("pe", lambda t, c=c, b0=b0: t.transpose(
                        PB[b0][0:8, 0:128], ell_tm[:, c * 8:(c + 1) * 8], c_idf[:, :]),
                        rd=["aell"], wr=[f"pb{b0}"], tag=f"ellT{c}")
                    p.step("act", lambda s, b0=b0: s.activation(
                        lell[:], PB[b0][0:8, 0:128], AF.Copy),
                        rd=[f"pb{b0}"], wr=["clell"], tag=f"ellcp{c}")
                    p.step("dve", lambda v: v.tensor_tensor_scan(
                        lrow[:], lell[:], lell[:], 0.0, ALU.add, ALU.bypass),
                        rd=["clell"], wr=["clrow"], tag=f"scan{c}", wb=True)
                    p.step("dve", lambda v: v.tensor_scalar(
                        dtmp[:], lrow[:], lrow[:, 127:128], None, ALU.subtract),
                        rd=["clrow"], wr=["cdtmp"], tag=f"dsub{c}")
                    p.step("act", lambda s: s.activation(drow[:], dtmp[:], AF.Exp,
                                                         scale=-1.0),
                           rd=["cdtmp"], wr=["cdrow"], tag=f"dexp{c}")
                    p.step("act", lambda s: s.activation(arow[:], lrow[:], AF.Exp),
                           rd=["clrow"], wr=["carow"], tag=f"aexp{c}")
                    p.step("dve", lambda v: v.tensor_scalar(
                        acbr[:], c_ones8[:], arow[:, 127:128], None, ALU.mult),
                        rd=["carow"], wr=["cacb"], tag=f"acb{c}")
                    b1 = nb()
                    for qi, (rowt, rname) in enumerate(
                            [(drow, "cdrow"), (arow, "carow"),
                             (acbr, "cacb"), (lrow, "clrow")]):
                        p.step("pe", lambda t, rowt=rowt, qi=qi, b1=b1: t.transpose(
                            PB[b1][:, qi * 8:(qi + 1) * 8], rowt[:, :],
                            c_idf[0:8, 0:8]),
                            rd=[rname], wr=[f"pb{b1}"], tag=f"scT{c}.{qi}")
                    p.step("act", lambda s, b1=b1: s.activation(
                        sc_tm[:], PB[b1][:, 0:32], AF.Copy),
                        rd=[f"pb{b1}"], wr=["csc"], tag=f"scev{c}")
                    p.step("dve", lambda v: v.tensor_scalar(
                        negA8[:], sc_tm[:, 8:16], -1.0, None, ALU.mult),
                        rd=["csc"], wr=["cnegA"], tag=f"negA{c}")

                    if A6STOP and A6STOP == (c, -1, 0):
                        p.barrier()
                        s6.pop_all().close()
                        sA.close()
                        return
                    for g2 in range(2):
                        hs = [g2 * 4 + q for q in range(4)]
                        sl = lambda buf, q: buf[:, q * 128:(q + 1) * 128]
                        qf = lambda h, c=c: qkvh[:, (0 * 8 + h) * 2048 + c * 128:
                                                 (0 * 8 + h) * 2048 + (c + 1) * 128]
                        kf = lambda h, c=c: qkvh[:, (1 * 8 + h) * 2048 + c * 128:
                                                 (1 * 8 + h) * 2048 + (c + 1) * 128]
                        vf = lambda h, c=c: qkvh[:, (2 * 8 + h) * 2048 + c * 128:
                                                 (2 * 8 + h) * 2048 + (c + 1) * 128]
                        bkk = nb()
                        bkq = nb()
                        for q, h in enumerate(hs):
                            p.step("pe", lambda t, h=h, q=q, bkk=bkk: t.matmul(
                                PB[bkk][:, q * 128:(q + 1) * 128], kf(h), kf(h),
                                start=True, stop=True),
                                rd=["aqkvh"], wr=[f"pb{bkk}"], tag=f"gkk{c}.{h}")
                            p.step("pe", lambda t, h=h, q=q, bkq=bkq: t.matmul(
                                PB[bkq][:, q * 128:(q + 1) * 128], kf(h), qf(h),
                                start=True, stop=True),
                                rd=["aqkvh"], wr=[f"pb{bkq}"], tag=f"gkq{c}.{h}")
                        if A6STOP and A6STOP == (c, g2, 9):
                            p.barrier()
                            s6.pop_all().close()
                            sA.close()
                            return
                        brow = nb()
                        bbc = nb()
                        for q, h in enumerate(hs):
                            p.step("pe", lambda t, h=h, q=q, brow=brow: t.transpose(
                                PB[brow][0:1, q * 128:(q + 1) * 128],
                                sc_tm[:, 24 + h:25 + h], c_idf[:, :]),
                                rd=["csc"], wr=[f"pb{brow}"], tag=f"lrT{c}.{h}")
                            p.step("act", lambda s, q=q, brow=brow: s.activation(
                                lrbuf[0:1, q * 128:(q + 1) * 128],
                                PB[brow][0:1, q * 128:(q + 1) * 128], AF.Copy),
                                rd=[f"pb{brow}"], wr=[f"clr{q}"], tag=f"lrcp{c}.{q}")
                            p.step("pe", lambda t, q=q, bbc=bbc: t.matmul(
                                PB[bbc][:, q * 128:(q + 1) * 128],
                                c_ones8[0:1, :], lrbuf[0:1, q * 128:(q + 1) * 128],
                                start=True, stop=True),
                                rd=[f"clr{q}"], wr=[f"pb{bbc}"], tag=f"lbc{c}.{q}")
                        for q, h in enumerate(hs):
                            p.step("dve", lambda v, h=h, q=q, bbc=bbc: v.scalar_tensor_tensor(
                                sl(rsS, q), PB[bbc][:, q * 128:(q + 1) * 128],
                                sc_tm[:, 24 + h:25 + h], c_maskS[:, :],
                                ALU.subtract, ALU.add),
                                rd=[f"pb{bbc}", "csc"], wr=["crsS"], tag=f"rsS{c}.{h}")
                            p.step("dve", lambda v, h=h, q=q, bbc=bbc: v.scalar_tensor_tensor(
                                sl(rsI, q), PB[bbc][:, q * 128:(q + 1) * 128],
                                sc_tm[:, 24 + h:25 + h], c_maskI[:, :],
                                ALU.subtract, ALU.add),
                                rd=[f"pb{bbc}", "csc"], wr=["crsI"], tag=f"rsI{c}.{h}")
                        p.step("act", lambda s: s.activation(RtS[:], rsS[:], AF.Exp),
                               rd=["crsS"], wr=["cRtS"], tag=f"expS{c}.{g2}")
                        p.step("act", lambda s: s.activation(RtI[:], rsI[:], AF.Exp),
                               rd=["crsI"], wr=["cRtI"], tag=f"expI{c}.{g2}")
                        for q, h in enumerate(hs):
                            p.step("dve", lambda v, h=h, q=q, c=c, bkk=bkk:
                                   v.scalar_tensor_tensor(
                                       sl(tDt, q), PB[bkk][:, q * 128:(q + 1) * 128],
                                       negb_tm[:, c * 8 + h:c * 8 + h + 1],
                                       sl(RtS, q), ALU.mult, ALU.mult),
                                   rd=[f"pb{bkk}", "cRtS", "anegb"], wr=["cDt"],
                                   tag=f"bt{c}.{h}")
                            p.step("dve", lambda v, q=q, bkq=bkq: v.tensor_mul(
                                sl(tMt, q), PB[bkq][:, q * 128:(q + 1) * 128],
                                sl(RtI, q)),
                                rd=[f"pb{bkq}", "cRtI"], wr=["cMt"], tag=f"mt{c}.{h}")
                        if A6STOP and A6STOP == (c, g2, 1):
                            p.barrier()
                            s6.pop_all().close()
                            sA.close()
                            return
                        bD0 = nb()
                        bVt = nb()
                        bW0 = nb()
                        for q, h in enumerate(hs):
                            p.step("pe", lambda t, q=q, bD0=bD0: t.matmul(
                                PB[bD0][:, q * 128:(q + 1) * 128], sl(tDt, q),
                                c_idb[:, :], start=True, stop=True),
                                rd=["cDt"], wr=[f"pb{bD0}"], tag=f"d0T{c}.{h}")
                            p.step("pe", lambda t, h=h, q=q, bVt=bVt: t.matmul(
                                PB[bVt][:, q * 128:(q + 1) * 128], vf(h),
                                c_idb[:, :], start=True, stop=True),
                                rd=["aqkvh"], wr=[f"pb{bVt}"], tag=f"vT{c}.{h}")
                            p.step("pe", lambda t, h=h, q=q, bW0=bW0: t.matmul(
                                PB[bW0][:, q * 128:(q + 1) * 128], kf(h),
                                S0b[:, h * 128:(h + 1) * 128],
                                start=True, stop=True),
                                rd=["aqkvh", "aS0b"], wr=[f"pb{bW0}"],
                                tag=f"w0{c}.{h}")
                        p.step("dve", lambda v, bD0=bD0: v.tensor_copy(
                            tD[:], PB[bD0][:, 0:512]),
                            rd=[f"pb{bD0}"], wr=["cD"], tag=f"d0ev{c}.{g2}")
                        p.step("act", lambda s, bVt=bVt: s.activation(
                            tVt[:], PB[bVt][:, 0:512], AF.Copy),
                            rd=[f"pb{bVt}"], wr=["cVt"], tag=f"vev{c}.{g2}")
                        for q, h in enumerate(hs):
                            p.step("dve", lambda v, h=h, q=q, bW0=bW0:
                                   v.scalar_tensor_tensor(
                                       sl(tW, q), PB[bW0][:, q * 128:(q + 1) * 128],
                                       negA8[:, h:h + 1], sl(tVt, q),
                                       ALU.mult, ALU.add),
                                   rd=[f"pb{bW0}", "cVt", "cnegA"], wr=["cW"],
                                   tag=f"vw{c}.{h}")
                        for kk in range(7):
                            ba = nb()
                            for q, h in enumerate(hs):
                                p.step("pe", lambda t, q=q, ba=ba: t.matmul(
                                    PB[ba][:, q * 128:(q + 1) * 128], sl(tDt, q),
                                    sl(tW, q), start=True, stop=False,
                                    skip_group_check=True),
                                    rd=["cDt", "cW"], wr=[f"pb{ba}"],
                                    tag=f"ap{c}.{g2}.{kk}.{q}a")
                                p.step("pe", lambda t, q=q, ba=ba: t.matmul(
                                    PB[ba][:, q * 128:(q + 1) * 128], c_idb[:, :],
                                    sl(tW, q), start=False, stop=True,
                                    skip_group_check=True),
                                    rd=["cW"], wr=[f"pb{ba}"],
                                    tag=f"ap{c}.{g2}.{kk}.{q}b")
                            if kk < 6:
                                bb = nb()
                                bc = nb()
                                for q in range(4):
                                    p.step("pe", lambda t, q=q, bb=bb: t.matmul(
                                        PB[bb][:, q * 128:(q + 1) * 128], sl(tDt, q),
                                        sl(tD, q), start=True, stop=True),
                                        rd=["cDt", "cD"], wr=[f"pb{bb}"],
                                        tag=f"sqD{c}.{g2}.{kk}.{q}")
                                    p.step("pe", lambda t, q=q, bc=bc: t.matmul(
                                        PB[bc][:, q * 128:(q + 1) * 128], sl(tD, q),
                                        sl(tDt, q), start=True, stop=True),
                                        rd=["cDt", "cD"], wr=[f"pb{bc}"],
                                        tag=f"sqDt{c}.{g2}.{kk}.{q}")
                                p.step("dve", lambda v, bb=bb: v.tensor_copy(
                                    tD[:], PB[bb][:, 0:512]),
                                    rd=[f"pb{bb}"], wr=["cD"], tag=f"evD{c}.{g2}.{kk}")
                                p.step("dve", lambda v, bc=bc: v.tensor_copy(
                                    tDt[:], PB[bc][:, 0:512]),
                                    rd=[f"pb{bc}"], wr=["cDt"],
                                    tag=f"evDt{c}.{g2}.{kk}")
                            if kk < 6:
                                p.step("act", lambda s, ba=ba: s.activation(
                                    tW[:], PB[ba][:, 0:512], AF.Copy),
                                    rd=[f"pb{ba}"], wr=["cW"], tag=f"evW{c}.{g2}.{kk}")
                            else:
                                for q, h in enumerate(hs):
                                    p.step("act", lambda s, q=q, h=h, c=c, ba=ba:
                                           s.activation(
                                               sl(tW, q),
                                               PB[ba][:, q * 128:(q + 1) * 128],
                                               AF.Copy,
                                               scale=beta_s[:, c * 8 + h:
                                                            c * 8 + h + 1]),
                                           rd=[f"pb{ba}"], wr=["cW"],
                                           tag=f"evU{c}.{g2}.{q}")
                        if A6STOP and A6STOP == (c, g2, 2):
                            p.barrier()
                            s6.pop_all().close()
                            sA.close()
                            return
                        bO = nb()
                        bO2 = nb()
                        for q, h in enumerate(hs):
                            p.step("pe", lambda t, q=q, bO=bO: t.matmul(
                                PB[bO][:, q * 128:(q + 1) * 128], sl(tMt, q),
                                sl(tW, q), start=True, stop=True),
                                rd=["cMt", "cW"], wr=[f"pb{bO}"], tag=f"o1{c}.{h}")
                            p.step("pe", lambda t, h=h, q=q, bO2=bO2: t.matmul(
                                PB[bO2][:, q * 128:(q + 1) * 128], qf(h),
                                S0b[:, h * 128:(h + 1) * 128],
                                start=True, stop=True),
                                rd=["aqkvh", "aS0b"], wr=[f"pb{bO2}"],
                                tag=f"o2{c}.{h}")
                        for q, h in enumerate(hs):
                            p.step("act", lambda s, q=q, h=h, bO2=bO2: s.activation(
                                sl(tO2, q), PB[bO2][:, q * 128:(q + 1) * 128],
                                AF.Copy, scale=sc_tm[:, 8 + h:9 + h]),
                                rd=[f"pb{bO2}", "csc"], wr=["cO2"],
                                tag=f"o2ev{c}.{h}")
                        p.step("dve", lambda v, bO=bO: v.tensor_copy(
                            tOt[:], PB[bO][:, 0:512]),
                            rd=[f"pb{bO}"], wr=["cOt"], tag=f"oev{c}.{g2}")
                        p.step("dve", lambda v: v.tensor_add(
                            tOt[:], tOt[:], tO2[:]),
                            rd=["cO2"], wr=["cOt"], tag=f"oadd{c}.{g2}")
                        if A6STOP and A6STOP == (c, g2, 3):
                            p.barrier()
                            s6.pop_all().close()
                            sA.close()
                            return
                        bK = nb()
                        bS = nb()
                        for q, h in enumerate(hs):
                            p.step("pe", lambda t, h=h, q=q, bK=bK: t.matmul(
                                PB[bK][:, q * 128:(q + 1) * 128], kf(h),
                                c_idb[:, :], start=True, stop=True),
                                rd=["aqkvh"], wr=[f"pb{bK}"], tag=f"kT{c}.{h}")
                        for q, h in enumerate(hs):
                            p.step("dve", lambda v, q=q, h=h, bK=bK: v.tensor_scalar(
                                sl(tKb, q), PB[bK][:, q * 128:(q + 1) * 128],
                                sc_tm[:, h:h + 1], None, ALU.mult),
                                rd=[f"pb{bK}", "csc"], wr=["cKb"], tag=f"kbev{c}.{h}")
                        for q, h in enumerate(hs):
                            p.step("pe", lambda t, q=q, bS=bS: t.matmul(
                                PB[bS][:, q * 128:(q + 1) * 128], sl(tKb, q),
                                sl(tW, q), start=True, stop=True),
                                rd=["cKb", "cW"], wr=[f"pb{bS}"], tag=f"smm{c}.{h}")
                        for q, h in enumerate(hs):
                            p.step("dve", lambda v, q=q, h=h, bS=bS:
                                   v.scalar_tensor_tensor(
                                       S0f[:, h * 128:(h + 1) * 128],
                                       S0f[:, h * 128:(h + 1) * 128],
                                       sc_tm[:, 16 + h:17 + h],
                                       PB[bS][:, q * 128:(q + 1) * 128],
                                       ALU.mult, ALU.add),
                                   rd=[f"pb{bS}", "csc"], wr=["aS0f"],
                                   tag=f"sup{c}.{h}")
                        p.step("pool", lambda g, g2=g2: g.tensor_copy(
                            S0b[:, g2 * 512:(g2 + 1) * 512],
                            S0f[:, g2 * 512:(g2 + 1) * 512]),
                            rd=["aS0f"], wr=["aS0b"], tag=f"s0b{c}.{g2}")
                        if A6STOP and A6STOP == (c, g2, 4):
                            p.barrier()
                            s6.pop_all().close()
                            sA.close()
                            return
                        bT = nb()
                        for q, h in enumerate(hs):
                            p.step("pe", lambda t, q=q, bT=bT: t.matmul(
                                PB[bT][:, q * 128:(q + 1) * 128], sl(tOt, q),
                                c_idb[:, :], start=True, stop=True),
                                rd=["cOt"], wr=[f"pb{bT}"], tag=f"oT{c}.{h}")
                        g0 = hs[0]
                        p.step("act", lambda s, c=c, g0=g0, bT=bT: s.activation(
                            O_fm[:].rearrange("p (g n) -> p g n", g=8)[
                                :, g0:g0 + 4, c * 128:(c + 1) * 128],
                            PB[bT][:, 0:512].rearrange("p (q n) -> p q n", q=4),
                            AF.Copy),
                            rd=[f"pb{bT}"], wr=[f"aOfm.{c}.{g2}"],
                            tag=f"ofmev{c}.{g2}")
                        if A6STOP and (c, g2) == A6STOP:
                            p.barrier()
                            s6.pop_all().close()
                            sA.close()
                            return
                if CUT == "a6":
                    p.barrier()
                    sA.close()
                    return
                p.barrier()

            for g_ in range(NH):
                p.step("gpq", lambda g, g_=g_: g.dma_start(
                    out=o_out[g_ * 128:(g_ + 1) * 128, :],
                    in_=O_fm[:, g_ * 2048:(g_ + 1) * 2048]),
                    rd=[f"aOfm.{c2}.{g22}" for c2 in range(16) for g22 in range(2)],
                    wr=["o_out_d"], tag=f"ost{g_}", ch="ost")
            p.barrier()
            p.step("pool", lambda g: g.collective_compute(
                "AllGather", ALU.bypass, replica_groups=PAIRS,
                ins=[o_out], outs=[o_gat]),
                rd=["o_out_d"], wr=["o_gat_d"], tag="cc_o", wb=True)
            p.barrier()
            sA.close()

            p.fence("ost")
            with ExitStack() as s8:
                asb = lambda name, shape, dt=F32: s8.enter_context(
                    nc.sbuf_tensor(f"a8{name}", list(shape), dt))
                Ofull = asb("Ofull", [128, 16 * 1024], BF16)
                lnfm2b = asb("lnfm2b", [128, 16 * 1024], BF16)
                ln2vb = lnfm2b[:].rearrange("p (k n) -> p k n", k=16)
                wgb = [asb(f"wgb{i}", [128, 16 * 1024], BF16) for i in range(2)]
                wstg8 = [asb(f"wstg{i}", [128, 1024]) for i in range(2)]
                gstage = asb("gstage", [128, 512], BF16)
                ystage = asb("ystage", [128, 4 * 512])
                xr = [asb(f"xr{i}", [128, 512]) for i in range(2)]

                Ofull3 = Ofull[:].rearrange("p (s n) -> p s n", s=16)
                for r in range(2):
                    def fno(g, r=r):
                        g.reg_load(rt, c_poff[0:1, 6 + r:7 + r])
                        return g.dma_start(
                            out=Ofull3[:, r * 8:(r + 1) * 8, :],
                            in_=bass.AP(o_gat_t, rt,
                                        [[D, 128], [128 * D, 8], [1, 1024]]))
                    p.step("gpq", fno, rd=["o_gat_d", "c_poff"], wr=["aOfull"],
                           tag=f"ofl{r}", ch="ofl")
                for k in range(16):
                    p.step("syq", lambda sy, k=k: sy.dma_start(
                        out=lnfm2b[:, k * 1024:(k + 1) * 1024],
                        in_=lnfm2d[:, k * 1024:(k + 1) * 1024]),
                        rd=["lnfm2d_d"], wr=["aln2b"], tag=f"lnld{k}", ch="lnld")
                for halfg in range(2):
                    jp = halfg % 2
                    for k in range(16):
                        sl2 = k % 2
                        p.step("syq", lambda sy, halfg=halfg, k=k, sl2=sl2: sy.dma_start(
                            out=wstg8[sl2][:],
                            in_=wgT[k * 128:(k + 1) * 128,
                                    halfg * 1024:(halfg + 1) * 1024]),
                            wr=[f"a8wst{sl2}"], tag=f"wgld{halfg}.{k}",
                            ch=f"a8wst{sl2}")
                        p.step("dve", lambda v, k=k, sl2=sl2, jp=jp: v.tensor_copy(
                            wgb[jp][:, k * 1024:(k + 1) * 1024], wstg8[sl2][:]),
                            rd=[f"a8wst{sl2}"],
                            wr=([f"awgb{jp}"] if k in (0, 15) else []),
                            tag=f"wgcast{halfg}.{k}")
                    for oc8 in range(8):
                        cc = halfg * 8 + oc8
                        for tt2 in range(2):
                            bank = (oc8 * 2 + tt2) % 8
                            for k in range(16):
                                p.step("pe", lambda t, k=k, oc8=oc8, tt2=tt2, jp=jp,
                                       bank=bank: t.matmul(
                                           PB[bank][:, 0:512],
                                           wgb[jp][:, k * 1024 + oc8 * 128:
                                                    k * 1024 + (oc8 + 1) * 128],
                                           ln2vb[:, k, tt2 * 512:(tt2 + 1) * 512],
                                           start=(k == 0), stop=(k == 15)),
                                       rd=(["awgb" + str(jp), "aln2b"]
                                           if k in (0, 15) else []),
                                       wr=[f"pb{bank}"] if k in (0, 15) else [],
                                       tag=f"gmm{halfg}.{oc8}.{tt2}.{k}")
                            p.step("act", lambda s, bank=bank: s.activation(
                                gstage[:], PB[bank][:, 0:512], AF.Silu),
                                rd=[f"pb{bank}"], wr=["agst"],
                                tag=f"gsilu{halfg}.{oc8}.{tt2}")
                            p.step("dve", lambda v, cc=cc, tt2=tt2: v.tensor_mul(
                                Ofull[:, cc * 1024 + tt2 * 512:
                                      cc * 1024 + (tt2 + 1) * 512],
                                Ofull[:, cc * 1024 + tt2 * 512:
                                      cc * 1024 + (tt2 + 1) * 512],
                                gstage[:]),
                                rd=["agst", "aOfull"], wr=["aOfull"],
                                tag=f"gate{halfg}.{oc8}.{tt2}")
                p.barrier()
                for ocg in range(2):
                    jp = ocg % 2
                    for k in range(16):
                        sl2 = k % 2
                        p.step("syq", lambda sy, ocg=ocg, k=k, sl2=sl2: sy.dma_start(
                            out=wstg8[sl2][:],
                            in_=woT[k * 128:(k + 1) * 128,
                                    ocg * 1024:(ocg + 1) * 1024]),
                            wr=[f"a8wst{sl2}"], tag=f"wold{ocg}.{k}",
                            ch=f"a8wst{sl2}")
                        p.step("dve", lambda v, k=k, sl2=sl2, jp=jp: v.tensor_copy(
                            wgb[jp][:, k * 1024:(k + 1) * 1024], wstg8[sl2][:]),
                            rd=[f"a8wst{sl2}"],
                            wr=([f"awgb{jp}"] if k in (0, 15) else []),
                            tag=f"wocast{ocg}.{k}")
                    for tt2 in range(2):
                        tok0 = tt2 * 512
                        for oc8 in range(8):
                            for k in range(16):
                                p.step("pe", lambda t, k=k, oc8=oc8, tt2=tt2, jp=jp:
                                       t.matmul(
                                           PB[oc8][:, 0:512],
                                           wgb[jp][:, k * 1024 + oc8 * 128:
                                                    k * 1024 + (oc8 + 1) * 128],
                                           Ofull[:, k * 1024 + tt2 * 512:
                                                 k * 1024 + (tt2 + 1) * 512],
                                           start=(k == 0), stop=(k == 15)),
                                       rd=(["awgb" + str(jp), "aOfull"]
                                           if k in (0, 15) else []),
                                       wr=[f"pb{oc8}"] if k in (0, 15) else [],
                                       tag=f"womm{ocg}.{tt2}.{oc8}.{k}")
                        for half in range(2):
                            for q4 in range(4):
                                oc = half * 4 + q4
                                p.step("act", lambda s, oc=oc, q4=q4: s.activation(
                                    ystage[:, q4 * 512:(q4 + 1) * 512],
                                    PB[oc][:, 0:512], AF.Copy),
                                    rd=[f"pb{oc}"], wr=[f"a8ys.{q4}"],
                                    tag=f"woev{ocg}.{tt2}.{oc}")
                            for ttt in range(4):
                                r0 = tok0 + ttt * 128
                                c0 = ocg * 1024 + half * 512
                                bank = ttt if half == 0 else 4 + ttt
                                xb2 = f"a8xr{ttt % 2}"
                                p.step("scq", lambda s, r0=r0, ttt=ttt, c0=c0:
                                       s.dma_start(out=xr[ttt % 2][:],
                                                   in_=src_x[r0:r0 + 128,
                                                             c0:c0 + 512]),
                                       rd=[f"{src_res}.{r0}.{ocg}.0",
                                           f"{src_res}.{r0}.{ocg}.1"],
                                       wr=[xb2], tag=f"aldxr{ocg}.{tt2}.{half}.{ttt}",
                                       ch=f"a8xr{ttt % 2}")
                                for q in range(4):
                                    p.step("pe", lambda t, q=q, ttt=ttt, bank=bank:
                                           t.transpose(
                                               PB[bank][:, q * 128:(q + 1) * 128],
                                               ystage[:, q * 512 + ttt * 128:
                                                       q * 512 + (ttt + 1) * 128],
                                               c_idf[:, :]),
                                           rd=[f"a8ys.{q}"], wr=[f"pb{bank}"],
                                           tag=f"aytr{ocg}.{tt2}.{ttt}.{half}.{q}")
                                p.step("dve", lambda v, ttt=ttt, bank=bank:
                                       v.tensor_add(xr[ttt % 2][:], xr[ttt % 2][:],
                                                    PB[bank][:, 0:512]),
                                       rd=[f"pb{bank}"], wr=[xb2],
                                       tag=f"ayadd{ocg}.{tt2}.{ttt}.{half}")
                                stx = p.step("gpq", lambda g, r0=r0, ttt=ttt, c0=c0:
                                             g.dma_start(
                                                 out=dst_x[r0:r0 + 128, c0:c0 + 512],
                                                 in_=xr[ttt % 2][:]),
                                             rd=[xb2],
                                             wr=[f"{dst_res}.{r0}.{ocg}.{half}"],
                                             tag=f"astx{ocg}.{tt2}.{half}.{ttt}",
                                             ch=f"st_{dst_res}")
                                if dst_x is y_out:
                                    final_deps.append(stx)
                    p.barrier()
            return

        def emit_ffn(call, src_x, dst_x, src_res, dst_res):
            with ExitStack() as fs:
                fsb = lambda name, shape, dt=F32: fs.enter_context(
                    nc.sbuf_tensor(f"f{call}{name}", list(shape), dt))
                lnfm = fsb("lnfm", [128, 16 * 1024], BF16)
                lnv = lnfm[:].rearrange("p (k n) -> p k n", k=16)
                xt = fsb("xt", [128, D])
                st24 = fsb("st24", [128, 24])
                mv = fsb("mv", [128, 2])
                rstd = fsb("rstd", [128, 2])
                xnb = fsb("xnb", [128, D], BF16)
                s_sc = fsb("s_sc", [128, 8])
                s_e1 = fsb("s_e1", [128, 8])
                s_mk = fsb("s_mk", [128, 8])
                s_e2 = fsb("s_e2", [128, 8])
                s_m1 = fsb("s_m1", [128, 1])
                s_m2 = fsb("s_m2", [128, 1])
                s_g1 = fsb("s_g1", [128, 1])
                s_g2 = fsb("s_g2", [128, 1])
                s_t1 = fsb("s_t1", [128, 8])
                combb = fsb("combb", [128, 64], BF16)
                comb_fm = fsb("comb_fm", [8, 1024], BF16)
                comb_bc = fsb("cbc", [128, 1024], BF16)
                w1sb = fsb("w1sb", [128, 16 * HF], BF16)
                wst = fsb("wst", [128, 2 * 704])
                hstage = fsb("hst", [128, 11 * 1024], BF16)
                hsv = hstage[:].rearrange("p (hc n) -> p hc n", hc=11)
                w2sb = [fsb(f"w2sb{i}", [128, 1408], BF16) for i in range(2)]
                y_acc = fsb("yac", [128, 16 * 1024])

                wslot = [0]

                def w_load(src_ap, tag):
                    sl = wslot[0] % 2
                    wslot[0] += 1
                    p.step("syq", lambda sy, src_ap=src_ap, sl=sl: sy.dma_start(
                        out=wst[:, sl * 704:(sl + 1) * 704], in_=src_ap),
                        rd=[], wr=[f"wst{sl}"], tag=tag, ch=f"wst{call}{sl}")
                    return sl

                def w2_pair(e, ot):
                    wi = ot % 2
                    for hf in range(2):
                        sl = w_load(w2R[e, ot, :, hf * 704:(hf + 1) * 704],
                                    f"w2ld{e}.{ot}.{hf}")
                        p.step("dve", lambda v, sl=sl, wi=wi, hf=hf: v.tensor_copy(
                            w2sb[wi][:, hf * 704:(hf + 1) * 704],
                            wst[:, sl * 704:(sl + 1) * 704]),
                            rd=[f"wst{sl}"], wr=[f"w2sb{wi}"],
                            tag=f"w2cast{e}.{ot}.{hf}")

                def w1_pair(e, k, extra=()):
                    for hf in range(2):
                        sl = w_load(w1T[e, k * 128:(k + 1) * 128,
                                        hf * 704:(hf + 1) * 704],
                                    f"w1ld{e}.{k}.{hf}")
                        p.step("dve", lambda v, k=k, sl=sl, hf=hf: v.tensor_copy(
                            w1sb[:, k * HF + hf * 704:k * HF + (hf + 1) * 704],
                            wst[:, sl * 704:(sl + 1) * 704]),
                            rd=[f"wst{sl}"],
                            wr=["w1rdy"] if (k == 15 and hf == 1) else [f"w1c{k}"],
                            extra=list(extra) if hf == 0 else [],
                            tag=f"w1cast{e}.{k}.{hf}")

                for k in range(16):
                    w1_pair(0, k)
                for ttt in range(8):
                    r0 = ttt * 128
                    p.step("scq", lambda s, r0=r0: s.dma_start(
                        out=xt[:], in_=src_x[r0:r0 + 128, :]),
                        rd=[f"{src_res}.{r0}.0.0", f"{src_res}.{r0}.0.1",
                            f"{src_res}.{r0}.1.0", f"{src_res}.{r0}.1.1"],
                        wr=["xtb"], tag=f"ldx{ttt}", ch=f"xt{call}")
                    for q in range(4):
                        p.step("dve", lambda v, q=q: v.bn_stats(
                            st24[:, q * 6:(q + 1) * 6],
                            xt[:, q * 512:(q + 1) * 512]),
                            rd=["xtb"], wr=[f"st24.{q}"], tag=f"bn{ttt}.{q}", wb=True)
                    p.step("dve", lambda v: v.bn_aggr(mv[:], st24[:]),
                           rd=["st24.0", "st24.1", "st24.2", "st24.3"],
                           wr=["mv"], tag="aggr", wb=True)
                    p.step("act", lambda s: s.activation(
                        rstd[:, 0:1], mv[:, 1:2], AF.Ln, bias=c_eps[:, :]),
                        rd=["mv"], wr=["rstd0"])
                    p.step("act", lambda s: s.activation(
                        rstd[:, 1:2], rstd[:, 0:1], AF.Exp, scale=-0.5),
                        rd=["rstd0"], wr=["rstd"])
                    p.step("dve", lambda v: v.tensor_scalar(
                        xnb[:], xt[:], mv[:, 0:1], rstd[:, 1:2],
                        ALU.subtract, ALU.mult),
                        rd=["xtb", "mv", "rstd"], wr=["xnb"], tag=f"nrm{ttt}")
                    for bg in range(4):
                        for q in range(4):
                            k = bg * 4 + q
                            p.step("pe", lambda t, k=k, bg=bg, q=q: t.matmul(
                                PB[bg][:, q * 128:(q + 1) * 128],
                                xnb[:, k * 128:(k + 1) * 128], c_idb[:, :],
                                start=True, stop=True),
                                rd=["xnb"], wr=[f"pb{bg}"], tag=f"trmm{ttt}.{bg}")
                        p.step("act", lambda s, bg=bg, ttt=ttt: s.activation(
                            lnv[:, bg * 4:(bg + 1) * 4, ttt * 128:(ttt + 1) * 128],
                            PB[bg][:, 0:512].rearrange("p (k n) -> p k n", k=4),
                            AF.Copy),
                            rd=[f"pb{bg}"], wr=[f"lnfm.{bg}.{ttt}"],
                            tag=f"trev{ttt}.{bg}")
                for ttt in range(8):
                    for k in range(16):
                        p.step("pe", lambda t, k=k, ttt=ttt: t.matmul(
                            PB[4][:, 0:8],
                            lnv[:, k, ttt * 128:(ttt + 1) * 128],
                            c_gwT[:, k * 8:(k + 1) * 8],
                            start=(k == 0), stop=(k == 15)),
                            rd=[f"lnfm.{k // 4}.{ttt}"],
                            wr=["pb4"] if k in (0, 15) else [],
                            tag=f"scmm{ttt}.{k}")
                    p.step("dve", lambda v: v.tensor_add(
                        s_sc[:], PB[4][:, 0:8], c_gbias[:]),
                        rd=["pb4"], wr=["s_sc"])
                    p.step("dve", lambda v: v.reduce_max(s_m1[:], s_sc[:], axis=AX),
                           rd=["s_sc"], wr=["s_m1"], wb=True)
                    p.step("dve", lambda v: v.tensor_scalar(
                        s_e1[:], s_sc[:], s_m1[:, :], None, ALU.is_equal),
                        rd=["s_sc", "s_m1"], wr=["s_e1"])
                    p.step("dve", lambda v: v.scalar_tensor_tensor(
                        s_mk[:], s_e1[:], -1e30, s_sc[:], ALU.mult, ALU.add),
                        rd=["s_e1", "s_sc"], wr=["s_mk"])
                    p.step("dve", lambda v: v.reduce_max(s_m2[:], s_mk[:], axis=AX),
                           rd=["s_mk"], wr=["s_m2"], wb=True)
                    p.step("dve", lambda v: v.tensor_scalar(
                        s_e2[:], s_mk[:], s_m2[:, :], None, ALU.is_equal),
                        rd=["s_mk", "s_m2"], wr=["s_e2"])
                    p.step("dve", lambda v: v.tensor_sub(s_m1[:], s_m1[:], s_m2[:]),
                           rd=["s_m2"], wr=["s_m1"])
                    p.step("act", lambda s: s.activation(
                        s_g1[:], s_m1[:, :], AF.Sigmoid), rd=["s_m1"], wr=["s_g1"])
                    p.step("act", lambda s: s.activation(
                        s_g2[:], s_m1[:, :], AF.Sigmoid, scale=-1.0),
                        rd=["s_m1"], wr=["s_g2"])
                    p.step("dve", lambda v: v.tensor_scalar(
                        s_t1[:], s_e1[:], s_g1[:, :], None, ALU.mult),
                        rd=["s_e1", "s_g1"], wr=["s_t1"])
                    p.step("dve", lambda v, ttt=ttt: v.scalar_tensor_tensor(
                        combb[:, ttt * 8:(ttt + 1) * 8], s_e2[:], s_g2[:, :], s_t1[:],
                        ALU.mult, ALU.add),
                        rd=["s_e2", "s_g2", "s_t1"], wr=[f"combb.{ttt}"], tag="comb")
                    pbx = 5 + ttt // 4
                    p.step("pe", lambda t, ttt=ttt, pbx=pbx: t.matmul(
                        PB[pbx][0:8, (ttt % 4) * 128:(ttt % 4 + 1) * 128],
                        combb[:, ttt * 8:(ttt + 1) * 8], c_idb[:, :],
                        start=True, stop=True),
                        rd=[f"combb.{ttt}"], wr=[f"pb{pbx}"], tag="cmbT")
                    p.step("dve", lambda v, ttt=ttt, pbx=pbx: v.tensor_copy(
                        comb_fm[:, ttt * 128:(ttt + 1) * 128],
                        PB[pbx][0:8, (ttt % 4) * 128:(ttt % 4 + 1) * 128]),
                        rd=[f"pb{pbx}"], wr=[f"comb_fm.{ttt}"], tag="cfm")
                cfm_all = [f"comb_fm.{i}" for i in range(8)]
                p.barrier()


                for e in range(NE):
                    p.step("pe", lambda t, e=e: t.matmul(
                        PB[4][:, 0:512], c_selb[0:8, e * 128:(e + 1) * 128],
                        comb_fm[0:8, 0:512], start=True, stop=True),
                        rd=cfm_all[:4], wr=["pb4"], tag=f"cbmm{e}.0")
                    p.step("pe", lambda t, e=e: t.matmul(
                        PB[5][:, 0:512], c_selb[0:8, e * 128:(e + 1) * 128],
                        comb_fm[0:8, 512:1024], start=True, stop=True),
                        rd=cfm_all[4:], wr=["pb5"], tag=f"cbmm{e}.1")
                    p.step("dve", lambda v: v.tensor_copy(
                        comb_bc[:, 0:512], PB[4][:, 0:512]),
                        rd=["pb4"], wr=["cbc0"], tag=f"cbev{e}.0")
                    p.step("dve", lambda v: v.tensor_copy(
                        comb_bc[:, 512:1024], PB[5][:, 0:512]),
                        rd=["pb5"], wr=["cbc1"], tag=f"cbev{e}.1")
                    h_done = None
                    for hc in range(11):
                        for th in range(2):
                            b = (hc * 2 + th) % 2
                            for k in range(16):
                                s = p.step("pe", lambda t, k=k, hc=hc, th=th, b=b:
                                           t.matmul(
                                               PB[b][:, 0:512],
                                               w1sb[:, k * HF + hc * 128:
                                                    k * HF + (hc + 1) * 128],
                                               lnv[:, k, th * 512:(th + 1) * 512],
                                               start=(k == 0), stop=(k == 15)),
                                           rd=["w1rdy"] if k == 0 else [],
                                           wr=[f"pb{b}"] if k in (0, 15) else [],
                                           tag=f"hmm{e}.{hc}.{th}.{k}")
                                if hc == 10 and th == 1 and k == 15:
                                    h_done = s
                            p.step("act", lambda s, e=e, hc=hc, th=th, b=b:
                                   s.activation(
                                       hsv[:, hc, th * 512:(th + 1) * 512],
                                       PB[b][:, 0:512], AF.Silu,
                                       bias=c_b1T[:, hc * 8 + e:hc * 8 + e + 1]),
                                   rd=[f"pb{b}"],
                                   wr=[f"hsi.{hc}.{th}", f"hst.{hc}.{th}"],
                                   tag=f"sil{e}.{hc}.{th}")
                            p.step("dve", lambda v, hc=hc, th=th: v.tensor_mul(
                                hsv[:, hc, th * 512:(th + 1) * 512],
                                hsv[:, hc, th * 512:(th + 1) * 512],
                                comb_bc[:, th * 512:(th + 1) * 512]),
                                rd=[f"hsi.{hc}.{th}", f"cbc{th}"],
                                wr=[f"hst.{hc}.{th}"] +
                                   (["hrdy"] if hc == 10 and th == 1 else []),
                                tag=f"hm{e}.{hc}.{th}")
                    w2_pair(e, 0)
                    w2_pair(e, 1)
                    for ot in range(16):
                        wi = ot % 2
                        for th in range(2):
                            b = 2 + (ot * 2 + th) % 2
                            if e == 0:
                                p.step("pe", lambda t, ot=ot, th=th, b=b, wi=wi:
                                       t.matmul(
                                           PB[b][:, 0:512],
                                           c_b2b[0:8, ot * 128:(ot + 1) * 128],
                                           comb_fm[0:8, th * 512:(th + 1) * 512],
                                           start=True, stop=False),
                                       rd=["hrdy", f"w2sb{wi}"],
                                       wr=[f"pb{b}"], tag=f"ymm{e}.{ot}.{th}.s")
                            for hc in range(11):
                                rdl = []
                                if hc == 0 and e > 0:
                                    rdl += ["hrdy", f"w2sb{wi}"]
                                if hc == 10:
                                    rdl.append(f"w2sb{wi}")
                                if ot == 15:
                                    rdl.append(f"hst.{hc}.{th}")
                                p.step("pe", lambda t, hc=hc, th=th, b=b, wi=wi, e=e:
                                       t.matmul(
                                           PB[b][:, 0:512],
                                           w2sb[wi][:, hc * 128:(hc + 1) * 128],
                                           hsv[:, hc, th * 512:(th + 1) * 512],
                                           start=(hc == 0 and e > 0),
                                           stop=(hc == 10)),
                                       rd=rdl,
                                       wr=[f"pb{b}"] if hc in (0, 10) else [],
                                       tag=f"ymm{e}.{ot}.{th}.{hc}")
                            ysl = y_acc[:, ot * 1024 + th * 512:
                                        ot * 1024 + (th + 1) * 512]
                            if e == 0:
                                p.step("dve", lambda v, ysl=ysl, b=b: v.tensor_copy(
                                    ysl, PB[b][:, 0:512]),
                                    rd=[f"pb{b}"], wr=[f"yac.{ot}.{th}"],
                                    tag=f"yev{e}.{ot}.{th}")
                            else:
                                p.step("dve", lambda v, ysl=ysl, b=b: v.tensor_add(
                                    ysl, ysl, PB[b][:, 0:512]),
                                    rd=[f"pb{b}", f"yac.{ot}.{th}"],
                                    wr=[f"yac.{ot}.{th}"],
                                    tag=f"yev{e}.{ot}.{th}")
                        if ot + 2 < 16:
                            w2_pair(e, ot + 2)
                        if e + 1 < NE:
                            w1_pair(e + 1, ot,
                                    [h_done] if ot == 0 else [])
                for tv in range(8):
                    r0 = tv * 128
                    p.step("scq", lambda s, r0=r0: s.dma_start(
                        out=xt[:], in_=src_x[r0:r0 + 128, :]),
                        rd=[f"{src_res}.{r0}.0.0", f"{src_res}.{r0}.0.1",
                            f"{src_res}.{r0}.1.0", f"{src_res}.{r0}.1.1"],
                        wr=["xtb"], tag=f"ldxr{tv}", ch=f"xt{call}")
                    bks = [4, 5, 6, 7] if tv % 2 == 0 else [0, 1, 2, 3]
                    for g in range(4):
                        b = bks[g]
                        for oq in range(4):
                            ot = g * 4 + oq
                            p.step("pe", lambda t, ot=ot, tv=tv, b=b, oq=oq:
                                   t.transpose(
                                       PB[b][:, oq * 128:(oq + 1) * 128],
                                       y_acc[:, ot * 1024 + tv * 128:
                                             ot * 1024 + tv * 128 + 128],
                                       c_idf[:, :]),
                                   rd=[f"yac.{ot}.{tv // 4}"], wr=[f"pb{b}"],
                                   tag=f"ytr{tv}.{ot}")
                        p.step("dve", lambda v, g=g, b=b: v.scalar_tensor_tensor(
                            xt[:, g * 512:(g + 1) * 512], PB[b][:, 0:512], 0.5,
                            xt[:, g * 512:(g + 1) * 512], ALU.mult, ALU.add),
                            rd=[f"pb{b}", "xtb"], wr=["xtb"], tag=f"yfin{tv}.{g}")
                    stx = p.step("gpq", lambda gq, r0=r0: gq.dma_start(
                        out=dst_x[r0:r0 + 128, :], in_=xt[:]),
                        rd=["xtb"],
                        wr=[f"{dst_res}.{r0}.{a}.{bb}"
                            for a in range(2) for bb in range(2)],
                        tag=f"stx{tv}", ch=f"st_{dst_res}")
                    if dst_x is y_out:
                        final_deps.append(stx)
                p.barrier()
            return

        f32w_attn = True
        emit_ffn(1, x_in, x1d if PHASES > 1 else y_out, "xin", "x1")
        if PHASES > 1:
            p.fence("st_x1")
            emit_attn(x1d, x2d if PHASES > 2 else y_out, "x1", "x2")
        if PHASES > 2:
            p.fence("st_x2", "lnld", "ofl")
            emit_ffn(2, x2d, y_out, "x2", "y")

        p.final_wait(final_deps)
    return nc


def kernel(**inputs):
    in_maps = [host_inputs(inputs, c) for c in range(8)]
    nc = build_program()
    res = run_bass_kernel_spmd(nc, in_maps, core_ids=list(range(8)))
    out = np.zeros((4, 2048, D), dtype=np.float32)
    for c in range(8):
        b, half = c // 2, c % 2
        out[b, half * T:(half + 1) * T, :] = res.results[c]["y_out"]
    return out
